# version 03
# baseline (speedup 1.0000x reference)
import numpy as np

import concourse.bass as bass
import concourse.mybir as mybir
from concourse.bass_utils import run_bass_kernel_spmd
from concourse.bass_core_rust import IndirectOffsetOnAxis

F32 = mybir.dt.float32
BF16 = mybir.dt.bfloat16
I32 = mybir.dt.int32
AF = mybir.ActivationFunctionType
ALU = mybir.AluOpType
AX = mybir.AxisListType

D = 4096
NT = 1024
BT = 4096
NH = 32
HD = 128
SCALE = float(1.0 / np.sqrt(128.0))
NCORES = 8



class Sem:
    __slots__ = ("name", "n", "h")

    def __init__(self, name):
        self.name = name
        self.n = 0
        self.h = None


class Pt:
    __slots__ = ("sem", "val")

    def __init__(self, sem, val):
        self.sem = sem
        self.val = val


class Slot:
    __slots__ = ("ws", "r")

    def __init__(self):
        self.ws = []
        self.r = []

    @property
    def w(self):
        return self.ws[-1] if self.ws else None

    def rw(self):
        return list(self.ws)

    def for_write(self):
        return list(self.r) + list(self.ws)

    def wrote(self, pt):
        assert pt is not None
        self.ws = [pt]
        self.r = []

    def add_write(self, pt):
        assert pt is not None
        self.ws.append(pt)

    def read(self, pt):
        if pt is not None:
            self.r.append(pt)

    def seed(self, pts):
        self.r.extend(p for p in pts if p is not None)


class Emitter:
    ENGINES = ("pe", "dv", "ac", "gp", "sy")

    def __init__(self):
        self.q = {e: [] for e in self.ENGINES}
        self.sems = {}
        self.waited = {e: {} for e in self.ENGINES}

    def S(self, name):
        if name not in self.sems:
            self.sems[name] = Sem(name)
        return self.sems[name]

    def emit(self, eng, fn, waits=(), inc=None, inck=1):
        seen = {}
        for pt in waits:
            if pt is None:
                continue
            key = pt.sem.name
            cur = seen.get(key, self.waited[eng].get(key, -1))
            if pt.val > cur:
                seen[key] = pt.val
        ws = []
        for key, val in seen.items():
            self.waited[eng][key] = max(self.waited[eng].get(key, -1), val)
            ws.append((self.sems[key], val))
        pt_out = None
        if inc is not None:
            inc.n += inck
            pt_out = Pt(inc, inc.n)
        self.q[eng].append((ws, fn, inc, inck))
        return pt_out

    def replay(self, block):
        def run(stream):
            def body(eng):
                for ws, fn, inc, inck in stream:
                    for sem, val in ws:
                        eng.wait_ge(sem.h, val)
                    ins = fn(eng)
                    if inc is not None:
                        ins.then_inc(inc.h, inck)
            return body

        block.tensor(run(self.q["pe"]))
        block.vector(run(self.q["dv"]))
        block.scalar(run(self.q["ac"]))
        block.gpsimd(run(self.q["gp"]))
        block.sync(run(self.q["sy"]))



GCOLS = 105536
GA0 = 0
GA1 = 32768
GC = 65536
GD = 81920
GE = 90112
GH = 98304
ST0 = 99392
ST1 = 100416
GD2 = 101440
ST2 = GD2

A1_K0 = GA1 + 0
A1_K1 = GA1 + 4096
A1_V0 = GA1 + 8192
A1_V1 = GA1 + 12288
A1_Q0 = GA1 + 16384
A1_Q1 = GA1 + 17408
A1_CTXS0 = GA1 + 18432
A1_CTXS1 = GA1 + 19456
A1_GST = GA1 + 20480
A1_GSTB = GA1 + 24576
A1_COS = GA1 + 28672
A1_SIN = GA1 + 29696
A1_ROT = GA1 + 30720
A1_RT1 = GA1 + 31744

D_AHT = GD + 0
D_BHT = GD + 512
D_ZSB = GD + 4608

H_ID = GH + 0
H_D0 = GH + 128
H_QKVB = GH + 640
H_OBT = GH + 832
H_FBT = GH + 896
H_ANM = GH + 960
H_ASUM = GH + 976
H_ASS = GH + 992
H_ARC = GH + 1000
H_LN = GH + 1008
H_LSI_I = GH + 1040
H_LSI_F = GH + 1048
H_BVAL = GH + 1056
H_TMP = GH + 1058
H_HSEL = GH + 1074
H_BTAB = GH + 1080

LN_S1, LN_NEGM, LN_M2, LN_VAR, LN_LNV, LN_RSTD, LN_SUMSQ = range(7)


def lncol(stat, par):
    return H_LN + stat * 4 + par * 2


class Builder:
    def __init__(self, upto=7, taps=()):
        self.upto = upto
        self.taps = taps
        self.nc = bass.Bass("TRN2", target_bir_lowering=False)
        nc = self.nc
        self.xin = nc.dram_tensor("xin", [BT, D], F32, kind="ExternalInput").ap()
        self.xq = nc.dram_tensor("xq", [NT, D], F32, kind="ExternalInput").ap()
        self.qkvw = nc.dram_tensor("qkvw", [96, 128, 4096], F32, kind="ExternalInput").ap()
        self.qkvbT = nc.dram_tensor("qkvbT", [128, 96], F32, kind="ExternalInput").ap()
        self.ow = nc.dram_tensor("ow", [32, 128, 4096], F32, kind="ExternalInput").ap()
        self.obT = nc.dram_tensor("obT", [128, 32], F32, kind="ExternalInput").ap()
        self.wqt = nc.dram_tensor("wqt", [32, 128, 4096], F32, kind="ExternalInput").ap()
        self.wot = nc.dram_tensor("wot", [32, 128, 4096], F32, kind="ExternalInput").ap()
        self.wft = nc.dram_tensor("wft", [32, 128, 4096], F32, kind="ExternalInput").ap()
        self.fbT = nc.dram_tensor("fbT", [128, 32], F32, kind="ExternalInput").ap()
        self.kcache = nc.dram_tensor("kcache", [512, 16384], F32, kind="ExternalInput").ap()
        self.vcache = nc.dram_tensor("vcache", [512, 16384], F32, kind="ExternalInput").ap()
        self.btab = nc.dram_tensor("btab", [256], I32, kind="ExternalInput").ap()
        self.lsi = nc.dram_tensor("lsi", [4], I32, kind="ExternalInput").ap()
        self.bval = nc.dram_tensor("bval", [1, 1], F32, kind="ExternalInput").ap()
        self.loraA = nc.dram_tensor("loraA", [3, 16, D], F32, kind="ExternalInput").ap()
        self.loraB = nc.dram_tensor("loraB", [3, D, 16], F32, kind="ExternalInput").ap()
        self.cosf = nc.dram_tensor("cosf", [128, NT], F32, kind="ExternalInput").ap()
        self.sinf = nc.dram_tensor("sinf", [128, NT], F32, kind="ExternalInput").ap()
        self.y = nc.dram_tensor("y", [NT, D], F32, kind="ExternalOutput").ap()
        self.xd = nc.dram_tensor("xd", [NT, D], F32, kind="Internal").ap()
        self.kfmd = nc.dram_tensor("kfmd", [D, BT], BF16, kind="Internal").ap()
        self.vfmd = nc.dram_tensor("vfmd", [D, BT], BF16, kind="Internal").ap()
        self.qfmd = nc.dram_tensor("qfmd", [D, NT], BF16, kind="Internal").ap()
        self.ctxd = nc.dram_tensor("ctxd", [D, NT], BF16, kind="Internal").ap()
        self.kgd = nc.dram_tensor("kgd", [8, BT, HD], BF16, kind="Internal").ap()
        self.vgd = nc.dram_tensor("vgd", [8, BT, HD], BF16, kind="Internal").ap()

        self.tap_aps = {}
        shapes = {"kfmd": ([D, BT], BF16), "vfmd": ([D, BT], BF16),
                  "qfmd": ([D, NT], BF16), "ctxd": ([D, NT], BF16),
                  "kgd": ([8, BT, HD], BF16), "vgd": ([8, BT, HD], BF16),
                  "xd": ([NT, D], F32)}
        for t in taps:
            if t == "adbg":
                self.dbg_pbT = nc.dram_tensor("tap_pbT", [128, 32768], BF16,
                                              kind="ExternalOutput").ap()
                self.dbg_probs = nc.dram_tensor("tap_probs", [128, 8192], BF16,
                                                kind="ExternalOutput").ap()
                self.dbg_small = nc.dram_tensor("tap_small", [128, 1024], F32,
                                                kind="ExternalOutput").ap()
                continue
            sh, dt = shapes[t]
            self.tap_aps[t] = nc.dram_tensor("tap_" + t, sh, dt,
                                             kind="ExternalOutput").ap()

        E = Emitter()
        self.E = E
        self.s_pe = E.S("s_pe")
        self.s_dv = E.S("s_dv")
        self.s_ac = E.S("s_ac")
        self.s_gp = E.S("s_gp")
        self.s_sy = E.S("s_sy")
        self.s_sc = E.S("s_sc")
        self.s_gd = E.S("s_gd")
        self.s_setup = E.S("s_setup")
        self.s_xt = [E.S("s_xt0"), E.S("s_xt1")]
        self.s_w = [E.S("s_w0"), E.S("s_w1")]
        self.s_k = [E.S("s_k0"), E.S("s_k1")]
        self.s_v = [E.S("s_v0"), E.S("s_v1")]
        self.s_q = [E.S("s_q0"), E.S("s_q1")]
        self.s_st = [E.S("s_st0"), E.S("s_st1"), E.S("s_st2")]
        self.s_stc = [E.S("s_stc0"), E.S("s_stc1")]
        self.s_stx = [E.S("s_stx0"), E.S("s_stx1")]
        self.s_cxl = E.S("s_cxl")
        self.s_misc = E.S("s_misc")
        self.s_lor = E.S("s_lor")
        self.s_gd2 = E.S("s_gd2")

        self.xt = [Slot(), Slot()]
        self.lnt = [Slot(), Slot()]
        self.lns = [Slot(), Slot()]
        self.wst = [Slot(), Slot()]
        self.xfm = [Slot(), Slot()]
        self.stg = [Slot(), Slot(), Slot()]
        self.psA = Slot()
        self.psB = Slot()
        self.psC = [Slot(), Slot()]
        self.psD = Slot()
        self.kh = [Slot(), Slot()]
        self.vh = [Slot(), Slot()]
        self.qh = [Slot(), Slot()]
        self.ctxs = [Slot(), Slot()]
        self.gst = Slot()
        self.rA0 = Slot()
        self.rA1 = Slot()
        self.identity = Slot()
        self.biases = Slot()
        self.att_nm = [Slot(), Slot(), Slot()]
        self.att_sums = [Slot(), Slot(), Slot()]
        self.att_ss = [Slot(), Slot(), Slot()]
        self.gstB = Slot()
        self.probsx = Slot()
        self._gather_steps = []
        self.kst_pts = [[] for _ in range(32)]
        self.vst_pts = [[] for _ in range(32)]
        self.qst_pts = [[] for _ in range(32)]
        self.lora = {k: Slot() for k in ("AhT", "BhT", "zsb", "hsel")}
        self.btsb = Slot()
        self.last_store = None

    def g_bf(self, col, n, parts=128, p0=0):
        return self.grand[p0:p0 + parts, col:col + n]

    def g_f32(self, col, n, parts=128, p0=0):
        return self.grand_f32[p0:p0 + parts, col // 2:col // 2 + n]

    def g_i32(self, col, n, parts=128, p0=0):
        return self.grand_i32[p0:p0 + parts, col // 2:col // 2 + n]

    def head_guards(self):
        pts = []
        for s in (self.kh + self.vh + self.qh + self.ctxs + [self.gst, self.gstB]):
            pts.extend(s.for_write())
        return pts

    def build(self):
        from contextlib import ExitStack
        nc, E = self.nc, self.E
        with ExitStack() as stack:
            grand_h = stack.enter_context(nc.sbuf_tensor("grand", [128, GCOLS], BF16))
            arena_h = stack.enter_context(nc.psum_tensor("arena", [128, 4096], F32))
            for name in list(E.sems):
                E.sems[name].h = stack.enter_context(nc.semaphore(name))
            self.grand = grand_h[:]
            self.grand_f32 = grand_h[:].bitcast(F32)
            self.grand_i32 = grand_h[:].bitcast(I32)
            self.arena = arena_h[:]
            self.arena_bf = arena_h[:].bitcast(BF16)

            self.p0_setup()
            if self.upto >= 1:
                self.p1_attn_a_proj()
            if self.upto >= 2:
                self.p2_gather()
                if self.upto == 2:
                    self.gather_flush()
            if self.upto >= 3:
                self.rA0.seed(self.xfm[0].for_write())
                for s in (self.kh + self.vh + self.qh + self.ctxs +
                          [self.gst, self.gstB]):
                    s.seed(self.xfm[1].for_write())
                self.probsx.seed(self.stg[2].for_write())
                self.p3_attention("a")
                if self.upto >= 2:
                    self.gather_flush()
                if "ctxd" in self.tap_aps:
                    self.tap_dram(self.tap_aps["ctxd"], self.ctxd, list(self.s_stc))
                if "adbg" in self.taps:
                    E = self.E
                    gates = [Pt(s, s.n) for s in (self.s_dv, self.s_ac, self.s_pe,
                                                  self.s_gp, self.s_sc)]
                    E.emit("sy", lambda e, o=self.dbg_pbT, i=self.g_bf(GA0, 32768):
                           e.dma_start(out=o, in_=i), waits=gates,
                           inc=self.s_sy, inck=16)
                    E.emit("sy", lambda e, o=self.dbg_probs, i=self.g_bf(GD, 8192):
                           e.dma_start(out=o, in_=i), waits=gates,
                           inc=self.s_sy, inck=16)
                    E.emit("sy", lambda e, o=self.dbg_small, i=self.g_f32(GH, 1024):
                           e.dma_start(out=o, in_=i), waits=gates,
                           inc=self.s_sy, inck=16)
            if self.upto >= 4:
                self.p4_outproj(self.ow, bias=True, xsrc=self.xq, dst=self.xd)
            if self.upto >= 5:
                self.p5_ffn(dst=self.xd)
            if self.upto >= 6:
                self.p6_attn_b()
            if self.upto >= 7:
                self.p5_ffn(dst=self.y)
            self.finalize()

            blk = stack.enter_context(nc.Block())
            E.replay(blk)
        return nc


    def p0_setup(self):
        E = self.E
        g = self.g_f32
        for dst_col, n, src in ((H_QKVB, 96, self.qkvbT), (H_OBT, 32, self.obT),
                                (H_FBT, 32, self.fbT)):
            pt = E.emit("sy", lambda e, o=g(dst_col, n), i=src:
                        e.dma_start(out=o, in_=i), inc=self.s_setup, inck=16)
        idap = self.g_bf(H_ID, 128)
        p1 = E.emit("gp", lambda e, o=idap: e.iota(
            o, pattern=[[1, 128]], base=0, channel_multiplier=-1,
            allow_small_or_imprecise_dtypes=True), inc=self.s_gp)
        p2 = E.emit("gp", lambda e, o=idap: e.tensor_single_scalar(
            o, o, 0.0, ALU.is_equal), waits=[p1], inc=self.s_gp)
        self.identity.wrote(p2)
        p3 = E.emit("sy", lambda e, o=self.g_i32(H_LSI_I, 4),
                    i=self.lsi.unsqueeze(0).broadcast_to([128, 4]):
                    e.dma_start(out=o, in_=i), inc=self.s_setup, inck=16)
        p4 = E.emit("sy", lambda e, o=g(H_BVAL, 1),
                    i=self.bval.broadcast_to([128, 1]):
                    e.dma_start(out=o, in_=i), inc=self.s_setup, inck=16)
        setup_pt = Pt(self.s_setup, self.s_setup.n)
        self.biases.wrote(setup_pt)
        p4 = setup_pt
        p5 = E.emit("dv", lambda e, o=g(H_LSI_F, 4), i=self.g_i32(H_LSI_I, 4):
                    e.tensor_copy(o, i), waits=[setup_pt], inc=self.s_dv)
        bv = g(H_BVAL, 1)
        lf = g(H_LSI_F, 4)

        def tcol(k):
            return g(H_TMP + 2 * k, 1)

        def tt(o, a, b, op, w):
            return E.emit("dv", lambda e, o=o, a=a, b=b, op=op:
                          e.tensor_tensor(o, a, b, op), waits=w, inc=self.s_dv)

        def ts2(o, a, s1, s2, w):
            return E.emit("dv", lambda e, o=o, a=a, s1=s1, s2=s2:
                          e.tensor_scalar(o, a, s1, s2, ALU.mult, ALU.add),
                          waits=w, inc=self.s_dv)

        c = [tt(tcol(k), bv, lf[:, k:k + 1], ALU.is_lt, [p4, p5]) for k in range(4)]
        pu = ts2(tcol(4), tcol(0), -1.0, 1.0, [c[0]])
        pv1 = tt(tcol(5), tcol(1), tcol(4), ALU.mult, [c[1], pu])
        pw = ts2(tcol(6), tcol(1), -2.0, 2.0, [c[1]])
        pv2 = tt(tcol(7), tcol(5), tcol(6), ALU.add, [pv1, pw])
        pinner = tt(tcol(4), tcol(2), tcol(7), ALU.mult, [c[2], pv2])
        pr2 = ts2(tcol(6), tcol(3), -1.0, 4.0, [c[3]])
        pnc2 = ts2(tcol(5), tcol(2), -1.0, 1.0, [c[2]])
        po2 = tt(tcol(6), tcol(5), tcol(6), ALU.mult, [pr2, pnc2])
        pres = tt(tcol(7), tcol(4), tcol(6), ALU.add, [pinner, po2])
        hs = None
        for a in range(3):
            hs = E.emit("dv", lambda e, o=g(H_HSEL + 2 * a, 1), i=tcol(7), v=float(2 + a):
                        e.tensor_single_scalar(o, i, v, ALU.is_equal),
                        waits=[pres], inc=self.s_dv)
        self.lora["hsel"].wrote(hs)

    def ln_pass(self, src, row0, ntiles, xfm_col, xfm_slot, gate=()):
        E = self.E
        g = self.g_f32
        last_evac = None
        first = True
        for j in range(ntiles):
            par = j % 2
            xt = self.xt[par]
            xtap = g(GC + par * 8192, 4096)
            lns = self.lns[par]
            r0 = row0 + j * 128
            pl = E.emit("sy", lambda e, o=xtap, i=src[r0:r0 + 128, :]:
                        e.dma_start(out=o, in_=i),
                        waits=xt.for_write() + (gate or []),
                        inc=self.s_xt[par], inck=16)
            xt.wrote(pl)
            lnt = self.lnt[par]
            lntap = self.g_bf(GD + par * 4096, 4096)
            c_sq, c_s1, c_nm, c_m2, c_v, c_lnv, c_rstd = (
                g(lncol(LN_SUMSQ, par), 1), g(lncol(LN_S1, par), 1),
                g(lncol(LN_NEGM, par), 1), g(lncol(LN_M2, par), 1),
                g(lncol(LN_VAR, par), 1), g(lncol(LN_LNV, par), 1),
                g(lncol(LN_RSTD, par), 1))
            psq = E.emit("ac", lambda e, o=lntap, i=xtap, a=c_sq:
                         e.activation(o, i, AF.Square, bias=0.0, scale=1.0, accum_out=a),
                         waits=[xt.w] + lnt.for_write() + lns.for_write(), inc=self.s_ac)
            ps1 = E.emit("dv", lambda e, o=c_s1, i=xtap:
                         e.reduce_sum(o, i, axis=AX.X), waits=[xt.w], inc=self.s_dv)
            pnm = E.emit("dv", lambda e, o=c_nm, i=c_s1:
                         e.tensor_scalar_mul(o, i, -1.0 / 4096.0),
                         waits=[ps1], inc=self.s_dv)
            pm2 = E.emit("dv", lambda e, o=c_m2, i=c_nm:
                         e.tensor_scalar(o, i, i, -1e-5, ALU.mult, ALU.add),
                         waits=[pnm], inc=self.s_dv)
            pvar = E.emit("dv", lambda e, o=c_v, a=c_sq, b=c_m2:
                          e.scalar_tensor_tensor(o, a, 1.0 / 4096.0, b,
                                                 ALU.mult, ALU.subtract),
                          waits=[psq, pm2], inc=self.s_dv)
            plnv = E.emit("ac", lambda e, o=c_lnv, i=c_v:
                          e.activation(o, i, AF.Ln, bias=0.0, scale=1.0),
                          waits=[pvar], inc=self.s_ac)
            prstd = E.emit("ac", lambda e, o=c_rstd, i=c_lnv:
                           e.activation(o, i, AF.Exp, bias=0.0, scale=-0.5),
                           waits=[plnv], inc=self.s_ac)
            pnorm = E.emit("dv", lambda e, o=lntap, i=xtap, s1=c_nm, s2=c_rstd:
                           e.tensor_scalar(o, i, s1, s2, ALU.add, ALU.mult),
                           waits=[prstd, pnm, xt.w, psq], inc=self.s_dv)
            lnt.wrote(pnorm)
            lns.read(pnorm)
            xt.read(pnorm)
            for bi in range(4):
                pc = self.psC[bi % 2]
                pcap = self.arena_bf[:, 4096 + (bi % 2) * 1024:
                                     4096 + (bi % 2) * 1024 + 1024]
                gpt = None
                for jj in range(8):
                    dc = bi * 8 + jj
                    w = ([pnorm, self.identity.w] + pc.for_write()) if jj == 0 else []
                    gpt = E.emit("pe", lambda e, o=pcap[:, jj * 128:jj * 128 + 128],
                                 i=lntap[:, dc * 128:dc * 128 + 128],
                                 idn=self.g_bf(H_ID, 128):
                                 e.transpose(o, i, idn),
                                 waits=w, inc=self.s_pe if jj == 7 else None)
                pc.wrote(gpt)
                lnt.read(gpt)
                base = xfm_col + (bi * 8) * 1024 + j * 128
                dst = self.grand[:, base: base + 8 * 1024].rearrange(
                    "p (c t) -> p c t", c=8)[:, :, 0:128]
                pev = E.emit("dv", lambda e, o=dst,
                             i=pcap.rearrange("p (c t) -> p c t", c=8):
                             e.tensor_copy(o, i),
                             waits=[gpt] + (xfm_slot.for_write() if first else []),
                             inc=self.s_dv)
                first = False
                pc.read(pev)
                last_evac = pev
        xfm_slot.wrote(last_evac)
        return last_evac

    def proj_fm(self, wdram, wchunk0, nchunks, bias_fn, scale, xfm_col, xfm_slot, sink,
                lora=False):
        E = self.E
        for oc in range(nchunks):
            ws = self.wst[oc % 2]
            wsap = self.g_bf(GE + (oc % 2) * 4096, 4096)
            pw = E.emit("gp", lambda e, o=wsap, i=wdram[wchunk0 + oc]:
                        e.dma_start(out=o, in_=i),
                        waits=ws.for_write(), inc=self.s_w[oc % 2], inck=16)
            ws.wrote(pw)
            ps = self.psA if (oc % 2 == 0) else self.psB
            psap = self.arena[:, (oc % 2) * 1024:(oc % 2) * 1024 + 1024]
            gpt = None
            for dc in range(32):
                for sp in range(2):
                    w = ([pw, xfm_slot.w] + ps.for_write()) if (dc == 0 and sp == 0) else []
                    is_last = (dc == 31 and sp == 1) and not lora
                    gpt = E.emit("pe", lambda e, o=psap[:, sp * 512:sp * 512 + 512],
                                 l=wsap[:, dc * 128:dc * 128 + 128],
                                 r=self.grand[:, xfm_col + dc * 1024 + sp * 512:
                                              xfm_col + dc * 1024 + sp * 512 + 512],
                                 st=(dc == 0), sp_=is_last:
                                 e.matmul(o, l, r, start=st, stop=sp_),
                                 waits=w, inc=self.s_pe if is_last else None)
            if lora:
                for sp in range(2):
                    is_last = sp == 1
                    gpt = E.emit("pe", lambda e, o=psap[:, sp * 512:sp * 512 + 512],
                                 l=self.g_bf(D_BHT + oc * 128, 128, parts=16),
                                 r=self.g_bf(D_ZSB + sp * 512, 512, parts=16),
                                 sp_=is_last:
                                 e.matmul(o, l, r, start=False, stop=sp_),
                                 waits=([self.lora["BhT"].w, self.lora["zsb"].w]
                                        if sp == 0 else []),
                                 inc=self.s_pe if is_last else None)
            ps.wrote(gpt)
            ws.read(gpt)
            xfm_slot.read(gpt)
            if lora:
                self.lora["BhT"].read(gpt)
                self.lora["zsb"].read(gpt)
            st = self.stg[oc % 2]
            stap = self.g_bf(ST0 + (oc % 2) * 1024, 1024)
            bap = bias_fn(oc) if bias_fn is not None else None
            if bap is not None and scale is not None:
                pev = E.emit("dv", lambda e, o=stap, i=psap, b=bap, s=scale:
                             e.tensor_scalar(o, i, b, s, ALU.add, ALU.mult),
                             waits=[gpt, self.biases.w] + st.for_write(), inc=self.s_dv)
            elif bap is not None:
                pev = E.emit("dv", lambda e, o=stap, i=psap, b=bap:
                             e.tensor_scalar(o, i, b, None, ALU.add),
                             waits=[gpt, self.biases.w] + st.for_write(), inc=self.s_dv)
            elif scale is not None:
                pev = E.emit("dv", lambda e, o=stap, i=psap, s=scale:
                             e.tensor_scalar_mul(o, i, s),
                             waits=[gpt] + st.for_write(), inc=self.s_dv)
            else:
                pev = E.emit("dv", lambda e, o=stap, i=psap: e.tensor_copy(o, i),
                             waits=[gpt] + st.for_write(), inc=self.s_dv)
            ps.read(pev)
            st.wrote(pev)
            sink(oc, stap, st, pev)

    def qkvb_col(self, chunk0):
        def f(oc):
            return self.g_f32(H_QKVB, 96)[:, chunk0 + oc:chunk0 + oc + 1]
        return f

    def p1_attn_a_proj(self):
        E = self.E
        for q in range(5):
            xfm_slot = self.xfm[q % 2]
            xfm_col = GA0 if q % 2 == 0 else GA1
            src = self.xin if q < 4 else self.xq
            row0 = q * 1024 if q < 4 else 0
            self.ln_pass(src, row0, 8, xfm_col, xfm_slot)
            if q < 4:
                def ksink(oc, stap, st, pev, q=q):
                    p = E.emit("ac", lambda e,
                               o=self.kfmd[oc * 128:(oc + 1) * 128, q * 1024:(q + 1) * 1024],
                               i=stap: e.dma_start(out=o, in_=i),
                               waits=[pev], inc=self.s_st[oc % 2], inck=16)
                    st.read(p)
                    self.kst_pts[oc].append(p)
                self.proj_fm(self.qkvw, 32, 32, self.qkvb_col(32), None,
                             xfm_col, xfm_slot, ksink)
                self.v_chunks(q, xfm_col, xfm_slot)
            else:
                def qsink(oc, stap, st, pev):
                    p = E.emit("ac", lambda e, o=self.qfmd[oc * 128:(oc + 1) * 128, :],
                               i=stap: e.dma_start(out=o, in_=i),
                               waits=[pev], inc=self.s_st[oc % 2], inck=16)
                    st.read(p)
                    self.qst_pts[oc].append(p)
                self.proj_fm(self.qkvw, 0, 32, self.qkvb_col(0), SCALE,
                             xfm_col, xfm_slot, qsink)
        self.kv_gate = [Pt(s, s.n) for s in self.s_st]
        for t in ("kfmd", "vfmd", "qfmd"):
            if t in self.tap_aps:
                self.tap_dram(self.tap_aps[t], getattr(self, t), list(self.s_st))

    def v_chunks(self, q, xfm_col, xfm_slot):
        E = self.E

        def vsink(oc, stap, st, pev, q=q):
            p = E.emit("ac", lambda e,
                       o=self.vfmd[oc * 128:(oc + 1) * 128, q * 1024:(q + 1) * 1024],
                       i=stap: e.dma_start(out=o, in_=i),
                       waits=[pev], inc=self.s_st[oc % 2], inck=16)
            st.read(p)
            self.vst_pts[oc].append(p)

        self.proj_fm(self.qkvw, 64, 32, self.qkvb_col(64), None,
                     xfm_col, xfm_slot, vsink)

    def tap_dram(self, dst, src, gate_sem):
        sems = gate_sem if isinstance(gate_sem, list) else [gate_sem]
        self.E.emit("sy", lambda e, o=dst, i=src: e.dma_start(out=o, in_=i),
                    waits=[Pt(s, s.n) for s in sems], inc=self.s_sy, inck=16)

    def p2_gather(self):
        E = self.E
        pb = E.emit("gp", lambda e, o=self.g_i32(H_BTAB, 2),
                    i=self.btab.rearrange("(a p) -> p a", p=128):
                    e.dma_start(out=o, in_=i, allow_slow_non_contiguous=True),
                    inc=self.s_gd, inck=16)
        self.btsb.wrote(pb)
        steps = []
        for src_, dst in ((self.kcache, self.kgd), (self.vcache, self.vgd)):
            for h in range(8):
                for half in range(2):
                    steps.append((src_, dst, h, half))

        def make_step(idx):
            src_, dst, h, half = steps[idx]
            par = idx % 2
            gslot = self.gst if par == 0 else self.gstB
            gsem = self.s_gd if par == 0 else self.s_gd2
            gstap = self.g_f32(A1_GST if par == 0 else A1_GSTB, 2048)

            def emit():
                pg = E.emit("gp", lambda e, o=gstap, i=src_, eo=h * 2048,
                            ix=self.g_i32(H_BTAB, 2)[:, half:half + 1]:
                            e.indirect_dma_start(o, None, i,
                                                 IndirectOffsetOnAxis(ix, 0),
                                                 element_offset=eo),
                            waits=[pb] + gslot.for_write(), inc=gsem, inck=16)
                gslot.wrote(pg)
                ps = E.emit("gp", lambda e,
                            o=dst[h, half * 2048:(half + 1) * 2048, :]
                            .rearrange("(p r) d -> p r d", p=128),
                            i=gstap.rearrange("p (r d) -> p r d", r=16):
                            e.dma_start(out=o, in_=i),
                            waits=[pg], inc=gsem, inck=16)
                gslot.read(ps)
            return emit

        self._gather_steps = [make_step(i) for i in range(len(steps))]

    def gather_tick(self, n=2):
        for _ in range(n):
            if self._gather_steps:
                self._gather_steps.pop(0)()

    def gather_flush(self):
        while self._gather_steps:
            self._gather_steps.pop(0)()
        self.gather_gate = [Pt(self.s_gd, self.s_gd.n),
                            Pt(self.s_gd2, self.s_gd2.n)]
        for t in ("kgd", "vgd"):
            if t in self.tap_aps:
                self.tap_dram(self.tap_aps[t], getattr(self, t), self.s_gd)

    def p3_attention(self, mode):
        E = self.E
        heads = [None] * NH

        def inplace_tm_fm_build(S, Sap, gate_pt):
            for bi in range(4):
                pc = self.psC[bi % 2]
                pcap = self.arena_bf[:, 4096 + (bi % 2) * 1024:
                                     4096 + (bi % 2) * 1024 + 1024]
                gpt = None
                for jj in range(8):
                    cc = bi * 8 + jj
                    w = ([gate_pt, self.identity.w] + pc.for_write()) if jj == 0 else []
                    gpt = E.emit("pe",
                                 lambda e, o=pcap[:, jj * 128:jj * 128 + 128],
                                 i=Sap[:, cc * 128:cc * 128 + 128],
                                 idn=self.g_bf(H_ID, 128):
                                 e.transpose(o, i, idn), waits=w,
                                 inc=self.s_pe if jj == 7 else None)
                pc.wrote(gpt)
                S.read(gpt)
                pev = E.emit("dv", lambda e,
                             o=Sap[:, bi * 1024:bi * 1024 + 1024],
                             i=pcap: e.tensor_copy(o, i),
                             waits=[gpt], inc=self.s_dv)
                pc.read(pev)
                if bi == 0:
                    S.wrote(pev)
                else:
                    S.add_write(pev)

        def load_head(h):
            if mode == "a":
                K, Kap = self.kh[h % 2], self.g_bf(A1_K0 + (h % 2) * 4096, 4096)
                V, Vap = self.vh[h % 2], self.g_bf(A1_V0 + (h % 2) * 4096, 4096)
                pk = E.emit("sy", lambda e, o=Kap, i=self.kfmd[h * 128:(h + 1) * 128, :]:
                            e.dma_start(out=o, in_=i),
                            waits=K.for_write() + self.kst_pts[h],
                            inc=self.s_k[h % 2], inck=16)
                K.wrote(pk)
                pv = E.emit("sy", lambda e, o=Vap, i=self.vfmd[h * 128:(h + 1) * 128, :]:
                            e.dma_start(out=o, in_=i),
                            waits=V.for_write() + self.vst_pts[h],
                            inc=self.s_v[h % 2], inck=16)
                V.wrote(pv)
                inplace_tm_fm_build(V, Vap, pv)
            else:
                kvh = h // 4
                K, Kap = self.kh[kvh % 2], self.g_bf(A1_K0 + (kvh % 2) * 4096, 4096)
                V, Vap = self.vh[kvh % 2], self.g_bf(A1_V0 + (kvh % 2) * 4096, 4096)
                if h % 4 == 0:
                    pk = E.emit("sy", lambda e,
                                o=Kap.rearrange("p (c d) -> p c d", c=32),
                                i=self.kgd[kvh].rearrange("(c p) d -> p c d", p=128):
                                e.dma_start(out=o, in_=i),
                                waits=K.for_write() + self.gather_gate,
                                inc=self.s_k[kvh % 2], inck=16)
                    K.wrote(pk)
                    inplace_tm_fm_build(K, Kap, pk)
                    pv = E.emit("sy", lambda e,
                                o=Vap.rearrange("p (c d) -> p c d", c=32),
                                i=self.vgd[kvh].rearrange("(c p) d -> p c d", p=128):
                                e.dma_start(out=o, in_=i),
                                waits=V.for_write() + self.gather_gate,
                                inc=self.s_v[kvh % 2], inck=16)
                    V.wrote(pv)
            Q, Qap = self.qh[h % 2], self.g_bf(A1_Q0 + (h % 2) * 1024, 1024)
            pq = E.emit("sy", lambda e, o=Qap, i=self.qfmd[h * 128:(h + 1) * 128, :]:
                        e.dma_start(out=o, in_=i),
                        waits=Q.for_write() + self.qst_pts[h],
                        inc=self.s_q[h % 2], inck=16)
            Q.wrote(pq)
            heads[h] = (K, Kap, V, Vap, Q, Qap)

        def emit_ctx(h):
            K, Kap, V, Vap, Q, Qap = heads[h]
            ctxap = self.arena[:, 3072:4096]
            gpt = None
            for c in range(32):
                for sp in range(2):
                    w = (self.rA0.rw() + V.rw() + self.psD.for_write()) \
                        if (c == 0 and sp == 0) else []
                    is_last = c == 31 and sp == 1
                    gpt = E.emit("pe", lambda e, o=ctxap[:, sp * 512:sp * 512 + 512],
                                 l=Vap[:, c * 128:c * 128 + 128],
                                 r=self.grand[:, GA0 + c * 1024 + sp * 512:
                                              GA0 + c * 1024 + sp * 512 + 512],
                                 st=(c == 0), spp=(c == 31):
                                 e.matmul(o, l, r, start=st, stop=spp),
                                 waits=w, inc=self.s_pe if is_last else None)
            self.psD.wrote(gpt)
            self.rA0.read(gpt)
            V.read(gpt)
            cs = self.ctxs[h % 2]
            csap = self.g_bf(A1_CTXS0 + (h % 2) * 1024, 1024)
            pev = E.emit("dv", lambda e, o=csap, i=ctxap: e.tensor_copy(o, i),
                         waits=[gpt] + cs.for_write(), inc=self.s_dv)
            self.psD.read(pev)
            cs.wrote(pev)
            pst = E.emit("ac", lambda e, o=self.ctxd[h * 128:(h + 1) * 128, :], i=csap:
                         e.dma_start(out=o, in_=i), waits=[pev],
                         inc=self.s_stc[h % 2], inck=16)
            cs.read(pst)

        NU = NH * 8
        unit_state = [None] * NU
        for gu in range(NU + 2):
            h, i = divmod(gu, 8)
            if gu == 0:
                load_head(0)
                load_head(1)
            if gu < NU:
                self.unit_quarters(gu, 0, 2, heads[h], unit_state)
            if gu >= 2:
                self.unit_combine_T(gu - 2, unit_state)
                if (gu - 2) % 8 == 7:
                    hh = (gu - 2) // 8
                    emit_ctx(hh)
                    if hh + 2 < NH:
                        load_head(hh + 2)
                    if mode == "a":
                        self.gather_tick(2)
            if gu < NU:
                self.unit_quarters(gu, 2, 4, heads[h], unit_state)

    def probs3(self, par):
        slots = [self.lnt[0], self.lnt[1], self.probsx]
        cols = [GD, GD + 4096, GD2]
        return slots[par], self.g_bf(cols[par], 4096)

    def unit_quarters(self, gu, k0, k1, head, unit_state):
        E = self.E
        g = self.g_f32
        K, Kap, V, Vap, Q, Qap = head
        i = gu % 8
        par = gu % 3
        if unit_state[gu] is None:
            unit_state[gu] = {"par": par}
        stt = unit_state[gu]
        probs, probsap = self.probs3(par)
        stt["probs"], stt["probsap"] = probs, probsap
        nm_ap = g(H_ANM + 2 * par, 1)
        sums_ap = g(H_ASUM + 4 * par, 2)
        stt["nm_ap"], stt["sums_ap"] = nm_ap, sums_ap
        half = k0 // 2
        mmpts = []
        for k in range(k0, k1):
            ps = self.psA if k % 2 == 0 else self.psB
            psap = self.arena[:, (k % 2) * 1024:(k % 2) * 1024 + 1024]
            gpt = None
            for sp in range(2):
                w = ([K.w, Q.w] + ps.for_write()) if sp == 0 else []
                gpt = E.emit("pe", lambda e, o=psap[:, sp * 512:sp * 512 + 512],
                             l=Qap[:, i * 128:i * 128 + 128],
                             r=Kap[:, k * 1024 + sp * 512:k * 1024 + sp * 512 + 512]:
                             e.matmul(o, l, r, start=True, stop=True),
                             waits=w, inc=self.s_pe if sp == 1 else None)
            ps.wrote(gpt)
            K.read(gpt)
            Q.read(gpt)
            mmpts.append(gpt)
        if half == 0:
            nm = self.att_nm[par]
            pmx = E.emit("dv", lambda e, o=nm_ap, i=self.arena[:, 0:1024:8]:
                         e.reduce_max(o, i, axis=AX.X, negate=True),
                         waits=[mmpts[0]] + nm.for_write(), inc=self.s_dv)
            nm.wrote(pmx)
            stt["pmx"] = pmx
        sums = self.att_sums[par]
        pex = E.emit("ac", lambda e, o=probsap[:, half * 2048:half * 2048 + 2048],
                     i=self.arena[:, 0:2048], b=nm_ap, a=sums_ap[:, half:half + 1]:
                     e.activation(o, i, AF.Exp, bias=b, scale=1.0, accum_out=a),
                     waits=[stt["pmx"], mmpts[1]] +
                           (probs.for_write() if half == 0 else []) +
                           (sums.for_write() if half == 0 else []),
                     inc=self.s_ac)
        self.psA.read(pex)
        self.psB.read(pex)
        probs.wrote(pex)
        sums.wrote(pex)
        if half == 1:
            stt["exp_last"] = pex
            self.att_nm[par].read(pex)

    def unit_combine_T(self, gu, unit_state):
        E = self.E
        g = self.g_f32
        stt = unit_state[gu]
        i = gu % 8
        par = stt["par"]
        probs, probsap = stt["probs"], stt["probsap"]
        ss = self.att_ss[par]
        pss = E.emit("dv", lambda e, o=g(H_ASS + 2 * par, 1), i=stt["sums_ap"]:
                     e.reduce_sum(o, i, axis=AX.X),
                     waits=[stt["exp_last"]] + ss.for_write(), inc=self.s_dv)
        prc = E.emit("dv", lambda e, o=g(H_ARC + 2 * par, 1), i=g(H_ASS + 2 * par, 1):
                     e.reciprocal(o, i), waits=[pss], inc=self.s_dv)
        ss.wrote(prc)
        self.att_sums[par].read(pss)
        pr = E.emit("dv", lambda e, o=probsap, s=g(H_ARC + 2 * par, 1):
                    e.tensor_scalar_mul(o, o, s),
                    waits=[prc, probs.w], inc=self.s_dv)
        probs.wrote(pr)
        ss.read(pr)
        for bi in range(4):
            pc = self.psC[bi % 2]
            pcap = self.arena_bf[:, 4096 + (bi % 2) * 1024:
                                 4096 + (bi % 2) * 1024 + 1024]
            gpt = None
            for jj in range(8):
                cc = bi * 8 + jj
                w = ([pr, self.identity.w] + pc.for_write()) if jj == 0 else []
                gpt = E.emit("pe", lambda e, o=pcap[:, jj * 128:jj * 128 + 128],
                             i=probsap[:, cc * 128:cc * 128 + 128],
                             dk=self.g_bf(H_ID, 128):
                             e.transpose(o, i, dk), waits=w,
                             inc=self.s_pe if jj == 7 else None)
            pc.wrote(gpt)
            probs.read(gpt)
            base = GA0 + (bi * 8) * 1024 + i * 128
            dst = self.grand[:, base: base + 8 * 1024].rearrange(
                "p (c t) -> p c t", c=8)[:, :, 0:128]
            eng = "ac" if bi == 3 else "dv"
            sem = self.s_ac if bi == 3 else self.s_dv
            if eng == "ac":
                fn = (lambda e, o=dst, i=pcap.rearrange("p (c t) -> p c t", c=8):
                      e.copy(o, i))
            else:
                fn = (lambda e, o=dst, i=pcap.rearrange("p (c t) -> p c t", c=8):
                      e.tensor_copy(o, i))
            pev = E.emit(eng, fn,
                         waits=[gpt] + (self.rA0.for_write() if (i == 0 and bi == 0)
                                        else []),
                         inc=sem)
            pc.read(pev)
            if i == 0 and bi == 0:
                self.rA0.wrote(pev)
            else:
                self.rA0.add_write(pev)

    def p4_outproj(self, wdram, bias, xsrc, dst):
        E = self.E
        pl = E.emit("sy", lambda e,
                    o=self.grand[:, GA0:GA0 + 32768].rearrange("p (c t) -> p c t", c=32),
                    i=self.ctxd.rearrange("(c p) t -> p c t", p=128):
                    e.dma_start(out=o, in_=i),
                    waits=self.rA0.for_write() + [Pt(s, s.n) for s in self.s_stc],
                    inc=self.s_cxl, inck=16)
        self.rA0.wrote(pl)
        bias_fn = (lambda oc: self.g_f32(H_OBT, 32)[:, oc:oc + 1]) if bias else None
        guards = self.head_guards() + self.rA1.for_write()

        def osink(oc, stap, st, pev):
            p = E.emit("dv", lambda e, o=self.g_bf(GA1 + oc * 1024, 1024), i=stap:
                       e.tensor_copy(o, i),
                       waits=[pev] + (guards if oc == 0 else []),
                       inc=self.s_dv)
            st.read(p)
            self.rA1.wrote(p)

        self.proj_fm(wdram, 0, 32, bias_fn, None, GA0, self.rA0, osink)
        self.add_pass(xsrc, dst)

    def add_pass(self, xsrc, dst):
        E = self.E
        g = self.g_f32
        for j in range(8):
            par = j % 2
            xt = self.xt[par]
            xtap = g(GC + par * 8192, 4096)
            pl = E.emit("sy", lambda e, o=xtap, i=xsrc[j * 128:(j + 1) * 128, :]:
                        e.dma_start(out=o, in_=i),
                        waits=xt.for_write() + (self.last_store or []),
                        inc=self.s_xt[par], inck=16)
            xt.wrote(pl)
            addpts = []
            for half in range(2):
                if half == 0:
                    slots = [self.psC[0], self.psC[1]]
                else:
                    slots = [self.psD]
                pap = self.arena_bf[:, (4 + half * 2) * 1024:
                                    (4 + half * 2) * 1024 + 2048]
                gpt = None
                for ocl in range(16):
                    oc = half * 16 + ocl
                    if ocl == 0:
                        w = [self.rA1.w, self.identity.w]
                        for s in slots:
                            w += s.for_write()
                    else:
                        w = []
                    gpt = E.emit("pe", lambda e, o=pap[:, ocl * 128:ocl * 128 + 128],
                                 i=self.g_bf(GA1 + oc * 1024 + j * 128, 128),
                                 idn=self.g_bf(H_ID, 128):
                                 e.transpose(o, i, idn), waits=w,
                                 inc=self.s_pe if ocl == 15 else None)
                for s in slots:
                    s.wrote(gpt)
                self.rA1.read(gpt)
                padd = E.emit("dv", lambda e, o=xtap[:, half * 2048:half * 2048 + 2048],
                              i=pap: e.tensor_tensor(o, o, i, ALU.add),
                              waits=[gpt, xt.w], inc=self.s_dv)
                for s in slots:
                    s.read(padd)
                addpts.append(padd)
            pst = E.emit("ac", lambda e, o=dst[j * 128:(j + 1) * 128, :], i=xtap:
                         e.dma_start(out=o, in_=i), waits=addpts,
                         inc=self.s_stx[par], inck=16)
            xt.read(pst)
        self.last_store = [Pt(s, s.n) for s in self.s_stx]

    def p5_ffn(self, dst):
        E = self.E
        g = self.g_f32
        self.ln_pass(self.xd, 0, 8, GA0, self.rA0, gate=self.last_store)
        hw = self.lora["hsel"].w
        for a in range(3):
            E.emit("gp", lambda e, o=self.g_bf(GA1 + a * 4096, 4096, parts=16),
                   i=self.loraA[a]:
                   e.dma_start(out=o, in_=i),
                   waits=(self.rA1.for_write() + self.head_guards())
                   if a == 0 else [],
                   inc=self.s_lor, inck=16)
        for a in range(3):
            E.emit("gp", lambda e,
                   o=self.g_bf(GA1 + 16384 + a * 512, 512).rearrange("p (c r) -> p c r", c=32),
                   i=self.loraB[a].rearrange("(c p) r -> p c r", p=128):
                   e.dma_start(out=o, in_=i), waits=[],
                   inc=self.s_lor, inck=16)
        lor_pt = Pt(self.s_lor, self.s_lor.n)
        apts = [lor_pt, lor_pt, lor_pt]
        ah = self.g_bf(GA1 + 3 * 4096, 4096, parts=16)
        p = E.emit("dv", lambda e, o=ah, i=self.g_bf(GA1 + 0, 4096, parts=16),
                   s=g(H_HSEL, 3, parts=16)[:, 0:1]:
                   e.tensor_scalar_mul(o, i, s), waits=[apts[0], hw], inc=self.s_dv)
        for a in (1, 2):
            p = E.emit("dv", lambda e, o=ah, i=self.g_bf(GA1 + a * 4096, 4096, parts=16),
                       s=g(H_HSEL, 3, parts=16)[:, a:a + 1]:
                       e.scalar_tensor_tensor(o, i, s, o, ALU.mult, ALU.add),
                       waits=[apts[a], p], inc=self.s_dv)
        pah = p
        self.rA1.wrote(pah)
        for bi in range(4):
            pc = self.psC[bi % 2]
            pcap = self.arena_bf[:, 4096 + (bi % 2) * 1024:
                                 4096 + (bi % 2) * 1024 + 1024]
            gpt = None
            for jj in range(8):
                cc = bi * 8 + jj
                w = ([pah, self.identity.w] + pc.for_write()) if jj == 0 else []
                gpt = E.emit("pe", lambda e, o=pcap[:, jj * 16:jj * 16 + 16],
                             i=ah[:, cc * 128:cc * 128 + 128],
                             idn=self.g_bf(H_ID, 128, parts=16)[:, 0:16]:
                             e.transpose(o, i, idn), waits=w,
                             inc=self.s_pe if jj == 7 else None)
            pc.wrote(gpt)
            self.rA1.read(gpt)
            pev = E.emit("dv", lambda e, o=self.g_bf(D_AHT + bi * 128, 128),
                         i=pcap[:, 0:128]: e.tensor_copy(o, i),
                         waits=[gpt] + (self.lora["AhT"].for_write() +
                                        self.lnt[0].for_write() if bi == 0 else []),
                         inc=self.s_dv)
            pc.read(pev)
            self.lora["AhT"].wrote(pev)
        bpts = [lor_pt, lor_pt, lor_pt]
        bh = self.g_bf(GA1 + 16384 + 3 * 512, 512)
        p = E.emit("dv", lambda e, o=bh, i=self.g_bf(GA1 + 16384, 512),
                   s=g(H_HSEL, 3)[:, 0:1]:
                   e.tensor_scalar_mul(o, i, s), waits=[bpts[0], hw], inc=self.s_dv)
        for a in (1, 2):
            p = E.emit("dv", lambda e, o=bh, i=self.g_bf(GA1 + 16384 + a * 512, 512),
                       s=g(H_HSEL, 3)[:, a:a + 1]:
                       e.scalar_tensor_tensor(o, i, s, o, ALU.mult, ALU.add),
                       waits=[bpts[a], p], inc=self.s_dv)
        pbh = p
        self.rA1.wrote(pbh)
        for bi in range(4):
            pc = self.psC[bi % 2]
            pcap = self.arena_bf[0:16, 4096 + (bi % 2) * 1024:
                                 4096 + (bi % 2) * 1024 + 1024]
            gpt = None
            for jj in range(8):
                cc = bi * 8 + jj
                w = ([pbh, self.identity.w] + pc.for_write()) if jj == 0 else []
                gpt = E.emit("pe", lambda e, o=pcap[:, jj * 128:jj * 128 + 128],
                             i=bh[:, cc * 16:cc * 16 + 16],
                             idn=self.g_bf(H_ID, 128):
                             e.transpose(o, i, idn), waits=w,
                             inc=self.s_pe if jj == 7 else None)
            pc.wrote(gpt)
            self.rA1.read(gpt)
            pev = E.emit("dv", lambda e, o=self.g_bf(D_BHT + bi * 1024, 1024, parts=16),
                         i=pcap: e.tensor_copy(o, i),
                         waits=[gpt] + (self.lora["BhT"].for_write() +
                                        self.lnt[1].for_write() if bi == 0 else []),
                         inc=self.s_dv)
            pc.read(pev)
            self.lora["BhT"].wrote(pev)
        zap = self.arena[0:16, 3072:4096]
        gpt = None
        for dc in range(32):
            for sp in range(2):
                w = ([self.lora["AhT"].w, self.rA0.w] + self.psD.for_write()) \
                    if (dc == 0 and sp == 0) else []
                is_last = dc == 31 and sp == 1
                gpt = E.emit("pe", lambda e, o=zap[:, sp * 512:sp * 512 + 512],
                             l=self.g_bf(D_AHT + dc * 16, 16),
                             r=self.grand[:, GA0 + dc * 1024 + sp * 512:
                                          GA0 + dc * 1024 + sp * 512 + 512],
                             st=(dc == 0), spp=is_last:
                             e.matmul(o, l, r, start=st, stop=spp),
                             waits=w, inc=self.s_pe if is_last else None)
        self.psD.wrote(gpt)
        self.lora["AhT"].read(gpt)
        self.rA0.read(gpt)
        pz = E.emit("dv", lambda e, o=self.g_bf(D_ZSB, 1024, parts=16), i=zap:
                    e.tensor_copy(o, i),
                    waits=[gpt] + self.lora["zsb"].for_write(), inc=self.s_dv)
        self.psD.read(pz)
        self.lora["zsb"].wrote(pz)

        def fsink(oc, stap, st, pev):
            p = E.emit("dv", lambda e, o=self.g_bf(GA1 + oc * 1024, 1024), i=stap:
                       e.tensor_copy(o, i),
                       waits=[pev] + (self.rA1.for_write() if oc == 0 else []),
                       inc=self.s_dv)
            st.read(p)
            self.rA1.wrote(p)

        self.proj_fm(self.wft, 0, 32,
                     lambda oc: self.g_f32(H_FBT, 32)[:, oc:oc + 1], None,
                     GA0, self.rA0, fsink, lora=True)
        self.add_pass(self.xd, dst)

    def p6_attn_b(self):
        E = self.E
        g = self.g_f32
        self.ln_pass(self.xd, 0, 8, GA0, self.rA0, gate=self.last_store)
        E.emit("gp", lambda e, o=self.g_bf(A1_COS, 1024), i=self.cosf:
               e.dma_start(out=o, in_=i),
               waits=self.rA1.for_write() + self.gstB.for_write() +
               [Pt(self.s_gd, self.s_gd.n), Pt(self.s_gd2, self.s_gd2.n)],
               inc=self.s_misc, inck=16)
        E.emit("gp", lambda e, o=self.g_bf(A1_SIN, 1024), i=self.sinf:
               e.dma_start(out=o, in_=i), inc=self.s_misc, inck=16)
        psin = Pt(self.s_misc, self.s_misc.n)
        self.rA1.wrote(psin)
        rope_guard = [psin]

        def qsink(oc, stap, st, pev):
            rot = self.g_bf(A1_ROT, 1024)
            t1 = self.g_bf(A1_RT1, 1024)
            p1 = E.emit("dv", lambda e, o=rot[0:64, :], i=stap[64:128, :]:
                        e.tensor_scalar_mul(o, i, -1.0),
                        waits=[pev] + rope_guard, inc=self.s_dv)
            p2 = E.emit("dv", lambda e, o=rot[64:128, :], i=stap[0:64, :]:
                        e.tensor_copy(o, i), waits=[pev], inc=self.s_dv)
            p3 = E.emit("dv", lambda e, o=t1, a=stap, b=self.g_bf(A1_COS, 1024):
                        e.tensor_tensor(o, a, b, ALU.mult), waits=[pev], inc=self.s_dv)
            p4 = E.emit("dv", lambda e, o=rot, a=rot, b=self.g_bf(A1_SIN, 1024):
                        e.tensor_tensor(o, a, b, ALU.mult), waits=[p1, p2],
                        inc=self.s_dv)
            p5 = E.emit("dv", lambda e, o=stap, a=t1, b=rot:
                        e.tensor_tensor(o, a, b, ALU.add), waits=[p3, p4],
                        inc=self.s_dv)
            st.wrote(p5)
            self.rA1.read(p5)
            p = E.emit("ac", lambda e, o=self.qfmd[oc * 128:(oc + 1) * 128, :], i=stap:
                       e.dma_start(out=o, in_=i), waits=[p5],
                       inc=self.s_st[oc % 2], inck=16)
            st.read(p)
            self.qst_pts[oc] = [p]

        self.proj_fm(self.wqt, 0, 32, None, SCALE, GA0, self.rA0, qsink)
        self.kv_gate = [Pt(s, s.n) for s in self.s_st]
        if "qfmd" in self.tap_aps:
            self.tap_dram(self.tap_aps["qfmd"], self.qfmd, list(self.s_st))
        for s in (self.kh + self.vh + self.qh + self.ctxs):
            s.seed(self.rA1.for_write())
        self.p3_attention("b")
        if "ctxd" in self.tap_aps:
            self.tap_dram(self.tap_aps["ctxd"], self.ctxd, list(self.s_stc))
        self.p4_outproj(self.wot, bias=False, xsrc=self.xd, dst=self.xd)

    def finalize(self):
        E = self.E
        allsems = ([self.s_gd, self.s_sy, self.s_setup, self.s_cxl, self.s_misc,
                    self.s_lor] + self.s_xt + self.s_w + self.s_k + self.s_v +
                   self.s_q + self.s_st + self.s_stc + self.s_stx)
        last = []
        if self.upto < 4:
            last = [E.emit("sy", lambda e: e.dma_start(out=self.y, in_=self.xq),
                           inc=self.s_sy, inck=16)]
        elif self.upto < 7:
            last = [E.emit("sy", lambda e: e.dma_start(out=self.y, in_=self.xd),
                           waits=list(self.last_store), inc=self.s_sy, inck=16)]
        else:
            last = list(self.last_store)
        if "xd" in self.tap_aps and self.last_store is not None:
            last = [E.emit("sy", lambda e:
                           e.dma_start(out=self.tap_aps["xd"], in_=self.xd),
                           waits=list(self.last_store), inc=self.s_sy, inck=16)]
        waits = last + [Pt(s, s.n) for s in allsems]
        E.emit("sy", lambda e: e.nop(), waits=waits)


def build_program(upto=7, taps=()):
    return Builder(upto=upto, taps=taps).build()



def _tile_w(W):
    dc = W.reshape(32, 128, -1, 128)
    return np.ascontiguousarray(dc.transpose(2, 1, 0, 3).reshape(-1, 128, 4096))


def _rope_tables():
    dim = 128
    inv_freq = (1.0 / (10000.0 ** (np.arange(0, dim, 2, dtype=np.float32)
                                   / np.float32(dim)))).astype(np.float32)
    t = np.arange(BT, dtype=np.float32)
    freqs = np.outer(t, inv_freq).astype(np.float32)
    emb = np.concatenate([freqs, freqs], axis=-1)
    return np.cos(emb).astype(np.float32), np.sin(emb).astype(np.float32)


_UPTO = 7
_TAPS = ()


def kernel(**inputs):
    x = np.ascontiguousarray(inputs["L0_attn_a_x"])
    qkvw = np.ascontiguousarray(inputs["L0_attn_a_attn_qkvw"])
    qkvb = np.asarray(inputs["L0_attn_a_attn_qkvb"])
    ow = np.ascontiguousarray(inputs["L0_attn_a_attn_ow"])
    ob = np.asarray(inputs["L0_attn_a_attn_ob"])
    kc = np.ascontiguousarray(np.asarray(inputs["L0_attn_b_k_cache"]).reshape(512, 16384))
    vc = np.ascontiguousarray(np.asarray(inputs["L0_attn_b_v_cache"]).reshape(512, 16384))
    wq = np.asarray(inputs["L0_attn_b_Wq"])
    wo = np.asarray(inputs["L0_attn_b_Wo"])
    bt = np.ascontiguousarray(np.asarray(inputs["L0_attn_b_block_table"]).astype(np.int32))
    fw = np.asarray(inputs["L0_ffn_weight"])
    fb = np.asarray(inputs["L0_ffn_bias"])
    lA = np.ascontiguousarray(inputs["L0_ffn_lora_A"])
    lB = np.ascontiguousarray(inputs["L0_ffn_lora_B"])
    lsi = np.ascontiguousarray(np.asarray(inputs["L0_ffn_lora_start_indices"]).astype(np.int32))

    qkvw = _tile_w(qkvw)
    ow = _tile_w(ow)
    wqt = _tile_w(np.ascontiguousarray(wq.T))
    wot = _tile_w(np.ascontiguousarray(wo.T))
    wft = _tile_w(np.ascontiguousarray(fw.T))
    qkvbT = np.ascontiguousarray(qkvb.reshape(96, 128).T)
    obT = np.ascontiguousarray(ob.reshape(32, 128).T)
    fbT = np.ascontiguousarray(fb.reshape(32, 128).T)
    cos_t, sin_t = _rope_tables()

    nc = build_program(upto=_UPTO, taps=_TAPS)
    in_maps = []
    for c in range(NCORES):
        b, qi = c // 4, c % 4
        xin = x[b]
        in_maps.append({
            "xin": xin,
            "xq": np.ascontiguousarray(xin[qi * NT:(qi + 1) * NT]),
            "qkvw": qkvw, "qkvbT": qkvbT, "ow": ow, "obT": obT,
            "wqt": wqt, "wot": wot, "wft": wft, "fbT": fbT,
            "kcache": kc, "vcache": vc, "btab": bt, "lsi": lsi,
            "bval": np.array([[float(b)]], dtype=np.float32),
            "loraA": lA, "loraB": lB,
            "cosf": np.ascontiguousarray(cos_t[qi * NT:(qi + 1) * NT].T),
            "sinf": np.ascontiguousarray(sin_t[qi * NT:(qi + 1) * NT].T),
        })
    res = run_bass_kernel_spmd(nc, in_maps, core_ids=list(range(NCORES)))
    out = np.empty((2, BT, D), dtype=np.float32)
    for c in range(NCORES):
        b, qi = c // 4, c % 4
        out[b, qi * NT:(qi + 1) * NT] = res.results[c]["y"]
    if _TAPS:
        kernel.tap_results = res.results
    return out


# version 04
# speedup vs baseline: 1.0390x; 1.0390x over previous
import numpy as np

import concourse.bass as bass
import concourse.mybir as mybir
from concourse.bass_utils import run_bass_kernel_spmd
from concourse.bass_core_rust import IndirectOffsetOnAxis

F32 = mybir.dt.float32
BF16 = mybir.dt.bfloat16
I32 = mybir.dt.int32
AF = mybir.ActivationFunctionType
ALU = mybir.AluOpType
AX = mybir.AxisListType

D = 4096
NT = 1024
BT = 4096
NH = 32
HD = 128
SCALE = float(1.0 / np.sqrt(128.0))
NCORES = 8



class Sem:
    __slots__ = ("name", "n", "h")

    def __init__(self, name):
        self.name = name
        self.n = 0
        self.h = None


class Pt:
    __slots__ = ("sem", "val")

    def __init__(self, sem, val):
        self.sem = sem
        self.val = val


class Slot:
    __slots__ = ("ws", "r")

    def __init__(self):
        self.ws = []
        self.r = []

    @property
    def w(self):
        return self.ws[-1] if self.ws else None

    def rw(self):
        return list(self.ws)

    def for_write(self):
        return list(self.r) + list(self.ws)

    def wrote(self, pt):
        assert pt is not None
        self.ws = [pt]
        self.r = []

    def add_write(self, pt):
        assert pt is not None
        self.ws.append(pt)

    def read(self, pt):
        if pt is not None:
            self.r.append(pt)

    def seed(self, pts):
        self.r.extend(p for p in pts if p is not None)


class Emitter:
    ENGINES = ("pe", "dv", "ac", "gp", "sy")

    def __init__(self):
        self.q = {e: [] for e in self.ENGINES}
        self.sems = {}
        self.waited = {e: {} for e in self.ENGINES}

    def S(self, name):
        if name not in self.sems:
            self.sems[name] = Sem(name)
        return self.sems[name]

    def emit(self, eng, fn, waits=(), inc=None, inck=1):
        seen = {}
        for pt in waits:
            if pt is None:
                continue
            key = pt.sem.name
            cur = seen.get(key, self.waited[eng].get(key, -1))
            if pt.val > cur:
                seen[key] = pt.val
        ws = []
        for key, val in seen.items():
            self.waited[eng][key] = max(self.waited[eng].get(key, -1), val)
            ws.append((self.sems[key], val))
        pt_out = None
        if inc is not None:
            inc.n += inck
            pt_out = Pt(inc, inc.n)
        self.q[eng].append((ws, fn, inc, inck))
        return pt_out

    def replay(self, block):
        def run(stream):
            def body(eng):
                for ws, fn, inc, inck in stream:
                    for sem, val in ws:
                        eng.wait_ge(sem.h, val)
                    ins = fn(eng)
                    if inc is not None:
                        ins.then_inc(inc.h, inck)
            return body

        block.tensor(run(self.q["pe"]))
        block.vector(run(self.q["dv"]))
        block.scalar(run(self.q["ac"]))
        block.gpsimd(run(self.q["gp"]))
        block.sync(run(self.q["sy"]))



GCOLS = 105536
GA0 = 0
GA1 = 32768
GC = 65536
GD = 81920
GE = 90112
GH = 98304
ST0 = 99392
ST1 = 100416
GD2 = 101440
ST2 = GD2

A1_K0 = GA1 + 0
A1_K1 = GA1 + 4096
A1_V0 = GA1 + 8192
A1_V1 = GA1 + 12288
A1_Q0 = GA1 + 16384
A1_Q1 = GA1 + 17408
A1_CTXS0 = GA1 + 18432
A1_CTXS1 = GA1 + 19456
A1_GST = GA1 + 20480
A1_GSTB = GA1 + 24576
A1_COS = GA1 + 28672
A1_SIN = GA1 + 29696
A1_ROT = GA1 + 30720
A1_RT1 = GA1 + 31744

D_AHT = GD + 0
D_BHT = GD + 512
D_ZSB = GD + 4608

H_ID = GH + 0
H_D0 = GH + 128
H_QKVB = GH + 640
H_OBT = GH + 832
H_FBT = GH + 896
H_ANM = GH + 960
H_ASUM = GH + 976
H_ASS = GH + 992
H_ARC = GH + 1000
H_LN = GH + 1008
H_LSI_I = GH + 1040
H_LSI_F = GH + 1048
H_BVAL = GH + 1056
H_TMP = GH + 1058
H_HSEL = GH + 1074
H_BTAB = GH + 1080

LN_S1, LN_NEGM, LN_M2, LN_VAR, LN_LNV, LN_RSTD, LN_SUMSQ = range(7)


def lncol(stat, par):
    return H_LN + stat * 4 + par * 2


class Builder:
    def __init__(self, upto=7, taps=()):
        self.upto = upto
        self.taps = taps
        self.nc = bass.Bass("TRN2", target_bir_lowering=False)
        nc = self.nc
        self.xq = nc.dram_tensor("xq", [NT, D], F32, kind="ExternalInput").ap()
        self.qkvw = nc.dram_tensor("qkvw", [96, 128, 4096], F32, kind="ExternalInput").ap()
        self.qkvbT = nc.dram_tensor("qkvbT", [128, 96], F32, kind="ExternalInput").ap()
        self.ow = nc.dram_tensor("ow", [32, 128, 4096], F32, kind="ExternalInput").ap()
        self.obT = nc.dram_tensor("obT", [128, 32], F32, kind="ExternalInput").ap()
        self.wqt = nc.dram_tensor("wqt", [32, 128, 4096], F32, kind="ExternalInput").ap()
        self.wot = nc.dram_tensor("wot", [32, 128, 4096], F32, kind="ExternalInput").ap()
        self.wft = nc.dram_tensor("wft", [32, 128, 4096], F32, kind="ExternalInput").ap()
        self.fbT = nc.dram_tensor("fbT", [128, 32], F32, kind="ExternalInput").ap()
        self.kcache = nc.dram_tensor("kcache", [512, 16384], F32, kind="ExternalInput").ap()
        self.vcache = nc.dram_tensor("vcache", [512, 16384], F32, kind="ExternalInput").ap()
        self.btab = nc.dram_tensor("btab", [256], I32, kind="ExternalInput").ap()
        self.lsi = nc.dram_tensor("lsi", [4], I32, kind="ExternalInput").ap()
        self.bval = nc.dram_tensor("bval", [1, 1], F32, kind="ExternalInput").ap()
        self.loraA = nc.dram_tensor("loraA", [3, 16, D], F32, kind="ExternalInput").ap()
        self.loraB = nc.dram_tensor("loraB", [3, D, 16], F32, kind="ExternalInput").ap()
        self.cosf = nc.dram_tensor("cosf", [128, NT], F32, kind="ExternalInput").ap()
        self.sinf = nc.dram_tensor("sinf", [128, NT], F32, kind="ExternalInput").ap()
        self.y = nc.dram_tensor("y", [NT, D], F32, kind="ExternalOutput").ap()
        self.xd = nc.dram_tensor("xd", [NT, D], F32, kind="Internal").ap()
        self.kvin = nc.dram_tensor("kvin", [2 * D, NT], BF16, kind="Internal").ap()
        self.kvag = nc.dram_tensor("kvag", [8 * D, NT], BF16, kind="Internal").ap()
        self.qfmd = nc.dram_tensor("qfmd", [D, NT], BF16, kind="Internal").ap()
        self.ctxd = nc.dram_tensor("ctxd", [D, NT], BF16, kind="Internal").ap()
        self.kgd = nc.dram_tensor("kgd", [8, BT, HD], BF16, kind="Internal").ap()
        self.vgd = nc.dram_tensor("vgd", [8, BT, HD], BF16, kind="Internal").ap()

        self.tap_aps = {}
        shapes = {"kvag": ([8 * D, NT], BF16),
                  "qfmd": ([D, NT], BF16), "ctxd": ([D, NT], BF16),
                  "kgd": ([8, BT, HD], BF16), "vgd": ([8, BT, HD], BF16),
                  "xd": ([NT, D], F32)}
        for t in taps:
            if t == "adbg":
                self.dbg_pbT = nc.dram_tensor("tap_pbT", [128, 32768], BF16,
                                              kind="ExternalOutput").ap()
                self.dbg_probs = nc.dram_tensor("tap_probs", [128, 8192], BF16,
                                                kind="ExternalOutput").ap()
                self.dbg_small = nc.dram_tensor("tap_small", [128, 1024], F32,
                                                kind="ExternalOutput").ap()
                continue
            sh, dt = shapes[t]
            self.tap_aps[t] = nc.dram_tensor("tap_" + t, sh, dt,
                                             kind="ExternalOutput").ap()

        E = Emitter()
        self.E = E
        self.s_pe = E.S("s_pe")
        self.s_dv = E.S("s_dv")
        self.s_ac = E.S("s_ac")
        self.s_gp = E.S("s_gp")
        self.s_sy = E.S("s_sy")
        self.s_sc = E.S("s_sc")
        self.s_gd = E.S("s_gd")
        self.s_setup = E.S("s_setup")
        self.s_xt = [E.S("s_xt0"), E.S("s_xt1")]
        self.s_w = [E.S("s_w0"), E.S("s_w1")]
        self.s_k = [E.S("s_k0"), E.S("s_k1")]
        self.s_v = [E.S("s_v0"), E.S("s_v1")]
        self.s_q = [E.S("s_q0"), E.S("s_q1")]
        self.s_st = [E.S("s_st0"), E.S("s_st1"), E.S("s_st2")]
        self.s_stc = [E.S("s_stc0"), E.S("s_stc1")]
        self.s_stx = [E.S("s_stx0"), E.S("s_stx1")]
        self.s_cxl = E.S("s_cxl")
        self.s_misc = E.S("s_misc")
        self.s_lor = E.S("s_lor")
        self.s_gd2 = E.S("s_gd2")
        self.s_ag = E.S("s_ag")

        self.xt = [Slot(), Slot()]
        self.lnt = [Slot(), Slot()]
        self.lns = [Slot(), Slot()]
        self.wst = [Slot(), Slot()]
        self.xfm = [Slot(), Slot()]
        self.stg = [Slot(), Slot(), Slot()]
        self.psA = Slot()
        self.psB = Slot()
        self.psC = [Slot(), Slot()]
        self.psD = Slot()
        self.kh = [Slot(), Slot()]
        self.vh = [Slot(), Slot()]
        self.qh = [Slot(), Slot()]
        self.ctxs = [Slot(), Slot()]
        self.gst = Slot()
        self.rA0 = Slot()
        self.rA1 = Slot()
        self.identity = Slot()
        self.biases = Slot()
        self.att_nm = [Slot(), Slot(), Slot()]
        self.att_sums = [Slot(), Slot(), Slot()]
        self.att_ss = [Slot(), Slot(), Slot()]
        self.gstB = Slot()
        self.probsx = Slot()
        self._gather_steps = []
        self.kst_pts = [[] for _ in range(32)]
        self.vst_pts = [[] for _ in range(32)]
        self.qst_pts = [[] for _ in range(32)]
        self.lora = {k: Slot() for k in ("AhT", "BhT", "zsb", "hsel")}
        self.btsb = Slot()
        self.last_store = None

    def g_bf(self, col, n, parts=128, p0=0):
        return self.grand[p0:p0 + parts, col:col + n]

    def g_f32(self, col, n, parts=128, p0=0):
        return self.grand_f32[p0:p0 + parts, col // 2:col // 2 + n]

    def g_i32(self, col, n, parts=128, p0=0):
        return self.grand_i32[p0:p0 + parts, col // 2:col // 2 + n]

    def head_guards(self):
        pts = []
        for s in (self.kh + self.vh + self.qh + self.ctxs + [self.gst, self.gstB]):
            pts.extend(s.for_write())
        return pts

    def build(self):
        from contextlib import ExitStack
        nc, E = self.nc, self.E
        with ExitStack() as stack:
            grand_h = stack.enter_context(nc.sbuf_tensor("grand", [128, GCOLS], BF16))
            arena_h = stack.enter_context(nc.psum_tensor("arena", [128, 4096], F32))
            for name in list(E.sems):
                E.sems[name].h = stack.enter_context(nc.semaphore(name))
            self.grand = grand_h[:]
            self.grand_f32 = grand_h[:].bitcast(F32)
            self.grand_i32 = grand_h[:].bitcast(I32)
            self.arena = arena_h[:]
            self.arena_bf = arena_h[:].bitcast(BF16)

            self.p0_setup()
            if self.upto >= 1:
                self.p1_attn_a_proj()
            if self.upto >= 2:
                self.p2_gather()
                if self.upto == 2:
                    self.gather_flush()
            if self.upto >= 3:
                self.rA0.seed(self.xfm[0].for_write())
                for s in (self.kh + self.vh + self.qh + self.ctxs +
                          [self.gst, self.gstB]):
                    s.seed(self.xfm[1].for_write())
                self.probsx.seed(self.stg[2].for_write())
                self.p3_attention("a")
                if self.upto >= 2:
                    self.gather_flush()
                if "ctxd" in self.tap_aps:
                    self.tap_dram(self.tap_aps["ctxd"], self.ctxd, list(self.s_stc))
                if "adbg" in self.taps:
                    E = self.E
                    gates = [Pt(s, s.n) for s in (self.s_dv, self.s_ac, self.s_pe,
                                                  self.s_gp, self.s_sc)]
                    E.emit("sy", lambda e, o=self.dbg_pbT, i=self.g_bf(GA0, 32768):
                           e.dma_start(out=o, in_=i), waits=gates,
                           inc=self.s_sy, inck=16)
                    E.emit("sy", lambda e, o=self.dbg_probs, i=self.g_bf(GD, 8192):
                           e.dma_start(out=o, in_=i), waits=gates,
                           inc=self.s_sy, inck=16)
                    E.emit("sy", lambda e, o=self.dbg_small, i=self.g_f32(GH, 1024):
                           e.dma_start(out=o, in_=i), waits=gates,
                           inc=self.s_sy, inck=16)
            if self.upto >= 4:
                self.p4_outproj(self.ow, bias=True, xsrc=self.xq, dst=self.xd)
            if self.upto >= 5:
                self.p5_ffn(dst=self.xd)
            if self.upto >= 6:
                self.p6_attn_b()
            if self.upto >= 7:
                self.p5_ffn(dst=self.y)
            self.finalize()

            blk = stack.enter_context(nc.Block())
            E.replay(blk)
        return nc


    def p0_setup(self):
        E = self.E
        g = self.g_f32
        for dst_col, n, src in ((H_QKVB, 96, self.qkvbT), (H_OBT, 32, self.obT),
                                (H_FBT, 32, self.fbT)):
            pt = E.emit("sy", lambda e, o=g(dst_col, n), i=src:
                        e.dma_start(out=o, in_=i), inc=self.s_setup, inck=16)
        idap = self.g_bf(H_ID, 128)
        p1 = E.emit("gp", lambda e, o=idap: e.iota(
            o, pattern=[[1, 128]], base=0, channel_multiplier=-1,
            allow_small_or_imprecise_dtypes=True), inc=self.s_gp)
        p2 = E.emit("gp", lambda e, o=idap: e.tensor_single_scalar(
            o, o, 0.0, ALU.is_equal), waits=[p1], inc=self.s_gp)
        self.identity.wrote(p2)
        p3 = E.emit("sy", lambda e, o=self.g_i32(H_LSI_I, 4),
                    i=self.lsi.unsqueeze(0).broadcast_to([128, 4]):
                    e.dma_start(out=o, in_=i), inc=self.s_setup, inck=16)
        p4 = E.emit("sy", lambda e, o=g(H_BVAL, 1),
                    i=self.bval.broadcast_to([128, 1]):
                    e.dma_start(out=o, in_=i), inc=self.s_setup, inck=16)
        setup_pt = Pt(self.s_setup, self.s_setup.n)
        self.biases.wrote(setup_pt)
        p4 = setup_pt
        p5 = E.emit("dv", lambda e, o=g(H_LSI_F, 4), i=self.g_i32(H_LSI_I, 4):
                    e.tensor_copy(o, i), waits=[setup_pt], inc=self.s_dv)
        bv = g(H_BVAL, 1)
        lf = g(H_LSI_F, 4)

        def tcol(k):
            return g(H_TMP + 2 * k, 1)

        def tt(o, a, b, op, w):
            return E.emit("dv", lambda e, o=o, a=a, b=b, op=op:
                          e.tensor_tensor(o, a, b, op), waits=w, inc=self.s_dv)

        def ts2(o, a, s1, s2, w):
            return E.emit("dv", lambda e, o=o, a=a, s1=s1, s2=s2:
                          e.tensor_scalar(o, a, s1, s2, ALU.mult, ALU.add),
                          waits=w, inc=self.s_dv)

        c = [tt(tcol(k), bv, lf[:, k:k + 1], ALU.is_lt, [p4, p5]) for k in range(4)]
        pu = ts2(tcol(4), tcol(0), -1.0, 1.0, [c[0]])
        pv1 = tt(tcol(5), tcol(1), tcol(4), ALU.mult, [c[1], pu])
        pw = ts2(tcol(6), tcol(1), -2.0, 2.0, [c[1]])
        pv2 = tt(tcol(7), tcol(5), tcol(6), ALU.add, [pv1, pw])
        pinner = tt(tcol(4), tcol(2), tcol(7), ALU.mult, [c[2], pv2])
        pr2 = ts2(tcol(6), tcol(3), -1.0, 4.0, [c[3]])
        pnc2 = ts2(tcol(5), tcol(2), -1.0, 1.0, [c[2]])
        po2 = tt(tcol(6), tcol(5), tcol(6), ALU.mult, [pr2, pnc2])
        pres = tt(tcol(7), tcol(4), tcol(6), ALU.add, [pinner, po2])
        hs = None
        for a in range(3):
            hs = E.emit("dv", lambda e, o=g(H_HSEL + 2 * a, 1), i=tcol(7), v=float(2 + a):
                        e.tensor_single_scalar(o, i, v, ALU.is_equal),
                        waits=[pres], inc=self.s_dv)
        self.lora["hsel"].wrote(hs)

    def ln_pass(self, src, row0, ntiles, xfm_col, xfm_slot, gate=()):
        E = self.E
        g = self.g_f32
        last_evac = None
        first = True
        for j in range(ntiles):
            par = j % 2
            xt = self.xt[par]
            xtap = g(GC + par * 8192, 4096)
            lns = self.lns[par]
            r0 = row0 + j * 128
            pl = E.emit("sy", lambda e, o=xtap, i=src[r0:r0 + 128, :]:
                        e.dma_start(out=o, in_=i),
                        waits=xt.for_write() + (gate or []),
                        inc=self.s_xt[par], inck=16)
            xt.wrote(pl)
            lnt = self.lnt[par]
            lntap = self.g_bf(GD + par * 4096, 4096)
            c_sq, c_s1, c_nm, c_m2, c_v, c_lnv, c_rstd = (
                g(lncol(LN_SUMSQ, par), 1), g(lncol(LN_S1, par), 1),
                g(lncol(LN_NEGM, par), 1), g(lncol(LN_M2, par), 1),
                g(lncol(LN_VAR, par), 1), g(lncol(LN_LNV, par), 1),
                g(lncol(LN_RSTD, par), 1))
            psq = E.emit("ac", lambda e, o=lntap, i=xtap, a=c_sq:
                         e.activation(o, i, AF.Square, bias=0.0, scale=1.0, accum_out=a),
                         waits=[xt.w] + lnt.for_write() + lns.for_write(), inc=self.s_ac)
            ps1 = E.emit("dv", lambda e, o=c_s1, i=xtap:
                         e.reduce_sum(o, i, axis=AX.X), waits=[xt.w], inc=self.s_dv)
            pnm = E.emit("dv", lambda e, o=c_nm, i=c_s1:
                         e.tensor_scalar_mul(o, i, -1.0 / 4096.0),
                         waits=[ps1], inc=self.s_dv)
            pm2 = E.emit("dv", lambda e, o=c_m2, i=c_nm:
                         e.tensor_scalar(o, i, i, -1e-5, ALU.mult, ALU.add),
                         waits=[pnm], inc=self.s_dv)
            pvar = E.emit("dv", lambda e, o=c_v, a=c_sq, b=c_m2:
                          e.scalar_tensor_tensor(o, a, 1.0 / 4096.0, b,
                                                 ALU.mult, ALU.subtract),
                          waits=[psq, pm2], inc=self.s_dv)
            plnv = E.emit("ac", lambda e, o=c_lnv, i=c_v:
                          e.activation(o, i, AF.Ln, bias=0.0, scale=1.0),
                          waits=[pvar], inc=self.s_ac)
            prstd = E.emit("ac", lambda e, o=c_rstd, i=c_lnv:
                           e.activation(o, i, AF.Exp, bias=0.0, scale=-0.5),
                           waits=[plnv], inc=self.s_ac)
            pnorm = E.emit("dv", lambda e, o=lntap, i=xtap, s1=c_nm, s2=c_rstd:
                           e.tensor_scalar(o, i, s1, s2, ALU.add, ALU.mult),
                           waits=[prstd, pnm, xt.w, psq], inc=self.s_dv)
            lnt.wrote(pnorm)
            lns.read(pnorm)
            xt.read(pnorm)
            for bi in range(4):
                pc = self.psC[bi % 2]
                pcap = self.arena_bf[:, 4096 + (bi % 2) * 1024:
                                     4096 + (bi % 2) * 1024 + 1024]
                gpt = None
                for jj in range(8):
                    dc = bi * 8 + jj
                    w = ([pnorm, self.identity.w] + pc.for_write()) if jj == 0 else []
                    gpt = E.emit("pe", lambda e, o=pcap[:, jj * 128:jj * 128 + 128],
                                 i=lntap[:, dc * 128:dc * 128 + 128],
                                 idn=self.g_bf(H_ID, 128):
                                 e.transpose(o, i, idn),
                                 waits=w, inc=self.s_pe if jj == 7 else None)
                pc.wrote(gpt)
                lnt.read(gpt)
                base = xfm_col + (bi * 8) * 1024 + j * 128
                dst = self.grand[:, base: base + 8 * 1024].rearrange(
                    "p (c t) -> p c t", c=8)[:, :, 0:128]
                pev = E.emit("dv", lambda e, o=dst,
                             i=pcap.rearrange("p (c t) -> p c t", c=8):
                             e.tensor_copy(o, i),
                             waits=[gpt] + (xfm_slot.for_write() if first else []),
                             inc=self.s_dv)
                first = False
                pc.read(pev)
                last_evac = pev
        xfm_slot.wrote(last_evac)
        return last_evac

    def proj_fm(self, wdram, wchunk0, nchunks, bias_fn, scale, xfm_col, xfm_slot, sink,
                lora=False):
        E = self.E
        for oc in range(nchunks):
            ws = self.wst[oc % 2]
            wsap = self.g_bf(GE + (oc % 2) * 4096, 4096)
            pw = E.emit("gp", lambda e, o=wsap, i=wdram[wchunk0 + oc]:
                        e.dma_start(out=o, in_=i),
                        waits=ws.for_write(), inc=self.s_w[oc % 2], inck=16)
            ws.wrote(pw)
            ps = self.psA if (oc % 2 == 0) else self.psB
            psap = self.arena[:, (oc % 2) * 1024:(oc % 2) * 1024 + 1024]
            gpt = None
            for dc in range(32):
                for sp in range(2):
                    w = ([pw, xfm_slot.w] + ps.for_write()) if (dc == 0 and sp == 0) else []
                    is_last = (dc == 31 and sp == 1) and not lora
                    gpt = E.emit("pe", lambda e, o=psap[:, sp * 512:sp * 512 + 512],
                                 l=wsap[:, dc * 128:dc * 128 + 128],
                                 r=self.grand[:, xfm_col + dc * 1024 + sp * 512:
                                              xfm_col + dc * 1024 + sp * 512 + 512],
                                 st=(dc == 0), sp_=is_last:
                                 e.matmul(o, l, r, start=st, stop=sp_),
                                 waits=w, inc=self.s_pe if is_last else None)
            if lora:
                for sp in range(2):
                    is_last = sp == 1
                    gpt = E.emit("pe", lambda e, o=psap[:, sp * 512:sp * 512 + 512],
                                 l=self.g_bf(D_BHT + oc * 128, 128, parts=16),
                                 r=self.g_bf(D_ZSB + sp * 512, 512, parts=16),
                                 sp_=is_last:
                                 e.matmul(o, l, r, start=False, stop=sp_),
                                 waits=([self.lora["BhT"].w, self.lora["zsb"].w]
                                        if sp == 0 else []),
                                 inc=self.s_pe if is_last else None)
            ps.wrote(gpt)
            ws.read(gpt)
            xfm_slot.read(gpt)
            if lora:
                self.lora["BhT"].read(gpt)
                self.lora["zsb"].read(gpt)
            st = self.stg[oc % 2]
            stap = self.g_bf(ST0 + (oc % 2) * 1024, 1024)
            bap = bias_fn(oc) if bias_fn is not None else None
            if bap is not None and scale is not None:
                pev = E.emit("dv", lambda e, o=stap, i=psap, b=bap, s=scale:
                             e.tensor_scalar(o, i, b, s, ALU.add, ALU.mult),
                             waits=[gpt, self.biases.w] + st.for_write(), inc=self.s_dv)
            elif bap is not None:
                pev = E.emit("dv", lambda e, o=stap, i=psap, b=bap:
                             e.tensor_scalar(o, i, b, None, ALU.add),
                             waits=[gpt, self.biases.w] + st.for_write(), inc=self.s_dv)
            elif scale is not None:
                pev = E.emit("dv", lambda e, o=stap, i=psap, s=scale:
                             e.tensor_scalar_mul(o, i, s),
                             waits=[gpt] + st.for_write(), inc=self.s_dv)
            else:
                pev = E.emit("dv", lambda e, o=stap, i=psap: e.tensor_copy(o, i),
                             waits=[gpt] + st.for_write(), inc=self.s_dv)
            ps.read(pev)
            st.wrote(pev)
            sink(oc, stap, st, pev)

    def qkvb_col(self, chunk0):
        def f(oc):
            return self.g_f32(H_QKVB, 96)[:, chunk0 + oc:chunk0 + oc + 1]
        return f

    def p1_attn_a_proj(self):
        E = self.E
        xfm_slot = self.xfm[0]
        self.ln_pass(self.xq, 0, 8, GA0, xfm_slot)

        def kvsink(base):
            def sink(oc, stap, st, pev):
                p = E.emit("ac", lambda e,
                           o=self.kvin[base + oc * 128:base + (oc + 1) * 128, :],
                           i=stap: e.dma_start(out=o, in_=i),
                           waits=[pev], inc=self.s_st[oc % 2], inck=16)
                st.read(p)
            return sink

        self.proj_fm(self.qkvw, 32, 32, self.qkvb_col(32), None,
                     GA0, xfm_slot, kvsink(0))
        self.proj_fm(self.qkvw, 64, 32, self.qkvb_col(64), None,
                     GA0, xfm_slot, kvsink(D))
        kv_done = [Pt(s, s.n) for s in self.s_st]
        self.ag_pt = E.emit("gp", lambda e: e.collective_compute(
            "AllGather", ALU.bypass,
            replica_groups=[[0, 1, 2, 3], [4, 5, 6, 7]],
            ins=[self.kvin.opt()], outs=[self.kvag.opt()]),
            waits=kv_done, inc=self.s_ag, inck=1)

        def qsink(oc, stap, st, pev):
            p = E.emit("ac", lambda e, o=self.qfmd[oc * 128:(oc + 1) * 128, :],
                       i=stap: e.dma_start(out=o, in_=i),
                       waits=[pev], inc=self.s_st[oc % 2], inck=16)
            st.read(p)
            self.qst_pts[oc].append(p)
        self.proj_fm(self.qkvw, 0, 32, self.qkvb_col(0), SCALE,
                     GA0, xfm_slot, qsink)
        for t in ("qfmd",):
            if t in self.tap_aps:
                self.tap_dram(self.tap_aps[t], getattr(self, t), list(self.s_st))
        if "kvag" in self.tap_aps:
            self.tap_dram(self.tap_aps["kvag"], self.kvag, [self.s_ag])

    def tap_dram(self, dst, src, gate_sem):
        sems = gate_sem if isinstance(gate_sem, list) else [gate_sem]
        self.E.emit("sy", lambda e, o=dst, i=src: e.dma_start(out=o, in_=i),
                    waits=[Pt(s, s.n) for s in sems], inc=self.s_sy, inck=16)

    def p2_gather(self):
        E = self.E
        pb = E.emit("gp", lambda e, o=self.g_i32(H_BTAB, 2),
                    i=self.btab.rearrange("(a p) -> p a", p=128):
                    e.dma_start(out=o, in_=i, allow_slow_non_contiguous=True),
                    inc=self.s_gd, inck=16)
        self.btsb.wrote(pb)
        steps = []
        for src_, dst in ((self.kcache, self.kgd), (self.vcache, self.vgd)):
            for h in range(8):
                for half in range(2):
                    steps.append((src_, dst, h, half))

        def make_step(idx):
            src_, dst, h, half = steps[idx]
            par = idx % 2
            gslot = self.gst if par == 0 else self.gstB
            gsem = self.s_gd if par == 0 else self.s_gd2
            gstap = self.g_f32(A1_GST if par == 0 else A1_GSTB, 2048)

            def emit():
                pg = E.emit("gp", lambda e, o=gstap, i=src_, eo=h * 2048,
                            ix=self.g_i32(H_BTAB, 2)[:, half:half + 1]:
                            e.indirect_dma_start(o, None, i,
                                                 IndirectOffsetOnAxis(ix, 0),
                                                 element_offset=eo),
                            waits=[pb] + gslot.for_write(), inc=gsem, inck=16)
                gslot.wrote(pg)
                ps = E.emit("gp", lambda e,
                            o=dst[h, half * 2048:(half + 1) * 2048, :]
                            .rearrange("(p r) d -> p r d", p=128),
                            i=gstap.rearrange("p (r d) -> p r d", r=16):
                            e.dma_start(out=o, in_=i),
                            waits=[pg], inc=gsem, inck=16)
                gslot.read(ps)
            return emit

        self._gather_steps = [make_step(i) for i in range(len(steps))]

    def gather_tick(self, n=2):
        for _ in range(n):
            if self._gather_steps:
                self._gather_steps.pop(0)()

    def gather_flush(self):
        while self._gather_steps:
            self._gather_steps.pop(0)()
        self.gather_gate = [Pt(self.s_gd, self.s_gd.n),
                            Pt(self.s_gd2, self.s_gd2.n)]
        for t in ("kgd", "vgd"):
            if t in self.tap_aps:
                self.tap_dram(self.tap_aps[t], getattr(self, t), self.s_gd)

    def p3_attention(self, mode):
        E = self.E
        heads = [None] * NH

        def inplace_tm_fm_build(S, Sap, gate_pt):
            for bi in range(4):
                pc = self.psC[bi % 2]
                pcap = self.arena_bf[:, 4096 + (bi % 2) * 1024:
                                     4096 + (bi % 2) * 1024 + 1024]
                gpt = None
                for jj in range(8):
                    cc = bi * 8 + jj
                    w = ([gate_pt, self.identity.w] + pc.for_write()) if jj == 0 else []
                    gpt = E.emit("pe",
                                 lambda e, o=pcap[:, jj * 128:jj * 128 + 128],
                                 i=Sap[:, cc * 128:cc * 128 + 128],
                                 idn=self.g_bf(H_ID, 128):
                                 e.transpose(o, i, idn), waits=w,
                                 inc=self.s_pe if jj == 7 else None)
                pc.wrote(gpt)
                S.read(gpt)
                pev = E.emit("dv", lambda e,
                             o=Sap[:, bi * 1024:bi * 1024 + 1024],
                             i=pcap: e.tensor_copy(o, i),
                             waits=[gpt], inc=self.s_dv)
                pc.read(pev)
                if bi == 0:
                    S.wrote(pev)
                else:
                    S.add_write(pev)

        def load_head(h):
            if mode == "a":
                K, Kap = self.kh[h % 2], self.g_bf(A1_K0 + (h % 2) * 4096, 4096)
                V, Vap = self.vh[h % 2], self.g_bf(A1_V0 + (h % 2) * 4096, 4096)
                agv = self.kvag.rearrange("(j c p) t -> c p j t", j=4, p=128)
                pk = E.emit("sy", lambda e, o=Kap.rearrange("p (j t) -> p j t", j=4),
                            i=agv[h]:
                            e.dma_start(out=o, in_=i),
                            waits=K.for_write() + [self.ag_pt],
                            inc=self.s_k[h % 2], inck=16)
                K.wrote(pk)
                pv = E.emit("sy", lambda e, o=Vap.rearrange("p (j t) -> p j t", j=4),
                            i=agv[32 + h]:
                            e.dma_start(out=o, in_=i),
                            waits=V.for_write() + [self.ag_pt],
                            inc=self.s_v[h % 2], inck=16)
                V.wrote(pv)
                inplace_tm_fm_build(V, Vap, pv)
            else:
                kvh = h // 4
                K, Kap = self.kh[kvh % 2], self.g_bf(A1_K0 + (kvh % 2) * 4096, 4096)
                V, Vap = self.vh[kvh % 2], self.g_bf(A1_V0 + (kvh % 2) * 4096, 4096)
                if h % 4 == 0:
                    pk = E.emit("sy", lambda e,
                                o=Kap.rearrange("p (c d) -> p c d", c=32),
                                i=self.kgd[kvh].rearrange("(c p) d -> p c d", p=128):
                                e.dma_start(out=o, in_=i),
                                waits=K.for_write() + self.gather_gate,
                                inc=self.s_k[kvh % 2], inck=16)
                    K.wrote(pk)
                    inplace_tm_fm_build(K, Kap, pk)
                    pv = E.emit("sy", lambda e,
                                o=Vap.rearrange("p (c d) -> p c d", c=32),
                                i=self.vgd[kvh].rearrange("(c p) d -> p c d", p=128):
                                e.dma_start(out=o, in_=i),
                                waits=V.for_write() + self.gather_gate,
                                inc=self.s_v[kvh % 2], inck=16)
                    V.wrote(pv)
            Q, Qap = self.qh[h % 2], self.g_bf(A1_Q0 + (h % 2) * 1024, 1024)
            pq = E.emit("sy", lambda e, o=Qap, i=self.qfmd[h * 128:(h + 1) * 128, :]:
                        e.dma_start(out=o, in_=i),
                        waits=Q.for_write() + self.qst_pts[h],
                        inc=self.s_q[h % 2], inck=16)
            Q.wrote(pq)
            heads[h] = (K, Kap, V, Vap, Q, Qap)

        def emit_ctx(h):
            K, Kap, V, Vap, Q, Qap = heads[h]
            ctxap = self.arena[:, 3072:4096]
            gpt = None
            for c in range(32):
                for sp in range(2):
                    w = (self.rA0.rw() + V.rw() + self.psD.for_write()) \
                        if (c == 0 and sp == 0) else []
                    is_last = c == 31 and sp == 1
                    gpt = E.emit("pe", lambda e, o=ctxap[:, sp * 512:sp * 512 + 512],
                                 l=Vap[:, c * 128:c * 128 + 128],
                                 r=self.grand[:, GA0 + c * 1024 + sp * 512:
                                              GA0 + c * 1024 + sp * 512 + 512],
                                 st=(c == 0), spp=(c == 31):
                                 e.matmul(o, l, r, start=st, stop=spp),
                                 waits=w, inc=self.s_pe if is_last else None)
            self.psD.wrote(gpt)
            self.rA0.read(gpt)
            V.read(gpt)
            cs = self.ctxs[h % 2]
            csap = self.g_bf(A1_CTXS0 + (h % 2) * 1024, 1024)
            pev = E.emit("dv", lambda e, o=csap, i=ctxap: e.tensor_copy(o, i),
                         waits=[gpt] + cs.for_write(), inc=self.s_dv)
            self.psD.read(pev)
            cs.wrote(pev)
            pst = E.emit("ac", lambda e, o=self.ctxd[h * 128:(h + 1) * 128, :], i=csap:
                         e.dma_start(out=o, in_=i), waits=[pev],
                         inc=self.s_stc[h % 2], inck=16)
            cs.read(pst)

        NU = NH * 8
        unit_state = [None] * NU
        for gu in range(NU + 2):
            h, i = divmod(gu, 8)
            if gu == 0:
                load_head(0)
                load_head(1)
            if gu < NU:
                self.unit_quarters(gu, 0, 2, heads[h], unit_state)
            if gu >= 2:
                self.unit_combine_T(gu - 2, unit_state)
                if (gu - 2) % 8 == 7:
                    hh = (gu - 2) // 8
                    emit_ctx(hh)
                    if hh + 2 < NH:
                        load_head(hh + 2)
                    if mode == "a":
                        self.gather_tick(2)
            if gu < NU:
                self.unit_quarters(gu, 2, 4, heads[h], unit_state)

    def probs3(self, par):
        slots = [self.lnt[0], self.lnt[1], self.probsx]
        cols = [GD, GD + 4096, GD2]
        return slots[par], self.g_bf(cols[par], 4096)

    def unit_quarters(self, gu, k0, k1, head, unit_state):
        E = self.E
        g = self.g_f32
        K, Kap, V, Vap, Q, Qap = head
        i = gu % 8
        par = gu % 3
        if unit_state[gu] is None:
            unit_state[gu] = {"par": par}
        stt = unit_state[gu]
        probs, probsap = self.probs3(par)
        stt["probs"], stt["probsap"] = probs, probsap
        nm_ap = g(H_ANM + 2 * par, 1)
        sums_ap = g(H_ASUM + 4 * par, 2)
        stt["nm_ap"], stt["sums_ap"] = nm_ap, sums_ap
        half = k0 // 2
        mmpts = []
        for k in range(k0, k1):
            ps = self.psA if k % 2 == 0 else self.psB
            psap = self.arena[:, (k % 2) * 1024:(k % 2) * 1024 + 1024]
            gpt = None
            for sp in range(2):
                w = ([K.w, Q.w] + ps.for_write()) if sp == 0 else []
                gpt = E.emit("pe", lambda e, o=psap[:, sp * 512:sp * 512 + 512],
                             l=Qap[:, i * 128:i * 128 + 128],
                             r=Kap[:, k * 1024 + sp * 512:k * 1024 + sp * 512 + 512]:
                             e.matmul(o, l, r, start=True, stop=True),
                             waits=w, inc=self.s_pe if sp == 1 else None)
            ps.wrote(gpt)
            K.read(gpt)
            Q.read(gpt)
            mmpts.append(gpt)
        if half == 0:
            nm = self.att_nm[par]
            pmx = E.emit("dv", lambda e, o=nm_ap, i=self.arena[:, 0:1024:8]:
                         e.reduce_max(o, i, axis=AX.X, negate=True),
                         waits=[mmpts[0]] + nm.for_write(), inc=self.s_dv)
            nm.wrote(pmx)
            stt["pmx"] = pmx
        sums = self.att_sums[par]
        pex = E.emit("ac", lambda e, o=probsap[:, half * 2048:half * 2048 + 2048],
                     i=self.arena[:, 0:2048], b=nm_ap, a=sums_ap[:, half:half + 1]:
                     e.activation(o, i, AF.Exp, bias=b, scale=1.0, accum_out=a),
                     waits=[stt["pmx"], mmpts[1]] +
                           (probs.for_write() if half == 0 else []) +
                           (sums.for_write() if half == 0 else []),
                     inc=self.s_ac)
        self.psA.read(pex)
        self.psB.read(pex)
        probs.wrote(pex)
        sums.wrote(pex)
        if half == 1:
            stt["exp_last"] = pex
            self.att_nm[par].read(pex)

    def unit_combine_T(self, gu, unit_state):
        E = self.E
        g = self.g_f32
        stt = unit_state[gu]
        i = gu % 8
        par = stt["par"]
        probs, probsap = stt["probs"], stt["probsap"]
        ss = self.att_ss[par]
        pss = E.emit("dv", lambda e, o=g(H_ASS + 2 * par, 1), i=stt["sums_ap"]:
                     e.reduce_sum(o, i, axis=AX.X),
                     waits=[stt["exp_last"]] + ss.for_write(), inc=self.s_dv)
        prc = E.emit("dv", lambda e, o=g(H_ARC + 2 * par, 1), i=g(H_ASS + 2 * par, 1):
                     e.reciprocal(o, i), waits=[pss], inc=self.s_dv)
        ss.wrote(prc)
        self.att_sums[par].read(pss)
        pr = E.emit("dv", lambda e, o=probsap, s=g(H_ARC + 2 * par, 1):
                    e.tensor_scalar_mul(o, o, s),
                    waits=[prc, probs.w], inc=self.s_dv)
        probs.wrote(pr)
        ss.read(pr)
        for bi in range(4):
            pc = self.psC[bi % 2]
            pcap = self.arena_bf[:, 4096 + (bi % 2) * 1024:
                                 4096 + (bi % 2) * 1024 + 1024]
            gpt = None
            for jj in range(8):
                cc = bi * 8 + jj
                w = ([pr, self.identity.w] + pc.for_write()) if jj == 0 else []
                gpt = E.emit("pe", lambda e, o=pcap[:, jj * 128:jj * 128 + 128],
                             i=probsap[:, cc * 128:cc * 128 + 128],
                             dk=self.g_bf(H_ID, 128):
                             e.transpose(o, i, dk), waits=w,
                             inc=self.s_pe if jj == 7 else None)
            pc.wrote(gpt)
            probs.read(gpt)
            base = GA0 + (bi * 8) * 1024 + i * 128
            dst = self.grand[:, base: base + 8 * 1024].rearrange(
                "p (c t) -> p c t", c=8)[:, :, 0:128]
            eng = "ac" if bi == 3 else "dv"
            sem = self.s_ac if bi == 3 else self.s_dv
            if eng == "ac":
                fn = (lambda e, o=dst, i=pcap.rearrange("p (c t) -> p c t", c=8):
                      e.copy(o, i))
            else:
                fn = (lambda e, o=dst, i=pcap.rearrange("p (c t) -> p c t", c=8):
                      e.tensor_copy(o, i))
            pev = E.emit(eng, fn,
                         waits=[gpt] + (self.rA0.for_write() if (i == 0 and bi == 0)
                                        else []),
                         inc=sem)
            pc.read(pev)
            if i == 0 and bi == 0:
                self.rA0.wrote(pev)
            else:
                self.rA0.add_write(pev)

    def p4_outproj(self, wdram, bias, xsrc, dst):
        E = self.E
        pl = E.emit("sy", lambda e,
                    o=self.grand[:, GA0:GA0 + 32768].rearrange("p (c t) -> p c t", c=32),
                    i=self.ctxd.rearrange("(c p) t -> p c t", p=128):
                    e.dma_start(out=o, in_=i),
                    waits=self.rA0.for_write() + [Pt(s, s.n) for s in self.s_stc],
                    inc=self.s_cxl, inck=16)
        self.rA0.wrote(pl)
        bias_fn = (lambda oc: self.g_f32(H_OBT, 32)[:, oc:oc + 1]) if bias else None
        guards = self.head_guards() + self.rA1.for_write()

        def osink(oc, stap, st, pev):
            p = E.emit("dv", lambda e, o=self.g_bf(GA1 + oc * 1024, 1024), i=stap:
                       e.tensor_copy(o, i),
                       waits=[pev] + (guards if oc == 0 else []),
                       inc=self.s_dv)
            st.read(p)
            self.rA1.wrote(p)

        self.proj_fm(wdram, 0, 32, bias_fn, None, GA0, self.rA0, osink)
        self.add_pass(xsrc, dst)

    def add_pass(self, xsrc, dst):
        E = self.E
        g = self.g_f32
        for j in range(8):
            par = j % 2
            xt = self.xt[par]
            xtap = g(GC + par * 8192, 4096)
            pl = E.emit("sy", lambda e, o=xtap, i=xsrc[j * 128:(j + 1) * 128, :]:
                        e.dma_start(out=o, in_=i),
                        waits=xt.for_write() + (self.last_store or []),
                        inc=self.s_xt[par], inck=16)
            xt.wrote(pl)
            addpts = []
            for half in range(2):
                if half == 0:
                    slots = [self.psC[0], self.psC[1]]
                else:
                    slots = [self.psD]
                pap = self.arena_bf[:, (4 + half * 2) * 1024:
                                    (4 + half * 2) * 1024 + 2048]
                gpt = None
                for ocl in range(16):
                    oc = half * 16 + ocl
                    if ocl == 0:
                        w = [self.rA1.w, self.identity.w]
                        for s in slots:
                            w += s.for_write()
                    else:
                        w = []
                    gpt = E.emit("pe", lambda e, o=pap[:, ocl * 128:ocl * 128 + 128],
                                 i=self.g_bf(GA1 + oc * 1024 + j * 128, 128),
                                 idn=self.g_bf(H_ID, 128):
                                 e.transpose(o, i, idn), waits=w,
                                 inc=self.s_pe if ocl == 15 else None)
                for s in slots:
                    s.wrote(gpt)
                self.rA1.read(gpt)
                padd = E.emit("dv", lambda e, o=xtap[:, half * 2048:half * 2048 + 2048],
                              i=pap: e.tensor_tensor(o, o, i, ALU.add),
                              waits=[gpt, xt.w], inc=self.s_dv)
                for s in slots:
                    s.read(padd)
                addpts.append(padd)
            pst = E.emit("ac", lambda e, o=dst[j * 128:(j + 1) * 128, :], i=xtap:
                         e.dma_start(out=o, in_=i), waits=addpts,
                         inc=self.s_stx[par], inck=16)
            xt.read(pst)
        self.last_store = [Pt(s, s.n) for s in self.s_stx]

    def p5_ffn(self, dst):
        E = self.E
        g = self.g_f32
        self.ln_pass(self.xd, 0, 8, GA0, self.rA0, gate=self.last_store)
        hw = self.lora["hsel"].w
        for a in range(3):
            E.emit("gp", lambda e, o=self.g_bf(GA1 + a * 4096, 4096, parts=16),
                   i=self.loraA[a]:
                   e.dma_start(out=o, in_=i),
                   waits=(self.rA1.for_write() + self.head_guards())
                   if a == 0 else [],
                   inc=self.s_lor, inck=16)
        for a in range(3):
            E.emit("gp", lambda e,
                   o=self.g_bf(GA1 + 16384 + a * 512, 512).rearrange("p (c r) -> p c r", c=32),
                   i=self.loraB[a].rearrange("(c p) r -> p c r", p=128):
                   e.dma_start(out=o, in_=i), waits=[],
                   inc=self.s_lor, inck=16)
        lor_pt = Pt(self.s_lor, self.s_lor.n)
        apts = [lor_pt, lor_pt, lor_pt]
        ah = self.g_bf(GA1 + 3 * 4096, 4096, parts=16)
        p = E.emit("dv", lambda e, o=ah, i=self.g_bf(GA1 + 0, 4096, parts=16),
                   s=g(H_HSEL, 3, parts=16)[:, 0:1]:
                   e.tensor_scalar_mul(o, i, s), waits=[apts[0], hw], inc=self.s_dv)
        for a in (1, 2):
            p = E.emit("dv", lambda e, o=ah, i=self.g_bf(GA1 + a * 4096, 4096, parts=16),
                       s=g(H_HSEL, 3, parts=16)[:, a:a + 1]:
                       e.scalar_tensor_tensor(o, i, s, o, ALU.mult, ALU.add),
                       waits=[apts[a], p], inc=self.s_dv)
        pah = p
        self.rA1.wrote(pah)
        for bi in range(4):
            pc = self.psC[bi % 2]
            pcap = self.arena_bf[:, 4096 + (bi % 2) * 1024:
                                 4096 + (bi % 2) * 1024 + 1024]
            gpt = None
            for jj in range(8):
                cc = bi * 8 + jj
                w = ([pah, self.identity.w] + pc.for_write()) if jj == 0 else []
                gpt = E.emit("pe", lambda e, o=pcap[:, jj * 16:jj * 16 + 16],
                             i=ah[:, cc * 128:cc * 128 + 128],
                             idn=self.g_bf(H_ID, 128, parts=16)[:, 0:16]:
                             e.transpose(o, i, idn), waits=w,
                             inc=self.s_pe if jj == 7 else None)
            pc.wrote(gpt)
            self.rA1.read(gpt)
            pev = E.emit("dv", lambda e, o=self.g_bf(D_AHT + bi * 128, 128),
                         i=pcap[:, 0:128]: e.tensor_copy(o, i),
                         waits=[gpt] + (self.lora["AhT"].for_write() +
                                        self.lnt[0].for_write() if bi == 0 else []),
                         inc=self.s_dv)
            pc.read(pev)
            self.lora["AhT"].wrote(pev)
        bpts = [lor_pt, lor_pt, lor_pt]
        bh = self.g_bf(GA1 + 16384 + 3 * 512, 512)
        p = E.emit("dv", lambda e, o=bh, i=self.g_bf(GA1 + 16384, 512),
                   s=g(H_HSEL, 3)[:, 0:1]:
                   e.tensor_scalar_mul(o, i, s), waits=[bpts[0], hw], inc=self.s_dv)
        for a in (1, 2):
            p = E.emit("dv", lambda e, o=bh, i=self.g_bf(GA1 + 16384 + a * 512, 512),
                       s=g(H_HSEL, 3)[:, a:a + 1]:
                       e.scalar_tensor_tensor(o, i, s, o, ALU.mult, ALU.add),
                       waits=[bpts[a], p], inc=self.s_dv)
        pbh = p
        self.rA1.wrote(pbh)
        for bi in range(4):
            pc = self.psC[bi % 2]
            pcap = self.arena_bf[0:16, 4096 + (bi % 2) * 1024:
                                 4096 + (bi % 2) * 1024 + 1024]
            gpt = None
            for jj in range(8):
                cc = bi * 8 + jj
                w = ([pbh, self.identity.w] + pc.for_write()) if jj == 0 else []
                gpt = E.emit("pe", lambda e, o=pcap[:, jj * 128:jj * 128 + 128],
                             i=bh[:, cc * 16:cc * 16 + 16],
                             idn=self.g_bf(H_ID, 128):
                             e.transpose(o, i, idn), waits=w,
                             inc=self.s_pe if jj == 7 else None)
            pc.wrote(gpt)
            self.rA1.read(gpt)
            pev = E.emit("dv", lambda e, o=self.g_bf(D_BHT + bi * 1024, 1024, parts=16),
                         i=pcap: e.tensor_copy(o, i),
                         waits=[gpt] + (self.lora["BhT"].for_write() +
                                        self.lnt[1].for_write() if bi == 0 else []),
                         inc=self.s_dv)
            pc.read(pev)
            self.lora["BhT"].wrote(pev)
        zap = self.arena[0:16, 3072:4096]
        gpt = None
        for dc in range(32):
            for sp in range(2):
                w = ([self.lora["AhT"].w, self.rA0.w] + self.psD.for_write()) \
                    if (dc == 0 and sp == 0) else []
                is_last = dc == 31 and sp == 1
                gpt = E.emit("pe", lambda e, o=zap[:, sp * 512:sp * 512 + 512],
                             l=self.g_bf(D_AHT + dc * 16, 16),
                             r=self.grand[:, GA0 + dc * 1024 + sp * 512:
                                          GA0 + dc * 1024 + sp * 512 + 512],
                             st=(dc == 0), spp=is_last:
                             e.matmul(o, l, r, start=st, stop=spp),
                             waits=w, inc=self.s_pe if is_last else None)
        self.psD.wrote(gpt)
        self.lora["AhT"].read(gpt)
        self.rA0.read(gpt)
        pz = E.emit("dv", lambda e, o=self.g_bf(D_ZSB, 1024, parts=16), i=zap:
                    e.tensor_copy(o, i),
                    waits=[gpt] + self.lora["zsb"].for_write(), inc=self.s_dv)
        self.psD.read(pz)
        self.lora["zsb"].wrote(pz)

        def fsink(oc, stap, st, pev):
            p = E.emit("dv", lambda e, o=self.g_bf(GA1 + oc * 1024, 1024), i=stap:
                       e.tensor_copy(o, i),
                       waits=[pev] + (self.rA1.for_write() if oc == 0 else []),
                       inc=self.s_dv)
            st.read(p)
            self.rA1.wrote(p)

        self.proj_fm(self.wft, 0, 32,
                     lambda oc: self.g_f32(H_FBT, 32)[:, oc:oc + 1], None,
                     GA0, self.rA0, fsink, lora=True)
        self.add_pass(self.xd, dst)

    def p6_attn_b(self):
        E = self.E
        g = self.g_f32
        self.ln_pass(self.xd, 0, 8, GA0, self.rA0, gate=self.last_store)
        E.emit("gp", lambda e, o=self.g_bf(A1_COS, 1024), i=self.cosf:
               e.dma_start(out=o, in_=i),
               waits=self.rA1.for_write() + self.gstB.for_write() +
               [Pt(self.s_gd, self.s_gd.n), Pt(self.s_gd2, self.s_gd2.n)],
               inc=self.s_misc, inck=16)
        E.emit("gp", lambda e, o=self.g_bf(A1_SIN, 1024), i=self.sinf:
               e.dma_start(out=o, in_=i), inc=self.s_misc, inck=16)
        psin = Pt(self.s_misc, self.s_misc.n)
        self.rA1.wrote(psin)
        rope_guard = [psin]

        def qsink(oc, stap, st, pev):
            rot = self.g_bf(A1_ROT, 1024)
            t1 = self.g_bf(A1_RT1, 1024)
            p1 = E.emit("dv", lambda e, o=rot[0:64, :], i=stap[64:128, :]:
                        e.tensor_scalar_mul(o, i, -1.0),
                        waits=[pev] + rope_guard, inc=self.s_dv)
            p2 = E.emit("dv", lambda e, o=rot[64:128, :], i=stap[0:64, :]:
                        e.tensor_copy(o, i), waits=[pev], inc=self.s_dv)
            p3 = E.emit("dv", lambda e, o=t1, a=stap, b=self.g_bf(A1_COS, 1024):
                        e.tensor_tensor(o, a, b, ALU.mult), waits=[pev], inc=self.s_dv)
            p4 = E.emit("dv", lambda e, o=rot, a=rot, b=self.g_bf(A1_SIN, 1024):
                        e.tensor_tensor(o, a, b, ALU.mult), waits=[p1, p2],
                        inc=self.s_dv)
            p5 = E.emit("dv", lambda e, o=stap, a=t1, b=rot:
                        e.tensor_tensor(o, a, b, ALU.add), waits=[p3, p4],
                        inc=self.s_dv)
            st.wrote(p5)
            self.rA1.read(p5)
            p = E.emit("ac", lambda e, o=self.qfmd[oc * 128:(oc + 1) * 128, :], i=stap:
                       e.dma_start(out=o, in_=i), waits=[p5],
                       inc=self.s_st[oc % 2], inck=16)
            st.read(p)
            self.qst_pts[oc] = [p]

        self.proj_fm(self.wqt, 0, 32, None, SCALE, GA0, self.rA0, qsink)
        if "qfmd" in self.tap_aps:
            self.tap_dram(self.tap_aps["qfmd"], self.qfmd, list(self.s_st))
        for s in (self.kh + self.vh + self.qh + self.ctxs):
            s.seed(self.rA1.for_write())
        self.p3_attention("b")
        if "ctxd" in self.tap_aps:
            self.tap_dram(self.tap_aps["ctxd"], self.ctxd, list(self.s_stc))
        self.p4_outproj(self.wot, bias=False, xsrc=self.xd, dst=self.xd)

    def finalize(self):
        E = self.E
        allsems = ([self.s_gd, self.s_sy, self.s_setup, self.s_cxl, self.s_misc,
                    self.s_lor, self.s_ag] + self.s_xt + self.s_w + self.s_k +
                   self.s_v + self.s_q + self.s_st + self.s_stc + self.s_stx)
        last = []
        if self.upto < 4:
            last = [E.emit("sy", lambda e: e.dma_start(out=self.y, in_=self.xq),
                           inc=self.s_sy, inck=16)]
        elif self.upto < 7:
            last = [E.emit("sy", lambda e: e.dma_start(out=self.y, in_=self.xd),
                           waits=list(self.last_store), inc=self.s_sy, inck=16)]
        else:
            last = list(self.last_store)
        if "xd" in self.tap_aps and self.last_store is not None:
            last = [E.emit("sy", lambda e:
                           e.dma_start(out=self.tap_aps["xd"], in_=self.xd),
                           waits=list(self.last_store), inc=self.s_sy, inck=16)]
        waits = last + [Pt(s, s.n) for s in allsems]
        E.emit("sy", lambda e: e.nop(), waits=waits)


def build_program(upto=7, taps=()):
    return Builder(upto=upto, taps=taps).build()



def _tile_w(W):
    dc = W.reshape(32, 128, -1, 128)
    return np.ascontiguousarray(dc.transpose(2, 1, 0, 3).reshape(-1, 128, 4096))


def _rope_tables():
    dim = 128
    inv_freq = (1.0 / (10000.0 ** (np.arange(0, dim, 2, dtype=np.float32)
                                   / np.float32(dim)))).astype(np.float32)
    t = np.arange(BT, dtype=np.float32)
    freqs = np.outer(t, inv_freq).astype(np.float32)
    emb = np.concatenate([freqs, freqs], axis=-1)
    return np.cos(emb).astype(np.float32), np.sin(emb).astype(np.float32)


_UPTO = 7
_TAPS = ()


def kernel(**inputs):
    x = np.ascontiguousarray(inputs["L0_attn_a_x"])
    qkvw = np.ascontiguousarray(inputs["L0_attn_a_attn_qkvw"])
    qkvb = np.asarray(inputs["L0_attn_a_attn_qkvb"])
    ow = np.ascontiguousarray(inputs["L0_attn_a_attn_ow"])
    ob = np.asarray(inputs["L0_attn_a_attn_ob"])
    kc = np.ascontiguousarray(np.asarray(inputs["L0_attn_b_k_cache"]).reshape(512, 16384))
    vc = np.ascontiguousarray(np.asarray(inputs["L0_attn_b_v_cache"]).reshape(512, 16384))
    wq = np.asarray(inputs["L0_attn_b_Wq"])
    wo = np.asarray(inputs["L0_attn_b_Wo"])
    bt = np.ascontiguousarray(np.asarray(inputs["L0_attn_b_block_table"]).astype(np.int32))
    fw = np.asarray(inputs["L0_ffn_weight"])
    fb = np.asarray(inputs["L0_ffn_bias"])
    lA = np.ascontiguousarray(inputs["L0_ffn_lora_A"])
    lB = np.ascontiguousarray(inputs["L0_ffn_lora_B"])
    lsi = np.ascontiguousarray(np.asarray(inputs["L0_ffn_lora_start_indices"]).astype(np.int32))

    qkvw = _tile_w(qkvw)
    ow = _tile_w(ow)
    wqt = _tile_w(np.ascontiguousarray(wq.T))
    wot = _tile_w(np.ascontiguousarray(wo.T))
    wft = _tile_w(np.ascontiguousarray(fw.T))
    qkvbT = np.ascontiguousarray(qkvb.reshape(96, 128).T)
    obT = np.ascontiguousarray(ob.reshape(32, 128).T)
    fbT = np.ascontiguousarray(fb.reshape(32, 128).T)
    cos_t, sin_t = _rope_tables()

    nc = build_program(upto=_UPTO, taps=_TAPS)
    in_maps = []
    for c in range(NCORES):
        b, qi = c // 4, c % 4
        in_maps.append({
            "xq": np.ascontiguousarray(x[b, qi * NT:(qi + 1) * NT]),
            "qkvw": qkvw, "qkvbT": qkvbT, "ow": ow, "obT": obT,
            "wqt": wqt, "wot": wot, "wft": wft, "fbT": fbT,
            "kcache": kc, "vcache": vc, "btab": bt, "lsi": lsi,
            "bval": np.array([[float(b)]], dtype=np.float32),
            "loraA": lA, "loraB": lB,
            "cosf": np.ascontiguousarray(cos_t[qi * NT:(qi + 1) * NT].T),
            "sinf": np.ascontiguousarray(sin_t[qi * NT:(qi + 1) * NT].T),
        })
    res = run_bass_kernel_spmd(nc, in_maps, core_ids=list(range(NCORES)))
    out = np.empty((2, BT, D), dtype=np.float32)
    for c in range(NCORES):
        b, qi = c // 4, c % 4
        out[b, qi * NT:(qi + 1) * NT] = res.results[c]["y"]
    if _TAPS:
        kernel.tap_results = res.results
    return out


# version 05
# speedup vs baseline: 1.3521x; 1.3013x over previous
import numpy as np

import concourse.bass as bass
import concourse.mybir as mybir
from concourse.bass_utils import run_bass_kernel_spmd
from concourse.bass_core_rust import IndirectOffsetOnAxis

F32 = mybir.dt.float32
BF16 = mybir.dt.bfloat16
I32 = mybir.dt.int32
AF = mybir.ActivationFunctionType
ALU = mybir.AluOpType
AX = mybir.AxisListType

D = 4096
NT = 1024
BT = 4096
NH = 32
HD = 128
SCALE = float(1.0 / np.sqrt(128.0))
NCORES = 8



class Sem:
    __slots__ = ("name", "n", "h")

    def __init__(self, name):
        self.name = name
        self.n = 0
        self.h = None


class Pt:
    __slots__ = ("sem", "val")

    def __init__(self, sem, val):
        self.sem = sem
        self.val = val


class Slot:
    __slots__ = ("ws", "r")

    def __init__(self):
        self.ws = []
        self.r = []

    @property
    def w(self):
        return self.ws[-1] if self.ws else None

    def rw(self):
        return list(self.ws)

    def for_write(self):
        return list(self.r) + list(self.ws)

    def wrote(self, pt):
        assert pt is not None
        self.ws = [pt]
        self.r = []

    def add_write(self, pt):
        assert pt is not None
        self.ws.append(pt)

    def read(self, pt):
        if pt is not None:
            self.r.append(pt)

    def seed(self, pts):
        self.r.extend(p for p in pts if p is not None)


class Emitter:
    ENGINES = ("pe", "dv", "ac", "gp", "sy")

    def __init__(self):
        self.q = {e: [] for e in self.ENGINES}
        self.sems = {}
        self.waited = {e: {} for e in self.ENGINES}

    def S(self, name):
        if name not in self.sems:
            self.sems[name] = Sem(name)
        return self.sems[name]

    def emit(self, eng, fn, waits=(), inc=None, inck=1):
        seen = {}
        for pt in waits:
            if pt is None:
                continue
            key = pt.sem.name
            cur = seen.get(key, self.waited[eng].get(key, -1))
            if pt.val > cur:
                seen[key] = pt.val
        ws = []
        for key, val in seen.items():
            self.waited[eng][key] = max(self.waited[eng].get(key, -1), val)
            ws.append((self.sems[key], val))
        pt_out = None
        if inc is not None:
            inc.n += inck
            pt_out = Pt(inc, inc.n)
        self.q[eng].append((ws, fn, inc, inck))
        return pt_out

    def replay(self, block):
        def run(stream):
            def body(eng):
                for ws, fn, inc, inck in stream:
                    for sem, val in ws:
                        eng.wait_ge(sem.h, val)
                    ins = fn(eng)
                    if inc is not None:
                        ins.then_inc(inc.h, inck)
            return body

        block.tensor(run(self.q["pe"]))
        block.vector(run(self.q["dv"]))
        block.scalar(run(self.q["ac"]))
        block.gpsimd(run(self.q["gp"]))
        block.sync(run(self.q["sy"]))



GCOLS = 105536
GA0 = 0
GA1 = 32768
GC = 65536
GD = 81920
GE = 90112
GH = 98304
ST0 = 99392
ST1 = 100416
GD2 = 101440
ST2 = GD2

A1_K0 = GA1 + 0
A1_K1 = GA1 + 4096
A1_V0 = GA1 + 8192
A1_V1 = GA1 + 12288
A1_Q0 = GA1 + 16384
A1_Q1 = GA1 + 17408
A1_CTXS0 = GA1 + 18432
A1_CTXS1 = GA1 + 19456
A1_GST = GA1 + 20480
A1_GSTB = GA1 + 24576
A1_COS = GA1 + 28672
A1_SIN = GA1 + 29696
A1_ROT = GA1 + 30720
A1_RT1 = GA1 + 31744

D_AHT = GD + 0
D_BHT = GD + 512
D_ZSB = GD + 4608

H_ID = GH + 0
H_D0 = GH + 128
H_QKVB = GH + 640
H_OBT = GH + 832
H_FBT = GH + 896
H_ANM = GH + 960
H_ASUM = GH + 966
H_ASS = GH + 990
H_ARC = GH + 996
H_LN = GH + 1008
H_LSI_I = GH + 1040
H_LSI_F = GH + 1048
H_BVAL = GH + 1056
H_TMP = GH + 1058
H_HSEL = GH + 1074
H_BTAB = GH + 1080

LN_S1, LN_NEGM, LN_M2, LN_VAR, LN_LNV, LN_RSTD, LN_SUMSQ = range(7)


def lncol(stat, par):
    return H_LN + stat * 4 + par * 2


class Builder:
    def __init__(self, upto=7, taps=()):
        self.upto = upto
        self.taps = taps
        self.nc = bass.Bass("TRN2", target_bir_lowering=False)
        nc = self.nc
        self.xq = nc.dram_tensor("xq", [NT, D], F32, kind="ExternalInput").ap()
        self.qkvw = nc.dram_tensor("qkvw", [96, 128, 4096], F32, kind="ExternalInput").ap()
        self.qkvbT = nc.dram_tensor("qkvbT", [128, 96], F32, kind="ExternalInput").ap()
        self.ow = nc.dram_tensor("ow", [32, 128, 4096], F32, kind="ExternalInput").ap()
        self.obT = nc.dram_tensor("obT", [128, 32], F32, kind="ExternalInput").ap()
        self.wqt = nc.dram_tensor("wqt", [32, 128, 4096], F32, kind="ExternalInput").ap()
        self.wot = nc.dram_tensor("wot", [32, 128, 4096], F32, kind="ExternalInput").ap()
        self.wft = nc.dram_tensor("wft", [32, 128, 4096], F32, kind="ExternalInput").ap()
        self.fbT = nc.dram_tensor("fbT", [128, 32], F32, kind="ExternalInput").ap()
        self.kcache = nc.dram_tensor("kcache", [512, 16384], F32, kind="ExternalInput").ap()
        self.vcache = nc.dram_tensor("vcache", [512, 16384], F32, kind="ExternalInput").ap()
        self.btab = nc.dram_tensor("btab", [256], I32, kind="ExternalInput").ap()
        self.lsi = nc.dram_tensor("lsi", [4], I32, kind="ExternalInput").ap()
        self.bval = nc.dram_tensor("bval", [1, 1], F32, kind="ExternalInput").ap()
        self.loraA = nc.dram_tensor("loraA", [3, 16, D], F32, kind="ExternalInput").ap()
        self.loraB = nc.dram_tensor("loraB", [3, D, 16], F32, kind="ExternalInput").ap()
        self.cosf = nc.dram_tensor("cosf", [128, NT], F32, kind="ExternalInput").ap()
        self.sinf = nc.dram_tensor("sinf", [128, NT], F32, kind="ExternalInput").ap()
        self.y = nc.dram_tensor("y", [NT, D], F32, kind="ExternalOutput").ap()
        self.xd = nc.dram_tensor("xd", [NT, D], F32, kind="Internal").ap()
        self.kvin = {}
        self.kvag = {}
        for kv in "KV":
            for hf in range(2):
                self.kvin[kv, hf] = nc.dram_tensor(
                    "kvin%s%d" % (kv, hf), [D // 2, NT], BF16, kind="Internal").ap()
                self.kvag[kv, hf] = nc.dram_tensor(
                    "kvag%s%d" % (kv, hf), [2 * D, NT], BF16, kind="Internal").ap()
        self.ag_pts = {}
        self.qfmd = nc.dram_tensor("qfmd", [D, NT], BF16, kind="Internal").ap()
        self.ctxd = nc.dram_tensor("ctxd", [D, NT], BF16, kind="Internal").ap()
        self.kgd = nc.dram_tensor("kgd", [8, BT, HD], BF16, kind="Internal").ap()
        self.vgd = nc.dram_tensor("vgd", [8, BT, HD], BF16, kind="Internal").ap()

        self.tap_aps = {}
        shapes = {"qfmd": ([D, NT], BF16), "ctxd": ([D, NT], BF16),
                  "kgd": ([8, BT, HD], BF16), "vgd": ([8, BT, HD], BF16),
                  "xd": ([NT, D], F32)}
        for t in taps:
            if t == "adbg":
                self.dbg_pbT = nc.dram_tensor("tap_pbT", [128, 32768], BF16,
                                              kind="ExternalOutput").ap()
                self.dbg_probs = nc.dram_tensor("tap_probs", [128, 8192], BF16,
                                                kind="ExternalOutput").ap()
                self.dbg_small = nc.dram_tensor("tap_small", [128, 1024], F32,
                                                kind="ExternalOutput").ap()
                continue
            sh, dt = shapes[t]
            self.tap_aps[t] = nc.dram_tensor("tap_" + t, sh, dt,
                                             kind="ExternalOutput").ap()

        E = Emitter()
        self.E = E
        self.s_pe = E.S("s_pe")
        self.s_dv = E.S("s_dv")
        self.s_ac = E.S("s_ac")
        self.s_gp = E.S("s_gp")
        self.s_sy = E.S("s_sy")
        self.s_sc = E.S("s_sc")
        self.s_gd = E.S("s_gd")
        self.s_setup = E.S("s_setup")
        self.s_xt = [E.S("s_xt0"), E.S("s_xt1")]
        self.s_w = [E.S("s_w0"), E.S("s_w1")]
        self.s_k = [E.S("s_k0"), E.S("s_k1")]
        self.s_v = [E.S("s_v0"), E.S("s_v1")]
        self.s_q = [E.S("s_q0"), E.S("s_q1")]
        self.s_st = [E.S("s_st0"), E.S("s_st1"), E.S("s_st2")]
        self.s_stc = [E.S("s_stc0"), E.S("s_stc1")]
        self.s_stx = [E.S("s_stx0"), E.S("s_stx1")]
        self.s_cxl = E.S("s_cxl")
        self.s_misc = E.S("s_misc")
        self.s_lor = E.S("s_lor")
        self.s_gd2 = E.S("s_gd2")
        self.s_ag = E.S("s_ag")

        self.xt = [Slot(), Slot()]
        self.lnt = [Slot(), Slot()]
        self.lns = [Slot(), Slot()]
        self.wst = [Slot(), Slot()]
        self.xfm = [Slot(), Slot()]
        self.stg = [Slot(), Slot(), Slot()]
        self.psA = Slot()
        self.psB = Slot()
        self.psC = [Slot(), Slot()]
        self.psD = Slot()
        self.kh = [Slot(), Slot()]
        self.vh = [Slot(), Slot()]
        self.qh = [Slot(), Slot()]
        self.ctxs = [Slot(), Slot()]
        self.gst = Slot()
        self.rA0 = Slot()
        self.rA1 = Slot()
        self.identity = Slot()
        self.biases = Slot()
        self.att_nm = [Slot(), Slot(), Slot()]
        self.att_sums = [Slot(), Slot(), Slot()]
        self.att_ss = [Slot(), Slot(), Slot()]
        self.gstB = Slot()
        self.probsx = Slot()
        self._gather_steps = []
        self.kst_pts = [[] for _ in range(32)]
        self.vst_pts = [[] for _ in range(32)]
        self.qst_pts = [[] for _ in range(32)]
        self.lora = {k: Slot() for k in ("AhT", "BhT", "zsb", "hsel")}
        self.btsb = Slot()
        self.last_store = None

    def g_bf(self, col, n, parts=128, p0=0):
        return self.grand[p0:p0 + parts, col:col + n]

    def g_f32(self, col, n, parts=128, p0=0):
        return self.grand_f32[p0:p0 + parts, col // 2:col // 2 + n]

    def g_i32(self, col, n, parts=128, p0=0):
        return self.grand_i32[p0:p0 + parts, col // 2:col // 2 + n]

    def head_guards(self):
        pts = []
        for s in (self.kh + self.vh + self.qh + self.ctxs + [self.gst, self.gstB]):
            pts.extend(s.for_write())
        return pts

    def build(self):
        from contextlib import ExitStack
        nc, E = self.nc, self.E
        with ExitStack() as stack:
            grand_h = stack.enter_context(nc.sbuf_tensor("grand", [128, GCOLS], BF16))
            arena_h = stack.enter_context(nc.psum_tensor("arena", [128, 4096], F32))
            for name in list(E.sems):
                E.sems[name].h = stack.enter_context(nc.semaphore(name))
            self.grand = grand_h[:]
            self.grand_f32 = grand_h[:].bitcast(F32)
            self.grand_i32 = grand_h[:].bitcast(I32)
            self.arena = arena_h[:]
            self.arena_bf = arena_h[:].bitcast(BF16)

            self.p0_setup()
            if self.upto >= 1:
                self.p1_attn_a_proj()
            if self.upto >= 2:
                self.p2_gather()
                if self.upto == 2:
                    self.gather_flush()
            if self.upto >= 3:
                self.rA0.seed(self.xfm[0].for_write())
                for s in (self.kh + self.vh + self.qh + self.ctxs +
                          [self.gst, self.gstB]):
                    s.seed(self.xfm[1].for_write())
                self.probsx.seed(self.stg[2].for_write())
                self.p3_attention("a")
                if self.upto >= 2:
                    self.gather_flush()
                if "ctxd" in self.tap_aps:
                    self.tap_dram(self.tap_aps["ctxd"], self.ctxd, list(self.s_stc))
                if "adbg" in self.taps:
                    E = self.E
                    gates = [Pt(s, s.n) for s in (self.s_dv, self.s_ac, self.s_pe,
                                                  self.s_gp, self.s_sc)]
                    E.emit("sy", lambda e, o=self.dbg_pbT, i=self.g_bf(GA0, 32768):
                           e.dma_start(out=o, in_=i), waits=gates,
                           inc=self.s_sy, inck=16)
                    E.emit("sy", lambda e, o=self.dbg_probs, i=self.g_bf(GD, 8192):
                           e.dma_start(out=o, in_=i), waits=gates,
                           inc=self.s_sy, inck=16)
                    E.emit("sy", lambda e, o=self.dbg_small, i=self.g_f32(GH, 1024):
                           e.dma_start(out=o, in_=i), waits=gates,
                           inc=self.s_sy, inck=16)
            if self.upto >= 4:
                self.p4_outproj(self.ow, bias=True, xsrc=self.xq, dst=self.xd)
            if self.upto >= 5:
                self.p5_ffn(dst=self.xd)
            if self.upto >= 6:
                self.p6_attn_b()
            if self.upto >= 7:
                self.p5_ffn(dst=self.y)
            self.finalize()

            blk = stack.enter_context(nc.Block())
            E.replay(blk)
        return nc


    def p0_setup(self):
        E = self.E
        g = self.g_f32
        for dst_col, n, src in ((H_QKVB, 96, self.qkvbT), (H_OBT, 32, self.obT),
                                (H_FBT, 32, self.fbT)):
            pt = E.emit("sy", lambda e, o=g(dst_col, n), i=src:
                        e.dma_start(out=o, in_=i), inc=self.s_setup, inck=16)
        idap = self.g_bf(H_ID, 128)
        p1 = E.emit("gp", lambda e, o=idap: e.iota(
            o, pattern=[[1, 128]], base=0, channel_multiplier=-1,
            allow_small_or_imprecise_dtypes=True), inc=self.s_gp)
        p2 = E.emit("gp", lambda e, o=idap: e.tensor_single_scalar(
            o, o, 0.0, ALU.is_equal), waits=[p1], inc=self.s_gp)
        self.identity.wrote(p2)
        p3 = E.emit("sy", lambda e, o=self.g_i32(H_LSI_I, 4),
                    i=self.lsi.unsqueeze(0).broadcast_to([128, 4]):
                    e.dma_start(out=o, in_=i), inc=self.s_setup, inck=16)
        p4 = E.emit("sy", lambda e, o=g(H_BVAL, 1),
                    i=self.bval.broadcast_to([128, 1]):
                    e.dma_start(out=o, in_=i), inc=self.s_setup, inck=16)
        setup_pt = Pt(self.s_setup, self.s_setup.n)
        self.biases.wrote(setup_pt)
        p4 = setup_pt
        p5 = E.emit("dv", lambda e, o=g(H_LSI_F, 4), i=self.g_i32(H_LSI_I, 4):
                    e.tensor_copy(o, i), waits=[setup_pt], inc=self.s_dv)
        bv = g(H_BVAL, 1)
        lf = g(H_LSI_F, 4)

        def tcol(k):
            return g(H_TMP + 2 * k, 1)

        def tt(o, a, b, op, w):
            return E.emit("dv", lambda e, o=o, a=a, b=b, op=op:
                          e.tensor_tensor(o, a, b, op), waits=w, inc=self.s_dv)

        def ts2(o, a, s1, s2, w):
            return E.emit("dv", lambda e, o=o, a=a, s1=s1, s2=s2:
                          e.tensor_scalar(o, a, s1, s2, ALU.mult, ALU.add),
                          waits=w, inc=self.s_dv)

        c = [tt(tcol(k), bv, lf[:, k:k + 1], ALU.is_lt, [p4, p5]) for k in range(4)]
        pu = ts2(tcol(4), tcol(0), -1.0, 1.0, [c[0]])
        pv1 = tt(tcol(5), tcol(1), tcol(4), ALU.mult, [c[1], pu])
        pw = ts2(tcol(6), tcol(1), -2.0, 2.0, [c[1]])
        pv2 = tt(tcol(7), tcol(5), tcol(6), ALU.add, [pv1, pw])
        pinner = tt(tcol(4), tcol(2), tcol(7), ALU.mult, [c[2], pv2])
        pr2 = ts2(tcol(6), tcol(3), -1.0, 4.0, [c[3]])
        pnc2 = ts2(tcol(5), tcol(2), -1.0, 1.0, [c[2]])
        po2 = tt(tcol(6), tcol(5), tcol(6), ALU.mult, [pr2, pnc2])
        pres = tt(tcol(7), tcol(4), tcol(6), ALU.add, [pinner, po2])
        hs = None
        for a in range(3):
            hs = E.emit("dv", lambda e, o=g(H_HSEL + 2 * a, 1), i=tcol(7), v=float(2 + a):
                        e.tensor_single_scalar(o, i, v, ALU.is_equal),
                        waits=[pres], inc=self.s_dv)
        self.lora["hsel"].wrote(hs)

    def ln_pass(self, src, row0, ntiles, xfm_col, xfm_slot, gate=()):
        E = self.E
        g = self.g_f32
        last_evac = None
        first = True
        for j in range(ntiles):
            par = j % 2
            xt = self.xt[par]
            xtap = g(GC + par * 8192, 4096)
            lns = self.lns[par]
            r0 = row0 + j * 128
            pl = E.emit("sy", lambda e, o=xtap, i=src[r0:r0 + 128, :]:
                        e.dma_start(out=o, in_=i),
                        waits=xt.for_write() + (gate or []),
                        inc=self.s_xt[par], inck=16)
            xt.wrote(pl)
            lnt = self.lnt[par]
            lntap = self.g_bf(GD + par * 4096, 4096)
            c_sq, c_s1, c_nm, c_m2, c_v, c_lnv, c_rstd = (
                g(lncol(LN_SUMSQ, par), 1), g(lncol(LN_S1, par), 1),
                g(lncol(LN_NEGM, par), 1), g(lncol(LN_M2, par), 1),
                g(lncol(LN_VAR, par), 1), g(lncol(LN_LNV, par), 1),
                g(lncol(LN_RSTD, par), 1))
            psq = E.emit("ac", lambda e, o=lntap, i=xtap, a=c_sq:
                         e.activation(o, i, AF.Square, bias=0.0, scale=1.0, accum_out=a),
                         waits=[xt.w] + lnt.for_write() + lns.for_write(), inc=self.s_ac)
            ps1 = E.emit("dv", lambda e, o=c_s1, i=xtap:
                         e.reduce_sum(o, i, axis=AX.X), waits=[xt.w], inc=self.s_dv)
            pnm = E.emit("dv", lambda e, o=c_nm, i=c_s1:
                         e.tensor_scalar_mul(o, i, -1.0 / 4096.0),
                         waits=[ps1], inc=self.s_dv)
            pm2 = E.emit("dv", lambda e, o=c_m2, i=c_nm:
                         e.tensor_scalar(o, i, i, -1e-5, ALU.mult, ALU.add),
                         waits=[pnm], inc=self.s_dv)
            pvar = E.emit("dv", lambda e, o=c_v, a=c_sq, b=c_m2:
                          e.scalar_tensor_tensor(o, a, 1.0 / 4096.0, b,
                                                 ALU.mult, ALU.subtract),
                          waits=[psq, pm2], inc=self.s_dv)
            plnv = E.emit("ac", lambda e, o=c_lnv, i=c_v:
                          e.activation(o, i, AF.Ln, bias=0.0, scale=1.0),
                          waits=[pvar], inc=self.s_ac)
            prstd = E.emit("ac", lambda e, o=c_rstd, i=c_lnv:
                           e.activation(o, i, AF.Exp, bias=0.0, scale=-0.5),
                           waits=[plnv], inc=self.s_ac)
            pnorm = E.emit("dv", lambda e, o=lntap, i=xtap, s1=c_nm, s2=c_rstd:
                           e.tensor_scalar(o, i, s1, s2, ALU.add, ALU.mult),
                           waits=[prstd, pnm, xt.w, psq], inc=self.s_dv)
            lnt.wrote(pnorm)
            lns.read(pnorm)
            xt.read(pnorm)
            for bi in range(4):
                pc = self.psC[bi % 2]
                pcap = self.arena_bf[:, 4096 + (bi % 2) * 1024:
                                     4096 + (bi % 2) * 1024 + 1024]
                gpt = None
                for jj in range(8):
                    dc = bi * 8 + jj
                    w = ([pnorm, self.identity.w] + pc.for_write()) if jj == 0 else []
                    gpt = E.emit("pe", lambda e, o=pcap[:, jj * 128:jj * 128 + 128],
                                 i=lntap[:, dc * 128:dc * 128 + 128],
                                 idn=self.g_bf(H_ID, 128):
                                 e.transpose(o, i, idn),
                                 waits=w, inc=self.s_pe if jj == 7 else None)
                pc.wrote(gpt)
                lnt.read(gpt)
                base = xfm_col + (bi * 8) * 1024 + j * 128
                dst = self.grand[:, base: base + 8 * 1024].rearrange(
                    "p (c t) -> p c t", c=8)[:, :, 0:128]
                pev = E.emit("dv", lambda e, o=dst,
                             i=pcap.rearrange("p (c t) -> p c t", c=8):
                             e.tensor_copy(o, i),
                             waits=[gpt] + (xfm_slot.for_write() if first else []),
                             inc=self.s_dv)
                first = False
                pc.read(pev)
                last_evac = pev
        xfm_slot.wrote(last_evac)
        return last_evac

    def proj_fm(self, wdram, wchunk0, nchunks, bias_fn, scale, xfm_col, xfm_slot, sink,
                lora=False):
        E = self.E
        for oc in range(nchunks):
            ws = self.wst[oc % 2]
            wsap = self.g_bf(GE + (oc % 2) * 4096, 4096)
            pw = E.emit("gp", lambda e, o=wsap, i=wdram[wchunk0 + oc]:
                        e.dma_start(out=o, in_=i),
                        waits=ws.for_write(), inc=self.s_w[oc % 2], inck=16)
            ws.wrote(pw)
            ps = self.psA if (oc % 2 == 0) else self.psB
            psap = self.arena[:, (oc % 2) * 1024:(oc % 2) * 1024 + 1024]
            gpt = None
            for dc in range(32):
                for sp in range(2):
                    w = ([pw, xfm_slot.w] + ps.for_write()) if (dc == 0 and sp == 0) else []
                    is_last = (dc == 31 and sp == 1) and not lora
                    gpt = E.emit("pe", lambda e, o=psap[:, sp * 512:sp * 512 + 512],
                                 l=wsap[:, dc * 128:dc * 128 + 128],
                                 r=self.grand[:, xfm_col + dc * 1024 + sp * 512:
                                              xfm_col + dc * 1024 + sp * 512 + 512],
                                 st=(dc == 0), sp_=is_last:
                                 e.matmul(o, l, r, start=st, stop=sp_),
                                 waits=w, inc=self.s_pe if is_last else None)
            if lora:
                for sp in range(2):
                    is_last = sp == 1
                    gpt = E.emit("pe", lambda e, o=psap[:, sp * 512:sp * 512 + 512],
                                 l=self.g_bf(D_BHT + oc * 128, 128, parts=16),
                                 r=self.g_bf(D_ZSB + sp * 512, 512, parts=16),
                                 sp_=is_last:
                                 e.matmul(o, l, r, start=False, stop=sp_),
                                 waits=([self.lora["BhT"].w, self.lora["zsb"].w]
                                        if sp == 0 else []),
                                 inc=self.s_pe if is_last else None)
            ps.wrote(gpt)
            ws.read(gpt)
            xfm_slot.read(gpt)
            if lora:
                self.lora["BhT"].read(gpt)
                self.lora["zsb"].read(gpt)
            st = self.stg[oc % 2]
            stap = self.g_bf(ST0 + (oc % 2) * 1024, 1024)
            bap = bias_fn(oc) if bias_fn is not None else None
            if bap is not None and scale is not None:
                pev = E.emit("dv", lambda e, o=stap, i=psap, b=bap, s=scale:
                             e.tensor_scalar(o, i, b, s, ALU.add, ALU.mult),
                             waits=[gpt, self.biases.w] + st.for_write(), inc=self.s_dv)
            elif bap is not None:
                pev = E.emit("dv", lambda e, o=stap, i=psap, b=bap:
                             e.tensor_scalar(o, i, b, None, ALU.add),
                             waits=[gpt, self.biases.w] + st.for_write(), inc=self.s_dv)
            elif scale is not None:
                pev = E.emit("dv", lambda e, o=stap, i=psap, s=scale:
                             e.tensor_scalar_mul(o, i, s),
                             waits=[gpt] + st.for_write(), inc=self.s_dv)
            else:
                pev = E.emit("dv", lambda e, o=stap, i=psap: e.tensor_copy(o, i),
                             waits=[gpt] + st.for_write(), inc=self.s_dv)
            ps.read(pev)
            st.wrote(pev)
            sink(oc, stap, st, pev)

    def qkvb_col(self, chunk0):
        def f(oc):
            return self.g_f32(H_QKVB, 96)[:, chunk0 + oc:chunk0 + oc + 1]
        return f

    def p1_attn_a_proj(self):
        E = self.E
        xfm_slot = self.xfm[0]
        self.ln_pass(self.xq, 0, 8, GA0, xfm_slot)

        def kvsink(dst):
            def sink(oc, stap, st, pev):
                p = E.emit("ac", lambda e,
                           o=dst[oc * 128:(oc + 1) * 128, :],
                           i=stap: e.dma_start(out=o, in_=i),
                           waits=[pev], inc=self.s_st[oc % 2], inck=16)
                st.read(p)
            return sink

        for kv, base in (("K", 32), ("V", 64)):
            for hf in range(2):
                self.proj_fm(self.qkvw, base + hf * 16, 16,
                             self.qkvb_col(base + hf * 16), None,
                             GA0, xfm_slot, kvsink(self.kvin[kv, hf]))
                kv_done = [Pt(s, s.n) for s in self.s_st]
                self.ag_pts[kv, hf] = E.emit(
                    "gp", lambda e, i=self.kvin[kv, hf], o=self.kvag[kv, hf]:
                    e.collective_compute(
                        "AllGather", ALU.bypass,
                        replica_groups=[[0, 1, 2, 3], [4, 5, 6, 7]],
                        ins=[i.opt()], outs=[o.opt()]),
                    waits=kv_done, inc=self.s_ag, inck=1)

        def qsink(oc, stap, st, pev):
            p = E.emit("ac", lambda e, o=self.qfmd[oc * 128:(oc + 1) * 128, :],
                       i=stap: e.dma_start(out=o, in_=i),
                       waits=[pev], inc=self.s_st[oc % 2], inck=16)
            st.read(p)
            self.qst_pts[oc].append(p)
        self.proj_fm(self.qkvw, 0, 32, self.qkvb_col(0), SCALE,
                     GA0, xfm_slot, qsink)
        for t in ("qfmd",):
            if t in self.tap_aps:
                self.tap_dram(self.tap_aps[t], getattr(self, t), list(self.s_st))

    def tap_dram(self, dst, src, gate_sem):
        sems = gate_sem if isinstance(gate_sem, list) else [gate_sem]
        self.E.emit("sy", lambda e, o=dst, i=src: e.dma_start(out=o, in_=i),
                    waits=[Pt(s, s.n) for s in sems], inc=self.s_sy, inck=16)

    def p2_gather(self):
        E = self.E
        pb = E.emit("gp", lambda e, o=self.g_i32(H_BTAB, 2),
                    i=self.btab.rearrange("(a p) -> p a", p=128):
                    e.dma_start(out=o, in_=i, allow_slow_non_contiguous=True),
                    inc=self.s_gd, inck=16)
        self.btsb.wrote(pb)
        steps = []
        for src_, dst in ((self.kcache, self.kgd), (self.vcache, self.vgd)):
            for h in range(8):
                for half in range(2):
                    steps.append((src_, dst, h, half))

        def make_step(idx):
            src_, dst, h, half = steps[idx]
            par = idx % 2
            gslot = self.gst if par == 0 else self.gstB
            gsem = self.s_gd if par == 0 else self.s_gd2
            gstap = self.g_f32(A1_GST if par == 0 else A1_GSTB, 2048)

            def emit():
                pg = E.emit("gp", lambda e, o=gstap, i=src_, eo=h * 2048,
                            ix=self.g_i32(H_BTAB, 2)[:, half:half + 1]:
                            e.indirect_dma_start(o, None, i,
                                                 IndirectOffsetOnAxis(ix, 0),
                                                 element_offset=eo),
                            waits=[pb] + gslot.for_write(), inc=gsem, inck=16)
                gslot.wrote(pg)
                ps = E.emit("gp", lambda e,
                            o=dst[h, half * 2048:(half + 1) * 2048, :]
                            .rearrange("(p r) d -> p r d", p=128),
                            i=gstap.rearrange("p (r d) -> p r d", r=16):
                            e.dma_start(out=o, in_=i),
                            waits=[pg], inc=gsem, inck=16)
                gslot.read(ps)
            return emit

        self._gather_steps = [make_step(i) for i in range(len(steps))]

    def gather_tick(self, n=2):
        for _ in range(n):
            if self._gather_steps:
                self._gather_steps.pop(0)()

    def gather_flush(self):
        while self._gather_steps:
            self._gather_steps.pop(0)()
        self.gather_gate = [Pt(self.s_gd, self.s_gd.n),
                            Pt(self.s_gd2, self.s_gd2.n)]
        for t in ("kgd", "vgd"):
            if t in self.tap_aps:
                self.tap_dram(self.tap_aps[t], getattr(self, t), self.s_gd)

    def p3_attention(self, mode):
        E = self.E
        heads = [None] * NH

        def inplace_tm_fm_build(S, Sap, gate_pt):
            for bi in range(4):
                pc = self.psC[bi % 2]
                pcap = self.arena_bf[:, 4096 + (bi % 2) * 1024:
                                     4096 + (bi % 2) * 1024 + 1024]
                gpt = None
                for jj in range(8):
                    cc = bi * 8 + jj
                    w = ([gate_pt, self.identity.w] + pc.for_write()) if jj == 0 else []
                    gpt = E.emit("pe",
                                 lambda e, o=pcap[:, jj * 128:jj * 128 + 128],
                                 i=Sap[:, cc * 128:cc * 128 + 128],
                                 idn=self.g_bf(H_ID, 128):
                                 e.transpose(o, i, idn), waits=w,
                                 inc=self.s_pe if jj == 7 else None)
                pc.wrote(gpt)
                S.read(gpt)
                pev = E.emit("dv", lambda e,
                             o=Sap[:, bi * 1024:bi * 1024 + 1024],
                             i=pcap: e.tensor_copy(o, i),
                             waits=[gpt], inc=self.s_dv)
                pc.read(pev)
                if bi == 0:
                    S.wrote(pev)
                else:
                    S.add_write(pev)

        pending_builds = []

        def load_head(h):
            if mode == "a":
                K, Kap = self.kh[h % 2], self.g_bf(A1_K0 + (h % 2) * 4096, 4096)
                V, Vap = self.vh[h % 2], self.g_bf(A1_V0 + (h % 2) * 4096, 4096)
                agvK = self.kvag["K", h // 16].rearrange(
                    "(j c p) t -> c p j t", j=4, p=128)
                agvV = self.kvag["V", h // 16].rearrange(
                    "(j c p) t -> c p j t", j=4, p=128)
                pk = E.emit("sy", lambda e, o=Kap.rearrange("p (j t) -> p j t", j=4),
                            i=agvK[h % 16]:
                            e.dma_start(out=o, in_=i),
                            waits=K.for_write() + [self.ag_pts["K", h // 16]],
                            inc=self.s_k[h % 2], inck=16)
                K.wrote(pk)
                pv = E.emit("gp", lambda e, o=Vap.rearrange("p (j t) -> p j t", j=4),
                            i=agvV[h % 16]:
                            e.dma_start(out=o, in_=i),
                            waits=V.for_write() + [self.ag_pts["V", h // 16]],
                            inc=self.s_v[h % 2], inck=16)
                V.wrote(pv)
                pending_builds.append(
                    lambda V=V, Vap=Vap, pv=pv: inplace_tm_fm_build(V, Vap, pv))
            else:
                kvh = h // 4
                K, Kap = self.kh[kvh % 2], self.g_bf(A1_K0 + (kvh % 2) * 4096, 4096)
                V, Vap = self.vh[kvh % 2], self.g_bf(A1_V0 + (kvh % 2) * 4096, 4096)
                if h % 4 == 0:
                    pk = E.emit("sy", lambda e,
                                o=Kap.rearrange("p (c d) -> p c d", c=32),
                                i=self.kgd[kvh].rearrange("(c p) d -> p c d", p=128):
                                e.dma_start(out=o, in_=i),
                                waits=K.for_write() + self.gather_gate,
                                inc=self.s_k[kvh % 2], inck=16)
                    K.wrote(pk)
                    pending_builds.append(
                        lambda K=K, Kap=Kap, pk=pk: inplace_tm_fm_build(K, Kap, pk))
                    pv = E.emit("sy", lambda e,
                                o=Vap.rearrange("p (c d) -> p c d", c=32),
                                i=self.vgd[kvh].rearrange("(c p) d -> p c d", p=128):
                                e.dma_start(out=o, in_=i),
                                waits=V.for_write() + self.gather_gate,
                                inc=self.s_v[kvh % 2], inck=16)
                    V.wrote(pv)
            Q, Qap = self.qh[h % 2], self.g_bf(A1_Q0 + (h % 2) * 1024, 1024)
            pq = E.emit("sy", lambda e, o=Qap, i=self.qfmd[h * 128:(h + 1) * 128, :]:
                        e.dma_start(out=o, in_=i),
                        waits=Q.for_write() + self.qst_pts[h],
                        inc=self.s_q[h % 2], inck=16)
            Q.wrote(pq)
            heads[h] = (K, Kap, V, Vap, Q, Qap)

        def flush_builds():
            while pending_builds:
                pending_builds.pop(0)()

        def emit_ctx(h):
            K, Kap, V, Vap, Q, Qap = heads[h]
            ctxap = self.arena[:, 3072:4096]
            gpt = None
            for c in range(32):
                for sp in range(2):
                    w = (self.rA0.rw() + V.rw() + self.psD.for_write()) \
                        if (c == 0 and sp == 0) else []
                    is_last = c == 31 and sp == 1
                    gpt = E.emit("pe", lambda e, o=ctxap[:, sp * 512:sp * 512 + 512],
                                 l=Vap[:, c * 128:c * 128 + 128],
                                 r=self.grand[:, GA0 + c * 1024 + sp * 512:
                                              GA0 + c * 1024 + sp * 512 + 512],
                                 st=(c == 0), spp=(c == 31):
                                 e.matmul(o, l, r, start=st, stop=spp),
                                 waits=w, inc=self.s_pe if is_last else None)
            self.psD.wrote(gpt)
            self.rA0.read(gpt)
            V.read(gpt)
            cs = self.ctxs[h % 2]
            csap = self.g_bf(A1_CTXS0 + (h % 2) * 1024, 1024)
            pev = E.emit("dv", lambda e, o=csap, i=ctxap: e.tensor_copy(o, i),
                         waits=[gpt] + cs.for_write(), inc=self.s_dv)
            self.psD.read(pev)
            cs.wrote(pev)
            pst = E.emit("ac", lambda e, o=self.ctxd[h * 128:(h + 1) * 128, :], i=csap:
                         e.dma_start(out=o, in_=i), waits=[pev],
                         inc=self.s_stc[h % 2], inck=16)
            cs.read(pst)

        NU = NH * 8
        unit_state = [None] * NU
        for gu in range(NU + 2):
            h, i = divmod(gu, 8)
            if gu == 0:
                load_head(0)
                load_head(1)
                if mode == "b":
                    flush_builds()
            if gu < NU:
                self.unit_quarters(gu, 0, 2, heads[h], unit_state)
            if gu >= 1 and gu - 1 < NU:
                self.unit_norm(gu - 1, unit_state)
            if gu >= 2:
                self.unit_T(gu - 2, unit_state)
                if (gu - 2) % 8 == 3:
                    flush_builds()
                if (gu - 2) % 8 == 7:
                    hh = (gu - 2) // 8
                    emit_ctx(hh)
                    if hh + 2 < NH:
                        load_head(hh + 2)
                    if mode == "a":
                        self.gather_tick(2)
            if gu < NU:
                self.unit_quarters(gu, 2, 4, heads[h], unit_state)

    def probs3(self, par):
        slots = [self.lnt[0], self.lnt[1], self.probsx]
        cols = [GD, GD + 4096, GD2]
        return slots[par], self.g_bf(cols[par], 4096)

    def unit_quarters(self, gu, k0, k1, head, unit_state):
        E = self.E
        g = self.g_f32
        K, Kap, V, Vap, Q, Qap = head
        i = gu % 8
        par = gu % 3
        if unit_state[gu] is None:
            unit_state[gu] = {"par": par}
        stt = unit_state[gu]
        probs, probsap = self.probs3(par)
        stt["probs"], stt["probsap"] = probs, probsap
        nm_ap = g(H_ANM + 2 * par, 1)
        sums_ap = g(H_ASUM + 8 * par, 4)
        stt["nm_ap"], stt["sums_ap"] = nm_ap, sums_ap
        for k in range(k0, k1):
            ps = self.psA if k % 2 == 0 else self.psB
            psap = self.arena[:, (k % 2) * 1024:(k % 2) * 1024 + 1024]
            gpt = None
            for sp in range(2):
                w = ([K.w, Q.w] + ps.for_write()) if sp == 0 else []
                gpt = E.emit("pe", lambda e, o=psap[:, sp * 512:sp * 512 + 512],
                             l=Qap[:, i * 128:i * 128 + 128],
                             r=Kap[:, k * 1024 + sp * 512:k * 1024 + sp * 512 + 512]:
                             e.matmul(o, l, r, start=True, stop=True),
                             waits=w, inc=self.s_pe if sp == 1 else None)
            ps.wrote(gpt)
            K.read(gpt)
            Q.read(gpt)
            if k == 0:
                nm = self.att_nm[par]
                pmx = E.emit("dv", lambda e, o=nm_ap, i=self.arena[:, 0:1024:8]:
                             e.reduce_max(o, i, axis=AX.X, negate=True),
                             waits=[gpt] + nm.for_write(), inc=self.s_dv)
                nm.wrote(pmx)
                stt["pmx"] = pmx
            sums = self.att_sums[par]
            pex = E.emit("ac", lambda e, o=probsap[:, k * 1024:k * 1024 + 1024],
                         i=psap, b=nm_ap, a=sums_ap[:, k:k + 1]:
                         e.activation(o, i, AF.Exp, bias=b, scale=1.0, accum_out=a),
                         waits=[stt["pmx"], gpt] +
                               (probs.for_write() if k == 0 else []) +
                               (sums.for_write() if k == 0 else []),
                         inc=self.s_ac)
            ps.read(pex)
            if k == 0:
                probs.wrote(pex)
                sums.wrote(pex)
            else:
                probs.add_write(pex)
                sums.add_write(pex)
            if k == 3:
                stt["exp_last"] = pex
                self.att_nm[par].read(pex)

    def unit_norm(self, gu, unit_state):
        E = self.E
        g = self.g_f32
        stt = unit_state[gu]
        par = stt["par"]
        probs, probsap = stt["probs"], stt["probsap"]
        ss = self.att_ss[par]
        pss = E.emit("dv", lambda e, o=g(H_ASS + 2 * par, 1), i=stt["sums_ap"]:
                     e.reduce_sum(o, i, axis=AX.X),
                     waits=[stt["exp_last"]] + ss.for_write(), inc=self.s_dv)
        prc = E.emit("dv", lambda e, o=g(H_ARC + 2 * par, 1), i=g(H_ASS + 2 * par, 1):
                     e.reciprocal(o, i), waits=[pss], inc=self.s_dv)
        ss.wrote(prc)
        self.att_sums[par].read(pss)
        pr = E.emit("dv", lambda e, o=probsap, s=g(H_ARC + 2 * par, 1):
                    e.tensor_scalar_mul(o, o, s),
                    waits=[prc, probs.w], inc=self.s_dv)
        probs.wrote(pr)
        ss.read(pr)
        stt["pr"] = pr

    def unit_T(self, gu, unit_state):
        E = self.E
        stt = unit_state[gu]
        i = gu % 8
        probs, probsap = stt["probs"], stt["probsap"]
        pr = stt["pr"]
        for bi in range(4):
            pc = self.psC[bi % 2]
            pcap = self.arena_bf[:, 4096 + (bi % 2) * 1024:
                                 4096 + (bi % 2) * 1024 + 1024]
            gpt = None
            for jj in range(8):
                cc = bi * 8 + jj
                w = ([pr, self.identity.w] + pc.for_write()) if jj == 0 else []
                gpt = E.emit("pe", lambda e, o=pcap[:, jj * 128:jj * 128 + 128],
                             i=probsap[:, cc * 128:cc * 128 + 128],
                             dk=self.g_bf(H_ID, 128):
                             e.transpose(o, i, dk), waits=w,
                             inc=self.s_pe if jj == 7 else None)
            pc.wrote(gpt)
            probs.read(gpt)
            base = GA0 + (bi * 8) * 1024 + i * 128
            dst = self.grand[:, base: base + 8 * 1024].rearrange(
                "p (c t) -> p c t", c=8)[:, :, 0:128]
            eng = "ac" if bi == 3 else "dv"
            sem = self.s_ac if bi == 3 else self.s_dv
            if eng == "ac":
                fn = (lambda e, o=dst, i=pcap.rearrange("p (c t) -> p c t", c=8):
                      e.copy(o, i))
            else:
                fn = (lambda e, o=dst, i=pcap.rearrange("p (c t) -> p c t", c=8):
                      e.tensor_copy(o, i))
            pev = E.emit(eng, fn,
                         waits=[gpt] + (self.rA0.for_write() if (i == 0 and bi == 0)
                                        else []),
                         inc=sem)
            pc.read(pev)
            if i == 0 and bi == 0:
                self.rA0.wrote(pev)
            else:
                self.rA0.add_write(pev)

    def p4_outproj(self, wdram, bias, xsrc, dst):
        E = self.E
        pl = E.emit("sy", lambda e,
                    o=self.grand[:, GA0:GA0 + 32768].rearrange("p (c t) -> p c t", c=32),
                    i=self.ctxd.rearrange("(c p) t -> p c t", p=128):
                    e.dma_start(out=o, in_=i),
                    waits=self.rA0.for_write() + [Pt(s, s.n) for s in self.s_stc],
                    inc=self.s_cxl, inck=16)
        self.rA0.wrote(pl)
        bias_fn = (lambda oc: self.g_f32(H_OBT, 32)[:, oc:oc + 1]) if bias else None
        guards = self.head_guards() + self.rA1.for_write()

        def osink(oc, stap, st, pev):
            p = E.emit("dv", lambda e, o=self.g_bf(GA1 + oc * 1024, 1024), i=stap:
                       e.tensor_copy(o, i),
                       waits=[pev] + (guards if oc == 0 else []),
                       inc=self.s_dv)
            st.read(p)
            self.rA1.wrote(p)

        self.proj_fm(wdram, 0, 32, bias_fn, None, GA0, self.rA0, osink)
        self.add_pass(xsrc, dst)

    def add_pass(self, xsrc, dst):
        E = self.E
        g = self.g_f32
        for j in range(8):
            par = j % 2
            xt = self.xt[par]
            xtap = g(GC + par * 8192, 4096)
            pl = E.emit("sy", lambda e, o=xtap, i=xsrc[j * 128:(j + 1) * 128, :]:
                        e.dma_start(out=o, in_=i),
                        waits=xt.for_write() + (self.last_store or []),
                        inc=self.s_xt[par], inck=16)
            xt.wrote(pl)
            addpts = []
            for half in range(2):
                if half == 0:
                    slots = [self.psC[0], self.psC[1]]
                else:
                    slots = [self.psD]
                pap = self.arena_bf[:, (4 + half * 2) * 1024:
                                    (4 + half * 2) * 1024 + 2048]
                gpt = None
                for ocl in range(16):
                    oc = half * 16 + ocl
                    if ocl == 0:
                        w = [self.rA1.w, self.identity.w]
                        for s in slots:
                            w += s.for_write()
                    else:
                        w = []
                    gpt = E.emit("pe", lambda e, o=pap[:, ocl * 128:ocl * 128 + 128],
                                 i=self.g_bf(GA1 + oc * 1024 + j * 128, 128),
                                 idn=self.g_bf(H_ID, 128):
                                 e.transpose(o, i, idn), waits=w,
                                 inc=self.s_pe if ocl == 15 else None)
                for s in slots:
                    s.wrote(gpt)
                self.rA1.read(gpt)
                padd = E.emit("dv", lambda e, o=xtap[:, half * 2048:half * 2048 + 2048],
                              i=pap: e.tensor_tensor(o, o, i, ALU.add),
                              waits=[gpt, xt.w], inc=self.s_dv)
                for s in slots:
                    s.read(padd)
                addpts.append(padd)
            pst = E.emit("ac", lambda e, o=dst[j * 128:(j + 1) * 128, :], i=xtap:
                         e.dma_start(out=o, in_=i), waits=addpts,
                         inc=self.s_stx[par], inck=16)
            xt.read(pst)
        self.last_store = [Pt(s, s.n) for s in self.s_stx]

    def p5_ffn(self, dst):
        E = self.E
        g = self.g_f32
        self.ln_pass(self.xd, 0, 8, GA0, self.rA0, gate=self.last_store)
        hw = self.lora["hsel"].w
        for a in range(3):
            E.emit("gp", lambda e, o=self.g_bf(GA1 + a * 4096, 4096, parts=16),
                   i=self.loraA[a]:
                   e.dma_start(out=o, in_=i),
                   waits=(self.rA1.for_write() + self.head_guards())
                   if a == 0 else [],
                   inc=self.s_lor, inck=16)
        for a in range(3):
            E.emit("gp", lambda e,
                   o=self.g_bf(GA1 + 16384 + a * 512, 512).rearrange("p (c r) -> p c r", c=32),
                   i=self.loraB[a].rearrange("(c p) r -> p c r", p=128):
                   e.dma_start(out=o, in_=i), waits=[],
                   inc=self.s_lor, inck=16)
        lor_pt = Pt(self.s_lor, self.s_lor.n)
        apts = [lor_pt, lor_pt, lor_pt]
        ah = self.g_bf(GA1 + 3 * 4096, 4096, parts=16)
        p = E.emit("dv", lambda e, o=ah, i=self.g_bf(GA1 + 0, 4096, parts=16),
                   s=g(H_HSEL, 3, parts=16)[:, 0:1]:
                   e.tensor_scalar_mul(o, i, s), waits=[apts[0], hw], inc=self.s_dv)
        for a in (1, 2):
            p = E.emit("dv", lambda e, o=ah, i=self.g_bf(GA1 + a * 4096, 4096, parts=16),
                       s=g(H_HSEL, 3, parts=16)[:, a:a + 1]:
                       e.scalar_tensor_tensor(o, i, s, o, ALU.mult, ALU.add),
                       waits=[apts[a], p], inc=self.s_dv)
        pah = p
        self.rA1.wrote(pah)
        for bi in range(4):
            pc = self.psC[bi % 2]
            pcap = self.arena_bf[:, 4096 + (bi % 2) * 1024:
                                 4096 + (bi % 2) * 1024 + 1024]
            gpt = None
            for jj in range(8):
                cc = bi * 8 + jj
                w = ([pah, self.identity.w] + pc.for_write()) if jj == 0 else []
                gpt = E.emit("pe", lambda e, o=pcap[:, jj * 16:jj * 16 + 16],
                             i=ah[:, cc * 128:cc * 128 + 128],
                             idn=self.g_bf(H_ID, 128, parts=16)[:, 0:16]:
                             e.transpose(o, i, idn), waits=w,
                             inc=self.s_pe if jj == 7 else None)
            pc.wrote(gpt)
            self.rA1.read(gpt)
            pev = E.emit("dv", lambda e, o=self.g_bf(D_AHT + bi * 128, 128),
                         i=pcap[:, 0:128]: e.tensor_copy(o, i),
                         waits=[gpt] + (self.lora["AhT"].for_write() +
                                        self.lnt[0].for_write() if bi == 0 else []),
                         inc=self.s_dv)
            pc.read(pev)
            self.lora["AhT"].wrote(pev)
        bpts = [lor_pt, lor_pt, lor_pt]
        bh = self.g_bf(GA1 + 16384 + 3 * 512, 512)
        p = E.emit("dv", lambda e, o=bh, i=self.g_bf(GA1 + 16384, 512),
                   s=g(H_HSEL, 3)[:, 0:1]:
                   e.tensor_scalar_mul(o, i, s), waits=[bpts[0], hw], inc=self.s_dv)
        for a in (1, 2):
            p = E.emit("dv", lambda e, o=bh, i=self.g_bf(GA1 + 16384 + a * 512, 512),
                       s=g(H_HSEL, 3)[:, a:a + 1]:
                       e.scalar_tensor_tensor(o, i, s, o, ALU.mult, ALU.add),
                       waits=[bpts[a], p], inc=self.s_dv)
        pbh = p
        self.rA1.wrote(pbh)
        for bi in range(4):
            pc = self.psC[bi % 2]
            pcap = self.arena_bf[0:16, 4096 + (bi % 2) * 1024:
                                 4096 + (bi % 2) * 1024 + 1024]
            gpt = None
            for jj in range(8):
                cc = bi * 8 + jj
                w = ([pbh, self.identity.w] + pc.for_write()) if jj == 0 else []
                gpt = E.emit("pe", lambda e, o=pcap[:, jj * 128:jj * 128 + 128],
                             i=bh[:, cc * 16:cc * 16 + 16],
                             idn=self.g_bf(H_ID, 128):
                             e.transpose(o, i, idn), waits=w,
                             inc=self.s_pe if jj == 7 else None)
            pc.wrote(gpt)
            self.rA1.read(gpt)
            pev = E.emit("dv", lambda e, o=self.g_bf(D_BHT + bi * 1024, 1024, parts=16),
                         i=pcap: e.tensor_copy(o, i),
                         waits=[gpt] + (self.lora["BhT"].for_write() +
                                        self.lnt[1].for_write() if bi == 0 else []),
                         inc=self.s_dv)
            pc.read(pev)
            self.lora["BhT"].wrote(pev)
        zap = self.arena[0:16, 3072:4096]
        gpt = None
        for dc in range(32):
            for sp in range(2):
                w = ([self.lora["AhT"].w, self.rA0.w] + self.psD.for_write()) \
                    if (dc == 0 and sp == 0) else []
                is_last = dc == 31 and sp == 1
                gpt = E.emit("pe", lambda e, o=zap[:, sp * 512:sp * 512 + 512],
                             l=self.g_bf(D_AHT + dc * 16, 16),
                             r=self.grand[:, GA0 + dc * 1024 + sp * 512:
                                          GA0 + dc * 1024 + sp * 512 + 512],
                             st=(dc == 0), spp=is_last:
                             e.matmul(o, l, r, start=st, stop=spp),
                             waits=w, inc=self.s_pe if is_last else None)
        self.psD.wrote(gpt)
        self.lora["AhT"].read(gpt)
        self.rA0.read(gpt)
        pz = E.emit("dv", lambda e, o=self.g_bf(D_ZSB, 1024, parts=16), i=zap:
                    e.tensor_copy(o, i),
                    waits=[gpt] + self.lora["zsb"].for_write(), inc=self.s_dv)
        self.psD.read(pz)
        self.lora["zsb"].wrote(pz)

        def fsink(oc, stap, st, pev):
            p = E.emit("dv", lambda e, o=self.g_bf(GA1 + oc * 1024, 1024), i=stap:
                       e.tensor_copy(o, i),
                       waits=[pev] + (self.rA1.for_write() if oc == 0 else []),
                       inc=self.s_dv)
            st.read(p)
            self.rA1.wrote(p)

        self.proj_fm(self.wft, 0, 32,
                     lambda oc: self.g_f32(H_FBT, 32)[:, oc:oc + 1], None,
                     GA0, self.rA0, fsink, lora=True)
        self.add_pass(self.xd, dst)

    def p6_attn_b(self):
        E = self.E
        g = self.g_f32
        self.ln_pass(self.xd, 0, 8, GA0, self.rA0, gate=self.last_store)
        E.emit("gp", lambda e, o=self.g_bf(A1_COS, 1024), i=self.cosf:
               e.dma_start(out=o, in_=i),
               waits=self.rA1.for_write() + self.gstB.for_write() +
               [Pt(self.s_gd, self.s_gd.n), Pt(self.s_gd2, self.s_gd2.n)],
               inc=self.s_misc, inck=16)
        E.emit("gp", lambda e, o=self.g_bf(A1_SIN, 1024), i=self.sinf:
               e.dma_start(out=o, in_=i), inc=self.s_misc, inck=16)
        psin = Pt(self.s_misc, self.s_misc.n)
        self.rA1.wrote(psin)
        rope_guard = [psin]

        def qsink(oc, stap, st, pev):
            rot = self.g_bf(A1_ROT, 1024)
            t1 = self.g_bf(A1_RT1, 1024)
            p1 = E.emit("dv", lambda e, o=rot[0:64, :], i=stap[64:128, :]:
                        e.tensor_scalar_mul(o, i, -1.0),
                        waits=[pev] + rope_guard, inc=self.s_dv)
            p2 = E.emit("dv", lambda e, o=rot[64:128, :], i=stap[0:64, :]:
                        e.tensor_copy(o, i), waits=[pev], inc=self.s_dv)
            p3 = E.emit("dv", lambda e, o=t1, a=stap, b=self.g_bf(A1_COS, 1024):
                        e.tensor_tensor(o, a, b, ALU.mult), waits=[pev], inc=self.s_dv)
            p4 = E.emit("dv", lambda e, o=rot, a=rot, b=self.g_bf(A1_SIN, 1024):
                        e.tensor_tensor(o, a, b, ALU.mult), waits=[p1, p2],
                        inc=self.s_dv)
            p5 = E.emit("dv", lambda e, o=stap, a=t1, b=rot:
                        e.tensor_tensor(o, a, b, ALU.add), waits=[p3, p4],
                        inc=self.s_dv)
            st.wrote(p5)
            self.rA1.read(p5)
            p = E.emit("ac", lambda e, o=self.qfmd[oc * 128:(oc + 1) * 128, :], i=stap:
                       e.dma_start(out=o, in_=i), waits=[p5],
                       inc=self.s_st[oc % 2], inck=16)
            st.read(p)
            self.qst_pts[oc] = [p]

        self.proj_fm(self.wqt, 0, 32, None, SCALE, GA0, self.rA0, qsink)
        if "qfmd" in self.tap_aps:
            self.tap_dram(self.tap_aps["qfmd"], self.qfmd, list(self.s_st))
        for s in (self.kh + self.vh + self.qh + self.ctxs):
            s.seed(self.rA1.for_write())
        self.p3_attention("b")
        if "ctxd" in self.tap_aps:
            self.tap_dram(self.tap_aps["ctxd"], self.ctxd, list(self.s_stc))
        self.p4_outproj(self.wot, bias=False, xsrc=self.xd, dst=self.xd)

    def finalize(self):
        E = self.E
        allsems = ([self.s_gd, self.s_sy, self.s_setup, self.s_cxl, self.s_misc,
                    self.s_lor, self.s_ag] + self.s_xt + self.s_w + self.s_k +
                   self.s_v + self.s_q + self.s_st + self.s_stc + self.s_stx)
        last = []
        if self.upto < 4:
            last = [E.emit("sy", lambda e: e.dma_start(out=self.y, in_=self.xq),
                           inc=self.s_sy, inck=16)]
        elif self.upto < 7:
            last = [E.emit("sy", lambda e: e.dma_start(out=self.y, in_=self.xd),
                           waits=list(self.last_store), inc=self.s_sy, inck=16)]
        else:
            last = list(self.last_store)
        if "xd" in self.tap_aps and self.last_store is not None:
            last = [E.emit("sy", lambda e:
                           e.dma_start(out=self.tap_aps["xd"], in_=self.xd),
                           waits=list(self.last_store), inc=self.s_sy, inck=16)]
        waits = last + [Pt(s, s.n) for s in allsems]
        E.emit("sy", lambda e: e.nop(), waits=waits)


def build_program(upto=7, taps=()):
    return Builder(upto=upto, taps=taps).build()



def _tile_w(W):
    dc = W.reshape(32, 128, -1, 128)
    return np.ascontiguousarray(dc.transpose(2, 1, 0, 3).reshape(-1, 128, 4096))


def _rope_tables():
    dim = 128
    inv_freq = (1.0 / (10000.0 ** (np.arange(0, dim, 2, dtype=np.float32)
                                   / np.float32(dim)))).astype(np.float32)
    t = np.arange(BT, dtype=np.float32)
    freqs = np.outer(t, inv_freq).astype(np.float32)
    emb = np.concatenate([freqs, freqs], axis=-1)
    return np.cos(emb).astype(np.float32), np.sin(emb).astype(np.float32)


_UPTO = 7
_TAPS = ()


def kernel(**inputs):
    x = np.ascontiguousarray(inputs["L0_attn_a_x"])
    qkvw = np.ascontiguousarray(inputs["L0_attn_a_attn_qkvw"])
    qkvb = np.asarray(inputs["L0_attn_a_attn_qkvb"])
    ow = np.ascontiguousarray(inputs["L0_attn_a_attn_ow"])
    ob = np.asarray(inputs["L0_attn_a_attn_ob"])
    kc = np.ascontiguousarray(np.asarray(inputs["L0_attn_b_k_cache"]).reshape(512, 16384))
    vc = np.ascontiguousarray(np.asarray(inputs["L0_attn_b_v_cache"]).reshape(512, 16384))
    wq = np.asarray(inputs["L0_attn_b_Wq"])
    wo = np.asarray(inputs["L0_attn_b_Wo"])
    bt = np.ascontiguousarray(np.asarray(inputs["L0_attn_b_block_table"]).astype(np.int32))
    fw = np.asarray(inputs["L0_ffn_weight"])
    fb = np.asarray(inputs["L0_ffn_bias"])
    lA = np.ascontiguousarray(inputs["L0_ffn_lora_A"])
    lB = np.ascontiguousarray(inputs["L0_ffn_lora_B"])
    lsi = np.ascontiguousarray(np.asarray(inputs["L0_ffn_lora_start_indices"]).astype(np.int32))

    qkvw = _tile_w(qkvw)
    ow = _tile_w(ow)
    wqt = _tile_w(np.ascontiguousarray(wq.T))
    wot = _tile_w(np.ascontiguousarray(wo.T))
    wft = _tile_w(np.ascontiguousarray(fw.T))
    qkvbT = np.ascontiguousarray(qkvb.reshape(96, 128).T)
    obT = np.ascontiguousarray(ob.reshape(32, 128).T)
    fbT = np.ascontiguousarray(fb.reshape(32, 128).T)
    cos_t, sin_t = _rope_tables()

    nc = build_program(upto=_UPTO, taps=_TAPS)
    in_maps = []
    for c in range(NCORES):
        b, qi = c // 4, c % 4
        in_maps.append({
            "xq": np.ascontiguousarray(x[b, qi * NT:(qi + 1) * NT]),
            "qkvw": qkvw, "qkvbT": qkvbT, "ow": ow, "obT": obT,
            "wqt": wqt, "wot": wot, "wft": wft, "fbT": fbT,
            "kcache": kc, "vcache": vc, "btab": bt, "lsi": lsi,
            "bval": np.array([[float(b)]], dtype=np.float32),
            "loraA": lA, "loraB": lB,
            "cosf": np.ascontiguousarray(cos_t[qi * NT:(qi + 1) * NT].T),
            "sinf": np.ascontiguousarray(sin_t[qi * NT:(qi + 1) * NT].T),
        })
    res = run_bass_kernel_spmd(nc, in_maps, core_ids=list(range(NCORES)))
    out = np.empty((2, BT, D), dtype=np.float32)
    for c in range(NCORES):
        b, qi = c // 4, c % 4
        out[b, qi * NT:(qi + 1) * NT] = res.results[c]["y"]
    if _TAPS:
        kernel.tap_results = res.results
    return out


# version 06
# speedup vs baseline: 1.3728x; 1.0154x over previous
import numpy as np

import concourse.bass as bass
import concourse.mybir as mybir
from concourse.bass_utils import run_bass_kernel_spmd
from concourse.bass_core_rust import IndirectOffsetOnAxis

F32 = mybir.dt.float32
BF16 = mybir.dt.bfloat16
I32 = mybir.dt.int32
AF = mybir.ActivationFunctionType
ALU = mybir.AluOpType
AX = mybir.AxisListType

D = 4096
NT = 1024
BT = 4096
NH = 32
HD = 128
SCALE = float(1.0 / np.sqrt(128.0))
NCORES = 8



class Sem:
    __slots__ = ("name", "n", "h")

    def __init__(self, name):
        self.name = name
        self.n = 0
        self.h = None


class Pt:
    __slots__ = ("sem", "val")

    def __init__(self, sem, val):
        self.sem = sem
        self.val = val


class Slot:
    __slots__ = ("ws", "r")

    def __init__(self):
        self.ws = []
        self.r = []

    @property
    def w(self):
        return self.ws[-1] if self.ws else None

    def rw(self):
        return list(self.ws)

    def for_write(self):
        return list(self.r) + list(self.ws)

    def wrote(self, pt):
        assert pt is not None
        self.ws = [pt]
        self.r = []

    def add_write(self, pt):
        assert pt is not None
        self.ws.append(pt)

    def read(self, pt):
        if pt is not None:
            self.r.append(pt)

    def seed(self, pts):
        self.r.extend(p for p in pts if p is not None)


class Emitter:
    ENGINES = ("pe", "dv", "ac", "gp", "sy")

    def __init__(self):
        self.q = {e: [] for e in self.ENGINES}
        self.sems = {}
        self.waited = {e: {} for e in self.ENGINES}

    def S(self, name):
        if name not in self.sems:
            self.sems[name] = Sem(name)
        return self.sems[name]

    def emit(self, eng, fn, waits=(), inc=None, inck=1):
        seen = {}
        for pt in waits:
            if pt is None:
                continue
            key = pt.sem.name
            cur = seen.get(key, self.waited[eng].get(key, -1))
            if pt.val > cur:
                seen[key] = pt.val
        ws = []
        for key, val in seen.items():
            self.waited[eng][key] = max(self.waited[eng].get(key, -1), val)
            ws.append((self.sems[key], val))
        pt_out = None
        if inc is not None:
            inc.n += inck
            pt_out = Pt(inc, inc.n)
        self.q[eng].append((ws, fn, inc, inck))
        return pt_out

    def replay(self, block):
        def run(stream):
            def body(eng):
                for ws, fn, inc, inck in stream:
                    for sem, val in ws:
                        eng.wait_ge(sem.h, val)
                    ins = fn(eng)
                    if inc is not None:
                        ins.then_inc(inc.h, inck)
            return body

        block.tensor(run(self.q["pe"]))
        block.vector(run(self.q["dv"]))
        block.scalar(run(self.q["ac"]))
        block.gpsimd(run(self.q["gp"]))
        block.sync(run(self.q["sy"]))



GCOLS = 105536
GA0 = 0
GA1 = 32768
GC = 65536
GD = 81920
GE = 90112
GH = 98304
ST0 = 99392
ST1 = 100416
GD2 = 101440
ST2 = GD2

A1_K0 = GA1 + 0
A1_K1 = GA1 + 4096
A1_V0 = GA1 + 8192
A1_V1 = GA1 + 12288
A1_Q0 = GA1 + 16384
A1_Q1 = GA1 + 17408
A1_CTXS0 = GA1 + 18432
A1_CTXS1 = GA1 + 19456
A1_GST = GA1 + 20480
A1_GSTB = GA1 + 24576
A1_COS = GA1 + 28672
A1_SIN = GA1 + 29696
A1_ROT = GA1 + 30720
A1_RT1 = GA1 + 31744

D_AHT = GD + 0
D_BHT = GD + 512
D_ZSB = GD + 4608

H_ID = GH + 0
H_D0 = GH + 128
H_QKVB = GH + 640
H_OBT = GH + 832
H_FBT = GH + 896
H_ANM = GH + 960
H_ASUM = GH + 966
H_ASS = GH + 990
H_ARC = GH + 996
H_LN = GH + 1008
H_LSI_I = GH + 1040
H_LSI_F = GH + 1048
H_BVAL = GH + 1056
H_TMP = GH + 1058
H_HSEL = GH + 1074
H_BTAB = GH + 1080

LN_S1, LN_NEGM, LN_M2, LN_VAR, LN_LNV, LN_RSTD, LN_SUMSQ = range(7)


def lncol(stat, par):
    return H_LN + stat * 4 + par * 2


class Builder:
    def __init__(self, upto=7, taps=()):
        self.upto = upto
        self.taps = taps
        self.nc = bass.Bass("TRN2", target_bir_lowering=False)
        nc = self.nc
        self.xq = nc.dram_tensor("xq", [NT, D], F32, kind="ExternalInput").ap()
        self.qkvw = nc.dram_tensor("qkvw", [96, 128, 4096], F32, kind="ExternalInput").ap()
        self.qkvbT = nc.dram_tensor("qkvbT", [128, 96], F32, kind="ExternalInput").ap()
        self.ow = nc.dram_tensor("ow", [32, 128, 4096], F32, kind="ExternalInput").ap()
        self.obT = nc.dram_tensor("obT", [128, 32], F32, kind="ExternalInput").ap()
        self.wqt = nc.dram_tensor("wqt", [32, 128, 4096], F32, kind="ExternalInput").ap()
        self.wot = nc.dram_tensor("wot", [32, 128, 4096], F32, kind="ExternalInput").ap()
        self.wft = nc.dram_tensor("wft", [32, 128, 4096], F32, kind="ExternalInput").ap()
        self.fbT = nc.dram_tensor("fbT", [128, 32], F32, kind="ExternalInput").ap()
        self.kcache = nc.dram_tensor("kcache", [512, 16384], F32, kind="ExternalInput").ap()
        self.vcache = nc.dram_tensor("vcache", [512, 16384], F32, kind="ExternalInput").ap()
        self.btab = nc.dram_tensor("btab", [256], I32, kind="ExternalInput").ap()
        self.lsi = nc.dram_tensor("lsi", [4], I32, kind="ExternalInput").ap()
        self.bval = nc.dram_tensor("bval", [1, 1], F32, kind="ExternalInput").ap()
        self.loraA = nc.dram_tensor("loraA", [3, 16, D], F32, kind="ExternalInput").ap()
        self.loraB = nc.dram_tensor("loraB", [3, D, 16], F32, kind="ExternalInput").ap()
        self.cosf = nc.dram_tensor("cosf", [128, NT], F32, kind="ExternalInput").ap()
        self.sinf = nc.dram_tensor("sinf", [128, NT], F32, kind="ExternalInput").ap()
        self.y = nc.dram_tensor("y", [NT, D], F32, kind="ExternalOutput").ap()
        self.xd = nc.dram_tensor("xd", [NT, D], F32, kind="Internal").ap()
        self.kvin = {}
        self.kvag = {}
        for kv in "KV":
            for hf in range(2):
                self.kvin[kv, hf] = nc.dram_tensor(
                    "kvin%s%d" % (kv, hf), [D // 2, NT], BF16, kind="Internal").ap()
                self.kvag[kv, hf] = nc.dram_tensor(
                    "kvag%s%d" % (kv, hf), [2 * D, NT], BF16, kind="Internal").ap()
        self.ag_pts = {}
        self.qfmd = nc.dram_tensor("qfmd", [D, NT], BF16, kind="Internal").ap()
        self.ctxd = nc.dram_tensor("ctxd", [D, NT], BF16, kind="Internal").ap()
        self.kgd = nc.dram_tensor("kgd", [8, BT, HD], BF16, kind="Internal").ap()
        self.vgd = nc.dram_tensor("vgd", [8, BT, HD], BF16, kind="Internal").ap()

        self.tap_aps = {}
        shapes = {"qfmd": ([D, NT], BF16), "ctxd": ([D, NT], BF16),
                  "kgd": ([8, BT, HD], BF16), "vgd": ([8, BT, HD], BF16),
                  "xd": ([NT, D], F32)}
        for t in taps:
            if t == "adbg":
                self.dbg_pbT = nc.dram_tensor("tap_pbT", [128, 32768], BF16,
                                              kind="ExternalOutput").ap()
                self.dbg_probs = nc.dram_tensor("tap_probs", [128, 8192], BF16,
                                                kind="ExternalOutput").ap()
                self.dbg_small = nc.dram_tensor("tap_small", [128, 1024], F32,
                                                kind="ExternalOutput").ap()
                continue
            sh, dt = shapes[t]
            self.tap_aps[t] = nc.dram_tensor("tap_" + t, sh, dt,
                                             kind="ExternalOutput").ap()

        E = Emitter()
        self.E = E
        self.s_pe = E.S("s_pe")
        self.s_dv = E.S("s_dv")
        self.s_ac = E.S("s_ac")
        self.s_gp = E.S("s_gp")
        self.s_sy = E.S("s_sy")
        self.s_sc = E.S("s_sc")
        self.s_gd = E.S("s_gd")
        self.s_setup = E.S("s_setup")
        self.s_xt = [E.S("s_xt0"), E.S("s_xt1")]
        self.s_w = [E.S("s_w0"), E.S("s_w1")]
        self.s_k = [E.S("s_k0"), E.S("s_k1")]
        self.s_v = [E.S("s_v0"), E.S("s_v1")]
        self.s_q = [E.S("s_q0"), E.S("s_q1")]
        self.s_st = [E.S("s_st0"), E.S("s_st1"), E.S("s_st2")]
        self.s_stc = [E.S("s_stc0"), E.S("s_stc1")]
        self.s_stx = [E.S("s_stx0"), E.S("s_stx1")]
        self.s_cxl = E.S("s_cxl")
        self.s_misc = E.S("s_misc")
        self.s_lor = E.S("s_lor")
        self.s_gd2 = E.S("s_gd2")
        self.s_ag = E.S("s_ag")

        self.xt = [Slot(), Slot()]
        self.lnt = [Slot(), Slot()]
        self.lns = [Slot(), Slot()]
        self.wst = [Slot(), Slot()]
        self.xfm = [Slot(), Slot()]
        self.stg = [Slot(), Slot(), Slot()]
        self.psA = Slot()
        self.psB = Slot()
        self.psC = [Slot(), Slot()]
        self.psD = Slot()
        self.kh = [Slot(), Slot()]
        self.vh = [Slot(), Slot()]
        self.qh = [Slot(), Slot()]
        self.ctxs = [Slot(), Slot()]
        self.gst = Slot()
        self.rA0 = Slot()
        self.rA1 = Slot()
        self.identity = Slot()
        self.biases = Slot()
        self.att_nm = [Slot(), Slot(), Slot()]
        self.att_sums = [Slot(), Slot(), Slot()]
        self.att_ss = [Slot(), Slot(), Slot()]
        self.gstB = Slot()
        self.probsx = Slot()
        self._gather_steps = []
        self.kst_pts = [[] for _ in range(32)]
        self.vst_pts = [[] for _ in range(32)]
        self.qst_pts = [[] for _ in range(32)]
        self.lora = {k: Slot() for k in ("AhT", "BhT", "zsb", "hsel")}
        self.btsb = Slot()
        self.last_store = None

    def g_bf(self, col, n, parts=128, p0=0):
        return self.grand[p0:p0 + parts, col:col + n]

    def g_f32(self, col, n, parts=128, p0=0):
        return self.grand_f32[p0:p0 + parts, col // 2:col // 2 + n]

    def g_i32(self, col, n, parts=128, p0=0):
        return self.grand_i32[p0:p0 + parts, col // 2:col // 2 + n]

    def head_guards(self):
        pts = []
        for s in (self.kh + self.vh + self.qh + self.ctxs + [self.gst, self.gstB]):
            pts.extend(s.for_write())
        return pts

    def build(self):
        from contextlib import ExitStack
        nc, E = self.nc, self.E
        with ExitStack() as stack:
            grand_h = stack.enter_context(nc.sbuf_tensor("grand", [128, GCOLS], BF16))
            arena_h = stack.enter_context(nc.psum_tensor("arena", [128, 4096], F32))
            for name in list(E.sems):
                E.sems[name].h = stack.enter_context(nc.semaphore(name))
            self.grand = grand_h[:]
            self.grand_f32 = grand_h[:].bitcast(F32)
            self.grand_i32 = grand_h[:].bitcast(I32)
            self.arena = arena_h[:]
            self.arena_bf = arena_h[:].bitcast(BF16)

            self.p0_setup()
            if self.upto >= 1:
                self.p1_attn_a_proj()
            if self.upto >= 2:
                self.p2_gather()
                if self.upto == 2:
                    self.gather_flush()
            if self.upto >= 3:
                self.rA0.seed(self.xfm[0].for_write())
                for s in (self.kh + self.vh + self.qh + self.ctxs +
                          [self.gst, self.gstB]):
                    s.seed(self.xfm[1].for_write())
                self.probsx.seed(self.stg[2].for_write())
                self.p3_attention("a")
                if self.upto >= 2:
                    self.gather_flush()
                if "ctxd" in self.tap_aps:
                    self.tap_dram(self.tap_aps["ctxd"], self.ctxd, list(self.s_stc))
                if "adbg" in self.taps:
                    E = self.E
                    gates = [Pt(s, s.n) for s in (self.s_dv, self.s_ac, self.s_pe,
                                                  self.s_gp, self.s_sc)]
                    E.emit("sy", lambda e, o=self.dbg_pbT, i=self.g_bf(GA0, 32768):
                           e.dma_start(out=o, in_=i), waits=gates,
                           inc=self.s_sy, inck=16)
                    E.emit("sy", lambda e, o=self.dbg_probs, i=self.g_bf(GD, 8192):
                           e.dma_start(out=o, in_=i), waits=gates,
                           inc=self.s_sy, inck=16)
                    E.emit("sy", lambda e, o=self.dbg_small, i=self.g_f32(GH, 1024):
                           e.dma_start(out=o, in_=i), waits=gates,
                           inc=self.s_sy, inck=16)
            if self.upto >= 4:
                self.p4_outproj(self.ow, bias=True, xsrc=self.xq, dst=self.xd)
            if self.upto >= 5:
                self.p5_ffn(dst=self.xd)
            if self.upto >= 6:
                self.p6_attn_b()
            if self.upto >= 7:
                self.p5_ffn(dst=self.y)
            self.finalize()

            blk = stack.enter_context(nc.Block())
            E.replay(blk)
        return nc


    def p0_setup(self):
        E = self.E
        g = self.g_f32
        for dst_col, n, src in ((H_QKVB, 96, self.qkvbT), (H_OBT, 32, self.obT),
                                (H_FBT, 32, self.fbT)):
            pt = E.emit("sy", lambda e, o=g(dst_col, n), i=src:
                        e.dma_start(out=o, in_=i), inc=self.s_setup, inck=16)
        idap = self.g_bf(H_ID, 128)
        p1 = E.emit("gp", lambda e, o=idap: e.iota(
            o, pattern=[[1, 128]], base=0, channel_multiplier=-1,
            allow_small_or_imprecise_dtypes=True), inc=self.s_gp)
        p2 = E.emit("gp", lambda e, o=idap: e.tensor_single_scalar(
            o, o, 0.0, ALU.is_equal), waits=[p1], inc=self.s_gp)
        self.identity.wrote(p2)
        p3 = E.emit("sy", lambda e, o=self.g_i32(H_LSI_I, 4),
                    i=self.lsi.unsqueeze(0).broadcast_to([128, 4]):
                    e.dma_start(out=o, in_=i), inc=self.s_setup, inck=16)
        p4 = E.emit("sy", lambda e, o=g(H_BVAL, 1),
                    i=self.bval.broadcast_to([128, 1]):
                    e.dma_start(out=o, in_=i), inc=self.s_setup, inck=16)
        setup_pt = Pt(self.s_setup, self.s_setup.n)
        self.biases.wrote(setup_pt)
        p4 = setup_pt
        p5 = E.emit("dv", lambda e, o=g(H_LSI_F, 4), i=self.g_i32(H_LSI_I, 4):
                    e.tensor_copy(o, i), waits=[setup_pt], inc=self.s_dv)
        bv = g(H_BVAL, 1)
        lf = g(H_LSI_F, 4)

        def tcol(k):
            return g(H_TMP + 2 * k, 1)

        def tt(o, a, b, op, w):
            return E.emit("dv", lambda e, o=o, a=a, b=b, op=op:
                          e.tensor_tensor(o, a, b, op), waits=w, inc=self.s_dv)

        def ts2(o, a, s1, s2, w):
            return E.emit("dv", lambda e, o=o, a=a, s1=s1, s2=s2:
                          e.tensor_scalar(o, a, s1, s2, ALU.mult, ALU.add),
                          waits=w, inc=self.s_dv)

        c = [tt(tcol(k), bv, lf[:, k:k + 1], ALU.is_lt, [p4, p5]) for k in range(4)]
        pu = ts2(tcol(4), tcol(0), -1.0, 1.0, [c[0]])
        pv1 = tt(tcol(5), tcol(1), tcol(4), ALU.mult, [c[1], pu])
        pw = ts2(tcol(6), tcol(1), -2.0, 2.0, [c[1]])
        pv2 = tt(tcol(7), tcol(5), tcol(6), ALU.add, [pv1, pw])
        pinner = tt(tcol(4), tcol(2), tcol(7), ALU.mult, [c[2], pv2])
        pr2 = ts2(tcol(6), tcol(3), -1.0, 4.0, [c[3]])
        pnc2 = ts2(tcol(5), tcol(2), -1.0, 1.0, [c[2]])
        po2 = tt(tcol(6), tcol(5), tcol(6), ALU.mult, [pr2, pnc2])
        pres = tt(tcol(7), tcol(4), tcol(6), ALU.add, [pinner, po2])
        hs = None
        for a in range(3):
            hs = E.emit("dv", lambda e, o=g(H_HSEL + 2 * a, 1), i=tcol(7), v=float(2 + a):
                        e.tensor_single_scalar(o, i, v, ALU.is_equal),
                        waits=[pres], inc=self.s_dv)
        self.lora["hsel"].wrote(hs)

    def ln_pass(self, src, row0, ntiles, xfm_col, xfm_slot, gate=()):
        E = self.E
        g = self.g_f32
        last_evac = None
        half0 = None
        first = True
        for j in range(ntiles):
            par = j % 2
            xt = self.xt[par]
            xtap = g(GC + par * 8192, 4096)
            lns = self.lns[par]
            r0 = row0 + j * 128
            pl = E.emit("sy" if par == 0 else "gp",
                        lambda e, o=xtap, i=src[r0:r0 + 128, :]:
                        e.dma_start(out=o, in_=i),
                        waits=xt.for_write() + (gate or []),
                        inc=self.s_xt[par], inck=16)
            xt.wrote(pl)
            lnt = self.lnt[par]
            lntap = self.g_bf(GD + par * 4096, 4096)
            c_sq, c_s1, c_nm, c_m2, c_v, c_lnv, c_rstd = (
                g(lncol(LN_SUMSQ, par), 1), g(lncol(LN_S1, par), 1),
                g(lncol(LN_NEGM, par), 1), g(lncol(LN_M2, par), 1),
                g(lncol(LN_VAR, par), 1), g(lncol(LN_LNV, par), 1),
                g(lncol(LN_RSTD, par), 1))
            psq = E.emit("ac", lambda e, o=lntap, i=xtap, a=c_sq:
                         e.activation(o, i, AF.Square, bias=0.0, scale=1.0, accum_out=a),
                         waits=[xt.w] + lnt.for_write() + lns.for_write(), inc=self.s_ac)
            ps1 = E.emit("dv", lambda e, o=c_s1, i=xtap:
                         e.reduce_sum(o, i, axis=AX.X), waits=[xt.w], inc=self.s_dv)
            pnm = E.emit("dv", lambda e, o=c_nm, i=c_s1:
                         e.tensor_scalar_mul(o, i, -1.0 / 4096.0),
                         waits=[ps1], inc=self.s_dv)
            pm2 = E.emit("dv", lambda e, o=c_m2, i=c_nm:
                         e.tensor_scalar(o, i, i, -1e-5, ALU.mult, ALU.add),
                         waits=[pnm], inc=self.s_dv)
            pvar = E.emit("dv", lambda e, o=c_v, a=c_sq, b=c_m2:
                          e.scalar_tensor_tensor(o, a, 1.0 / 4096.0, b,
                                                 ALU.mult, ALU.subtract),
                          waits=[psq, pm2], inc=self.s_dv)
            plnv = E.emit("ac", lambda e, o=c_lnv, i=c_v:
                          e.activation(o, i, AF.Ln, bias=0.0, scale=1.0),
                          waits=[pvar], inc=self.s_ac)
            prstd = E.emit("ac", lambda e, o=c_rstd, i=c_lnv:
                           e.activation(o, i, AF.Exp, bias=0.0, scale=-0.5),
                           waits=[plnv], inc=self.s_ac)
            pnorm = E.emit("dv", lambda e, o=lntap, i=xtap, s1=c_nm, s2=c_rstd:
                           e.tensor_scalar(o, i, s1, s2, ALU.add, ALU.mult),
                           waits=[prstd, pnm, xt.w, psq], inc=self.s_dv)
            lnt.wrote(pnorm)
            lns.read(pnorm)
            xt.read(pnorm)
            for bi in range(4):
                pc = self.psC[bi % 2]
                pcap = self.arena_bf[:, 4096 + (bi % 2) * 1024:
                                     4096 + (bi % 2) * 1024 + 1024]
                gpt = None
                for jj in range(8):
                    dc = bi * 8 + jj
                    w = ([pnorm, self.identity.w] + pc.for_write()) if jj == 0 else []
                    gpt = E.emit("pe", lambda e, o=pcap[:, jj * 128:jj * 128 + 128],
                                 i=lntap[:, dc * 128:dc * 128 + 128],
                                 idn=self.g_bf(H_ID, 128):
                                 e.transpose(o, i, idn),
                                 waits=w, inc=self.s_pe if jj == 7 else None)
                pc.wrote(gpt)
                lnt.read(gpt)
                base = xfm_col + (bi * 8) * 1024 + j * 128
                dst = self.grand[:, base: base + 8 * 1024].rearrange(
                    "p (c t) -> p c t", c=8)[:, :, 0:128]
                pev = E.emit("dv", lambda e, o=dst,
                             i=pcap.rearrange("p (c t) -> p c t", c=8):
                             e.tensor_copy(o, i),
                             waits=[gpt] + (xfm_slot.for_write() if first else []),
                             inc=self.s_dv)
                first = False
                pc.read(pev)
                last_evac = pev
            if j == ntiles // 2 - 1:
                half0 = last_evac
        xfm_slot.wrote(last_evac)
        return [half0, last_evac]

    def proj_fm(self, wdram, wchunk0, nchunks, bias_fn, scale, xfm_col, xfm_slot, sink,
                lora=False, xfm_halves=None):
        E = self.E
        for oc in range(nchunks):
            ws = self.wst[oc % 2]
            wsap = self.g_bf(GE + (oc % 2) * 4096, 4096)
            pw = E.emit("gp", lambda e, o=wsap, i=wdram[wchunk0 + oc]:
                        e.dma_start(out=o, in_=i),
                        waits=ws.for_write(), inc=self.s_w[oc % 2], inck=16)
            ws.wrote(pw)
            ps = self.psA if (oc % 2 == 0) else self.psB
            psap = self.arena[:, (oc % 2) * 1024:(oc % 2) * 1024 + 1024]
            gpt = None
            for dc in range(32):
                for sp in range(2):
                    if dc == 0 and sp == 0:
                        w = [pw, xfm_halves[0] if xfm_halves else xfm_slot.w] \
                            + ps.for_write()
                    elif dc == 0 and sp == 1 and xfm_halves:
                        w = [xfm_halves[1]]
                    else:
                        w = []
                    is_last = (dc == 31 and sp == 1) and not lora
                    gpt = E.emit("pe", lambda e, o=psap[:, sp * 512:sp * 512 + 512],
                                 l=wsap[:, dc * 128:dc * 128 + 128],
                                 r=self.grand[:, xfm_col + dc * 1024 + sp * 512:
                                              xfm_col + dc * 1024 + sp * 512 + 512],
                                 st=(dc == 0), sp_=is_last:
                                 e.matmul(o, l, r, start=st, stop=sp_),
                                 waits=w, inc=self.s_pe if is_last else None)
            if lora:
                for sp in range(2):
                    is_last = sp == 1
                    gpt = E.emit("pe", lambda e, o=psap[:, sp * 512:sp * 512 + 512],
                                 l=self.g_bf(D_BHT + oc * 128, 128, parts=16),
                                 r=self.g_bf(D_ZSB + sp * 512, 512, parts=16),
                                 sp_=is_last:
                                 e.matmul(o, l, r, start=False, stop=sp_),
                                 waits=([self.lora["BhT"].w, self.lora["zsb"].w]
                                        if sp == 0 else []),
                                 inc=self.s_pe if is_last else None)
            ps.wrote(gpt)
            ws.read(gpt)
            xfm_slot.read(gpt)
            if lora:
                self.lora["BhT"].read(gpt)
                self.lora["zsb"].read(gpt)
            st = self.stg[oc % 2]
            stap = self.g_bf(ST0 + (oc % 2) * 1024, 1024)
            bap = bias_fn(oc) if bias_fn is not None else None
            if bap is not None and scale is not None:
                pev = E.emit("dv", lambda e, o=stap, i=psap, b=bap, s=scale:
                             e.tensor_scalar(o, i, b, s, ALU.add, ALU.mult),
                             waits=[gpt, self.biases.w] + st.for_write(), inc=self.s_dv)
            elif bap is not None:
                pev = E.emit("dv", lambda e, o=stap, i=psap, b=bap:
                             e.tensor_scalar(o, i, b, None, ALU.add),
                             waits=[gpt, self.biases.w] + st.for_write(), inc=self.s_dv)
            elif scale is not None:
                pev = E.emit("dv", lambda e, o=stap, i=psap, s=scale:
                             e.tensor_scalar_mul(o, i, s),
                             waits=[gpt] + st.for_write(), inc=self.s_dv)
            else:
                pev = E.emit("dv", lambda e, o=stap, i=psap: e.tensor_copy(o, i),
                             waits=[gpt] + st.for_write(), inc=self.s_dv)
            ps.read(pev)
            st.wrote(pev)
            sink(oc, stap, st, pev)

    def qkvb_col(self, chunk0):
        def f(oc):
            return self.g_f32(H_QKVB, 96)[:, chunk0 + oc:chunk0 + oc + 1]
        return f

    def p1_attn_a_proj(self):
        E = self.E
        xfm_slot = self.xfm[0]
        halves = self.ln_pass(self.xq, 0, 8, GA0, xfm_slot)

        def kvsink(dst):
            def sink(oc, stap, st, pev):
                p = E.emit("ac", lambda e,
                           o=dst[oc * 128:(oc + 1) * 128, :],
                           i=stap: e.dma_start(out=o, in_=i),
                           waits=[pev], inc=self.s_st[oc % 2], inck=16)
                st.read(p)
            return sink

        for kv, base in (("K", 32), ("V", 64)):
            for hf in range(2):
                self.proj_fm(self.qkvw, base + hf * 16, 16,
                             self.qkvb_col(base + hf * 16), None,
                             GA0, xfm_slot, kvsink(self.kvin[kv, hf]),
                             xfm_halves=halves)
                kv_done = [Pt(s, s.n) for s in self.s_st]
                self.ag_pts[kv, hf] = E.emit(
                    "gp", lambda e, i=self.kvin[kv, hf], o=self.kvag[kv, hf]:
                    e.collective_compute(
                        "AllGather", ALU.bypass,
                        replica_groups=[[0, 1, 2, 3], [4, 5, 6, 7]],
                        ins=[i.opt()], outs=[o.opt()]),
                    waits=kv_done, inc=self.s_ag, inck=1)

        def qsink(oc, stap, st, pev):
            p = E.emit("ac", lambda e, o=self.qfmd[oc * 128:(oc + 1) * 128, :],
                       i=stap: e.dma_start(out=o, in_=i),
                       waits=[pev], inc=self.s_st[oc % 2], inck=16)
            st.read(p)
            self.qst_pts[oc].append(p)
        self.proj_fm(self.qkvw, 0, 32, self.qkvb_col(0), SCALE,
                     GA0, xfm_slot, qsink, xfm_halves=halves)
        for t in ("qfmd",):
            if t in self.tap_aps:
                self.tap_dram(self.tap_aps[t], getattr(self, t), list(self.s_st))

    def tap_dram(self, dst, src, gate_sem):
        sems = gate_sem if isinstance(gate_sem, list) else [gate_sem]
        self.E.emit("sy", lambda e, o=dst, i=src: e.dma_start(out=o, in_=i),
                    waits=[Pt(s, s.n) for s in sems], inc=self.s_sy, inck=16)

    def p2_gather(self):
        E = self.E
        pb = E.emit("gp", lambda e, o=self.g_i32(H_BTAB, 2),
                    i=self.btab.rearrange("(a p) -> p a", p=128):
                    e.dma_start(out=o, in_=i, allow_slow_non_contiguous=True),
                    inc=self.s_gd, inck=16)
        self.btsb.wrote(pb)
        steps = []
        for src_, dst in ((self.kcache, self.kgd), (self.vcache, self.vgd)):
            for h in range(8):
                for half in range(2):
                    steps.append((src_, dst, h, half))

        def make_step(idx):
            src_, dst, h, half = steps[idx]
            par = idx % 2
            gslot = self.gst if par == 0 else self.gstB
            gsem = self.s_gd if par == 0 else self.s_gd2
            gstap = self.g_f32(A1_GST if par == 0 else A1_GSTB, 2048)

            def emit():
                pg = E.emit("gp", lambda e, o=gstap, i=src_, eo=h * 2048,
                            ix=self.g_i32(H_BTAB, 2)[:, half:half + 1]:
                            e.indirect_dma_start(o, None, i,
                                                 IndirectOffsetOnAxis(ix, 0),
                                                 element_offset=eo),
                            waits=[pb] + gslot.for_write(), inc=gsem, inck=16)
                gslot.wrote(pg)
                ps = E.emit("gp", lambda e,
                            o=dst[h, half * 2048:(half + 1) * 2048, :]
                            .rearrange("(p r) d -> p r d", p=128),
                            i=gstap.rearrange("p (r d) -> p r d", r=16):
                            e.dma_start(out=o, in_=i),
                            waits=[pg], inc=gsem, inck=16)
                gslot.read(ps)
            return emit

        self._gather_steps = [make_step(i) for i in range(len(steps))]

    def gather_tick(self, n=2):
        for _ in range(n):
            if self._gather_steps:
                self._gather_steps.pop(0)()

    def gather_flush(self):
        while self._gather_steps:
            self._gather_steps.pop(0)()
        self.gather_gate = [Pt(self.s_gd, self.s_gd.n),
                            Pt(self.s_gd2, self.s_gd2.n)]
        for t in ("kgd", "vgd"):
            if t in self.tap_aps:
                self.tap_dram(self.tap_aps[t], getattr(self, t), self.s_gd)

    def p3_attention(self, mode):
        E = self.E
        heads = [None] * NH

        def inplace_tm_fm_build(S, Sap, gate_pt):
            for bi in range(4):
                pc = self.psC[bi % 2]
                pcap = self.arena_bf[:, 4096 + (bi % 2) * 1024:
                                     4096 + (bi % 2) * 1024 + 1024]
                gpt = None
                for jj in range(8):
                    cc = bi * 8 + jj
                    w = ([gate_pt, self.identity.w] + pc.for_write()) if jj == 0 else []
                    gpt = E.emit("pe",
                                 lambda e, o=pcap[:, jj * 128:jj * 128 + 128],
                                 i=Sap[:, cc * 128:cc * 128 + 128],
                                 idn=self.g_bf(H_ID, 128):
                                 e.transpose(o, i, idn), waits=w,
                                 inc=self.s_pe if jj == 7 else None)
                pc.wrote(gpt)
                S.read(gpt)
                pev = E.emit("dv", lambda e,
                             o=Sap[:, bi * 1024:bi * 1024 + 1024],
                             i=pcap: e.tensor_copy(o, i),
                             waits=[gpt], inc=self.s_dv)
                pc.read(pev)
                if bi == 0:
                    S.wrote(pev)
                else:
                    S.add_write(pev)

        pending_builds = []

        def load_head(h):
            if mode == "a":
                K, Kap = self.kh[h % 2], self.g_bf(A1_K0 + (h % 2) * 4096, 4096)
                V, Vap = self.vh[h % 2], self.g_bf(A1_V0 + (h % 2) * 4096, 4096)
                agvK = self.kvag["K", h // 16].rearrange(
                    "(j c p) t -> c p j t", j=4, p=128)
                agvV = self.kvag["V", h // 16].rearrange(
                    "(j c p) t -> c p j t", j=4, p=128)
                pk = E.emit("sy", lambda e, o=Kap.rearrange("p (j t) -> p j t", j=4),
                            i=agvK[h % 16]:
                            e.dma_start(out=o, in_=i),
                            waits=K.for_write() + [self.ag_pts["K", h // 16]],
                            inc=self.s_k[h % 2], inck=16)
                K.wrote(pk)
                pv = E.emit("gp", lambda e, o=Vap.rearrange("p (j t) -> p j t", j=4),
                            i=agvV[h % 16]:
                            e.dma_start(out=o, in_=i),
                            waits=V.for_write() + [self.ag_pts["V", h // 16]],
                            inc=self.s_v[h % 2], inck=16)
                V.wrote(pv)
                pending_builds.append(
                    lambda V=V, Vap=Vap, pv=pv: inplace_tm_fm_build(V, Vap, pv))
            else:
                kvh = h // 4
                K, Kap = self.kh[kvh % 2], self.g_bf(A1_K0 + (kvh % 2) * 4096, 4096)
                V, Vap = self.vh[kvh % 2], self.g_bf(A1_V0 + (kvh % 2) * 4096, 4096)
                if h % 4 == 0:
                    pk = E.emit("sy", lambda e,
                                o=Kap.rearrange("p (c d) -> p c d", c=32),
                                i=self.kgd[kvh].rearrange("(c p) d -> p c d", p=128):
                                e.dma_start(out=o, in_=i),
                                waits=K.for_write() + self.gather_gate,
                                inc=self.s_k[kvh % 2], inck=16)
                    K.wrote(pk)
                    pending_builds.append(
                        lambda K=K, Kap=Kap, pk=pk: inplace_tm_fm_build(K, Kap, pk))
                    pv = E.emit("sy", lambda e,
                                o=Vap.rearrange("p (c d) -> p c d", c=32),
                                i=self.vgd[kvh].rearrange("(c p) d -> p c d", p=128):
                                e.dma_start(out=o, in_=i),
                                waits=V.for_write() + self.gather_gate,
                                inc=self.s_v[kvh % 2], inck=16)
                    V.wrote(pv)
            Q, Qap = self.qh[h % 2], self.g_bf(A1_Q0 + (h % 2) * 1024, 1024)
            pq = E.emit("sy", lambda e, o=Qap, i=self.qfmd[h * 128:(h + 1) * 128, :]:
                        e.dma_start(out=o, in_=i),
                        waits=Q.for_write() + self.qst_pts[h],
                        inc=self.s_q[h % 2], inck=16)
            Q.wrote(pq)
            heads[h] = (K, Kap, V, Vap, Q, Qap)

        def flush_builds():
            while pending_builds:
                pending_builds.pop(0)()

        def emit_ctx(h):
            K, Kap, V, Vap, Q, Qap = heads[h]
            ctxap = self.arena[:, 3072:4096]
            gpt = None
            for c in range(32):
                for sp in range(2):
                    w = (self.rA0.rw() + V.rw() + self.psD.for_write()) \
                        if (c == 0 and sp == 0) else []
                    is_last = c == 31 and sp == 1
                    gpt = E.emit("pe", lambda e, o=ctxap[:, sp * 512:sp * 512 + 512],
                                 l=Vap[:, c * 128:c * 128 + 128],
                                 r=self.grand[:, GA0 + c * 1024 + sp * 512:
                                              GA0 + c * 1024 + sp * 512 + 512],
                                 st=(c == 0), spp=(c == 31):
                                 e.matmul(o, l, r, start=st, stop=spp),
                                 waits=w, inc=self.s_pe if is_last else None)
            self.psD.wrote(gpt)
            self.rA0.read(gpt)
            V.read(gpt)
            cs = self.ctxs[h % 2]
            csap = self.g_bf(A1_CTXS0 + (h % 2) * 1024, 1024)
            pev = E.emit("dv", lambda e, o=csap, i=ctxap: e.tensor_copy(o, i),
                         waits=[gpt] + cs.for_write(), inc=self.s_dv)
            self.psD.read(pev)
            cs.wrote(pev)
            pst = E.emit("ac", lambda e, o=self.ctxd[h * 128:(h + 1) * 128, :], i=csap:
                         e.dma_start(out=o, in_=i), waits=[pev],
                         inc=self.s_stc[h % 2], inck=16)
            cs.read(pst)

        NU = NH * 8
        unit_state = [None] * NU
        for gu in range(NU + 2):
            h, i = divmod(gu, 8)
            if gu == 0:
                load_head(0)
                load_head(1)
                if mode == "b":
                    flush_builds()
            if gu < NU:
                self.unit_quarters(gu, 0, 2, heads[h], unit_state)
            if gu >= 1 and gu - 1 < NU:
                self.unit_norm(gu - 1, unit_state)
            if gu >= 2:
                self.unit_T(gu - 2, unit_state)
                if (gu - 2) % 8 == 3:
                    flush_builds()
                if (gu - 2) % 8 == 7:
                    hh = (gu - 2) // 8
                    emit_ctx(hh)
                    if hh + 2 < NH:
                        load_head(hh + 2)
                    if mode == "a":
                        self.gather_tick(2)
            if gu < NU:
                self.unit_quarters(gu, 2, 4, heads[h], unit_state)

    def probs3(self, par):
        slots = [self.lnt[0], self.lnt[1], self.probsx]
        cols = [GD, GD + 4096, GD2]
        return slots[par], self.g_bf(cols[par], 4096)

    def unit_quarters(self, gu, k0, k1, head, unit_state):
        E = self.E
        g = self.g_f32
        K, Kap, V, Vap, Q, Qap = head
        i = gu % 8
        par = gu % 3
        if unit_state[gu] is None:
            unit_state[gu] = {"par": par}
        stt = unit_state[gu]
        probs, probsap = self.probs3(par)
        stt["probs"], stt["probsap"] = probs, probsap
        nm_ap = g(H_ANM + 2 * par, 1)
        sums_ap = g(H_ASUM + 8 * par, 4)
        stt["nm_ap"], stt["sums_ap"] = nm_ap, sums_ap
        for k in range(k0, k1):
            ps = self.psA if k % 2 == 0 else self.psB
            psap = self.arena[:, (k % 2) * 1024:(k % 2) * 1024 + 1024]
            gpt = None
            for sp in range(2):
                w = ([K.w, Q.w] + ps.for_write()) if sp == 0 else []
                gpt = E.emit("pe", lambda e, o=psap[:, sp * 512:sp * 512 + 512],
                             l=Qap[:, i * 128:i * 128 + 128],
                             r=Kap[:, k * 1024 + sp * 512:k * 1024 + sp * 512 + 512]:
                             e.matmul(o, l, r, start=True, stop=True),
                             waits=w, inc=self.s_pe if sp == 1 else None)
            ps.wrote(gpt)
            K.read(gpt)
            Q.read(gpt)
            if k == 0:
                nm = self.att_nm[par]
                pmx = E.emit("dv", lambda e, o=nm_ap, i=self.arena[:, 0:1024:8]:
                             e.reduce_max(o, i, axis=AX.X, negate=True),
                             waits=[gpt] + nm.for_write(), inc=self.s_dv)
                nm.wrote(pmx)
                stt["pmx"] = pmx
            sums = self.att_sums[par]
            pex = E.emit("ac", lambda e, o=probsap[:, k * 1024:k * 1024 + 1024],
                         i=psap, b=nm_ap, a=sums_ap[:, k:k + 1]:
                         e.activation(o, i, AF.Exp, bias=b, scale=1.0, accum_out=a),
                         waits=[stt["pmx"], gpt] +
                               (probs.for_write() if k == 0 else []) +
                               (sums.for_write() if k == 0 else []),
                         inc=self.s_ac)
            ps.read(pex)
            if k == 0:
                probs.wrote(pex)
                sums.wrote(pex)
            else:
                probs.add_write(pex)
                sums.add_write(pex)
            if k == 3:
                stt["exp_last"] = pex
                self.att_nm[par].read(pex)

    def unit_norm(self, gu, unit_state):
        E = self.E
        g = self.g_f32
        stt = unit_state[gu]
        par = stt["par"]
        probs, probsap = stt["probs"], stt["probsap"]
        ss = self.att_ss[par]
        pss = E.emit("dv", lambda e, o=g(H_ASS + 2 * par, 1), i=stt["sums_ap"]:
                     e.reduce_sum(o, i, axis=AX.X),
                     waits=[stt["exp_last"]] + ss.for_write(), inc=self.s_dv)
        prc = E.emit("dv", lambda e, o=g(H_ARC + 2 * par, 1), i=g(H_ASS + 2 * par, 1):
                     e.reciprocal(o, i), waits=[pss], inc=self.s_dv)
        ss.wrote(prc)
        self.att_sums[par].read(pss)
        pr = E.emit("dv", lambda e, o=probsap, s=g(H_ARC + 2 * par, 1):
                    e.tensor_scalar_mul(o, o, s),
                    waits=[prc, probs.w], inc=self.s_dv)
        probs.wrote(pr)
        ss.read(pr)
        stt["pr"] = pr

    def unit_T(self, gu, unit_state):
        E = self.E
        stt = unit_state[gu]
        i = gu % 8
        probs, probsap = stt["probs"], stt["probsap"]
        pr = stt["pr"]
        for bi in range(4):
            pc = self.psC[bi % 2]
            pcap = self.arena_bf[:, 4096 + (bi % 2) * 1024:
                                 4096 + (bi % 2) * 1024 + 1024]
            gpt = None
            for jj in range(8):
                cc = bi * 8 + jj
                w = ([pr, self.identity.w] + pc.for_write()) if jj == 0 else []
                gpt = E.emit("pe", lambda e, o=pcap[:, jj * 128:jj * 128 + 128],
                             i=probsap[:, cc * 128:cc * 128 + 128],
                             dk=self.g_bf(H_ID, 128):
                             e.transpose(o, i, dk), waits=w,
                             inc=self.s_pe if jj == 7 else None)
            pc.wrote(gpt)
            probs.read(gpt)
            base = GA0 + (bi * 8) * 1024 + i * 128
            dst = self.grand[:, base: base + 8 * 1024].rearrange(
                "p (c t) -> p c t", c=8)[:, :, 0:128]
            pev = E.emit("dv",
                         lambda e, o=dst, i=pcap.rearrange("p (c t) -> p c t", c=8):
                         e.tensor_copy(o, i),
                         waits=[gpt] + (self.rA0.for_write() if (i == 0 and bi == 0)
                                        else []),
                         inc=self.s_dv)
            pc.read(pev)
            if i == 0 and bi == 0:
                self.rA0.wrote(pev)
            else:
                self.rA0.add_write(pev)

    def p4_outproj(self, wdram, bias, xsrc, dst):
        E = self.E
        pl = E.emit("sy", lambda e,
                    o=self.grand[:, GA0:GA0 + 32768].rearrange("p (c t) -> p c t", c=32),
                    i=self.ctxd.rearrange("(c p) t -> p c t", p=128):
                    e.dma_start(out=o, in_=i),
                    waits=self.rA0.for_write() + [Pt(s, s.n) for s in self.s_stc],
                    inc=self.s_cxl, inck=16)
        self.rA0.wrote(pl)
        bias_fn = (lambda oc: self.g_f32(H_OBT, 32)[:, oc:oc + 1]) if bias else None
        guards = self.head_guards() + self.rA1.for_write()

        def osink(oc, stap, st, pev):
            p = E.emit("dv", lambda e, o=self.g_bf(GA1 + oc * 1024, 1024), i=stap:
                       e.tensor_copy(o, i),
                       waits=[pev] + (guards if oc == 0 else []),
                       inc=self.s_dv)
            st.read(p)
            self.rA1.wrote(p)

        self.proj_fm(wdram, 0, 32, bias_fn, None, GA0, self.rA0, osink)
        self.add_pass(xsrc, dst)

    def add_pass(self, xsrc, dst):
        E = self.E
        g = self.g_f32
        for j in range(8):
            par = j % 2
            xt = self.xt[par]
            xtap = g(GC + par * 8192, 4096)
            pl = E.emit("sy" if par == 0 else "gp",
                        lambda e, o=xtap, i=xsrc[j * 128:(j + 1) * 128, :]:
                        e.dma_start(out=o, in_=i),
                        waits=xt.for_write() + (self.last_store or []),
                        inc=self.s_xt[par], inck=16)
            xt.wrote(pl)
            addpts = []
            for half in range(2):
                if half == 0:
                    slots = [self.psC[0], self.psC[1]]
                else:
                    slots = [self.psD]
                pap = self.arena_bf[:, (4 + half * 2) * 1024:
                                    (4 + half * 2) * 1024 + 2048]
                gpt = None
                for ocl in range(16):
                    oc = half * 16 + ocl
                    if ocl == 0:
                        w = [self.rA1.w, self.identity.w]
                        for s in slots:
                            w += s.for_write()
                    else:
                        w = []
                    gpt = E.emit("pe", lambda e, o=pap[:, ocl * 128:ocl * 128 + 128],
                                 i=self.g_bf(GA1 + oc * 1024 + j * 128, 128),
                                 idn=self.g_bf(H_ID, 128):
                                 e.transpose(o, i, idn), waits=w,
                                 inc=self.s_pe if ocl == 15 else None)
                for s in slots:
                    s.wrote(gpt)
                self.rA1.read(gpt)
                padd = E.emit("dv", lambda e, o=xtap[:, half * 2048:half * 2048 + 2048],
                              i=pap: e.tensor_tensor(o, o, i, ALU.add),
                              waits=[gpt, xt.w], inc=self.s_dv)
                for s in slots:
                    s.read(padd)
                addpts.append(padd)
            pst = E.emit("ac", lambda e, o=dst[j * 128:(j + 1) * 128, :], i=xtap:
                         e.dma_start(out=o, in_=i), waits=addpts,
                         inc=self.s_stx[par], inck=16)
            xt.read(pst)
        self.last_store = [Pt(s, s.n) for s in self.s_stx]

    def p5_ffn(self, dst):
        E = self.E
        g = self.g_f32
        halves = self.ln_pass(self.xd, 0, 8, GA0, self.rA0, gate=self.last_store)
        hw = self.lora["hsel"].w
        for a in range(3):
            E.emit("gp", lambda e, o=self.g_bf(GA1 + a * 4096, 4096, parts=16),
                   i=self.loraA[a]:
                   e.dma_start(out=o, in_=i),
                   waits=(self.rA1.for_write() + self.head_guards())
                   if a == 0 else [],
                   inc=self.s_lor, inck=16)
        for a in range(3):
            E.emit("gp", lambda e,
                   o=self.g_bf(GA1 + 16384 + a * 512, 512).rearrange("p (c r) -> p c r", c=32),
                   i=self.loraB[a].rearrange("(c p) r -> p c r", p=128):
                   e.dma_start(out=o, in_=i), waits=[],
                   inc=self.s_lor, inck=16)
        lor_pt = Pt(self.s_lor, self.s_lor.n)
        apts = [lor_pt, lor_pt, lor_pt]
        ah = self.g_bf(GA1 + 3 * 4096, 4096, parts=16)
        p = E.emit("dv", lambda e, o=ah, i=self.g_bf(GA1 + 0, 4096, parts=16),
                   s=g(H_HSEL, 3, parts=16)[:, 0:1]:
                   e.tensor_scalar_mul(o, i, s), waits=[apts[0], hw], inc=self.s_dv)
        for a in (1, 2):
            p = E.emit("dv", lambda e, o=ah, i=self.g_bf(GA1 + a * 4096, 4096, parts=16),
                       s=g(H_HSEL, 3, parts=16)[:, a:a + 1]:
                       e.scalar_tensor_tensor(o, i, s, o, ALU.mult, ALU.add),
                       waits=[apts[a], p], inc=self.s_dv)
        pah = p
        self.rA1.wrote(pah)
        for bi in range(4):
            pc = self.psC[bi % 2]
            pcap = self.arena_bf[:, 4096 + (bi % 2) * 1024:
                                 4096 + (bi % 2) * 1024 + 1024]
            gpt = None
            for jj in range(8):
                cc = bi * 8 + jj
                w = ([pah, self.identity.w] + pc.for_write()) if jj == 0 else []
                gpt = E.emit("pe", lambda e, o=pcap[:, jj * 16:jj * 16 + 16],
                             i=ah[:, cc * 128:cc * 128 + 128],
                             idn=self.g_bf(H_ID, 128, parts=16)[:, 0:16]:
                             e.transpose(o, i, idn), waits=w,
                             inc=self.s_pe if jj == 7 else None)
            pc.wrote(gpt)
            self.rA1.read(gpt)
            pev = E.emit("dv", lambda e, o=self.g_bf(D_AHT + bi * 128, 128),
                         i=pcap[:, 0:128]: e.tensor_copy(o, i),
                         waits=[gpt] + (self.lora["AhT"].for_write() +
                                        self.lnt[0].for_write() if bi == 0 else []),
                         inc=self.s_dv)
            pc.read(pev)
            self.lora["AhT"].wrote(pev)
        bpts = [lor_pt, lor_pt, lor_pt]
        bh = self.g_bf(GA1 + 16384 + 3 * 512, 512)
        p = E.emit("dv", lambda e, o=bh, i=self.g_bf(GA1 + 16384, 512),
                   s=g(H_HSEL, 3)[:, 0:1]:
                   e.tensor_scalar_mul(o, i, s), waits=[bpts[0], hw], inc=self.s_dv)
        for a in (1, 2):
            p = E.emit("dv", lambda e, o=bh, i=self.g_bf(GA1 + 16384 + a * 512, 512),
                       s=g(H_HSEL, 3)[:, a:a + 1]:
                       e.scalar_tensor_tensor(o, i, s, o, ALU.mult, ALU.add),
                       waits=[bpts[a], p], inc=self.s_dv)
        pbh = p
        self.rA1.wrote(pbh)
        for bi in range(4):
            pc = self.psC[bi % 2]
            pcap = self.arena_bf[0:16, 4096 + (bi % 2) * 1024:
                                 4096 + (bi % 2) * 1024 + 1024]
            gpt = None
            for jj in range(8):
                cc = bi * 8 + jj
                w = ([pbh, self.identity.w] + pc.for_write()) if jj == 0 else []
                gpt = E.emit("pe", lambda e, o=pcap[:, jj * 128:jj * 128 + 128],
                             i=bh[:, cc * 16:cc * 16 + 16],
                             idn=self.g_bf(H_ID, 128):
                             e.transpose(o, i, idn), waits=w,
                             inc=self.s_pe if jj == 7 else None)
            pc.wrote(gpt)
            self.rA1.read(gpt)
            pev = E.emit("dv", lambda e, o=self.g_bf(D_BHT + bi * 1024, 1024, parts=16),
                         i=pcap: e.tensor_copy(o, i),
                         waits=[gpt] + (self.lora["BhT"].for_write() +
                                        self.lnt[1].for_write() if bi == 0 else []),
                         inc=self.s_dv)
            pc.read(pev)
            self.lora["BhT"].wrote(pev)
        zap = self.arena[0:16, 3072:4096]
        gpt = None
        for dc in range(32):
            for sp in range(2):
                w = ([self.lora["AhT"].w, self.rA0.w] + self.psD.for_write()) \
                    if (dc == 0 and sp == 0) else []
                is_last = dc == 31 and sp == 1
                gpt = E.emit("pe", lambda e, o=zap[:, sp * 512:sp * 512 + 512],
                             l=self.g_bf(D_AHT + dc * 16, 16),
                             r=self.grand[:, GA0 + dc * 1024 + sp * 512:
                                          GA0 + dc * 1024 + sp * 512 + 512],
                             st=(dc == 0), spp=is_last:
                             e.matmul(o, l, r, start=st, stop=spp),
                             waits=w, inc=self.s_pe if is_last else None)
        self.psD.wrote(gpt)
        self.lora["AhT"].read(gpt)
        self.rA0.read(gpt)
        pz = E.emit("dv", lambda e, o=self.g_bf(D_ZSB, 1024, parts=16), i=zap:
                    e.tensor_copy(o, i),
                    waits=[gpt] + self.lora["zsb"].for_write(), inc=self.s_dv)
        self.psD.read(pz)
        self.lora["zsb"].wrote(pz)

        def fsink(oc, stap, st, pev):
            p = E.emit("dv", lambda e, o=self.g_bf(GA1 + oc * 1024, 1024), i=stap:
                       e.tensor_copy(o, i),
                       waits=[pev] + (self.rA1.for_write() if oc == 0 else []),
                       inc=self.s_dv)
            st.read(p)
            self.rA1.wrote(p)

        self.proj_fm(self.wft, 0, 32,
                     lambda oc: self.g_f32(H_FBT, 32)[:, oc:oc + 1], None,
                     GA0, self.rA0, fsink, lora=True, xfm_halves=halves)
        self.add_pass(self.xd, dst)

    def p6_attn_b(self):
        E = self.E
        g = self.g_f32
        halves = self.ln_pass(self.xd, 0, 8, GA0, self.rA0, gate=self.last_store)
        E.emit("gp", lambda e, o=self.g_bf(A1_COS, 1024), i=self.cosf:
               e.dma_start(out=o, in_=i),
               waits=self.rA1.for_write() + self.gstB.for_write() +
               [Pt(self.s_gd, self.s_gd.n), Pt(self.s_gd2, self.s_gd2.n)],
               inc=self.s_misc, inck=16)
        E.emit("gp", lambda e, o=self.g_bf(A1_SIN, 1024), i=self.sinf:
               e.dma_start(out=o, in_=i), inc=self.s_misc, inck=16)
        psin = Pt(self.s_misc, self.s_misc.n)
        self.rA1.wrote(psin)
        rope_guard = [psin]

        def qsink(oc, stap, st, pev):
            rot = self.g_bf(A1_ROT, 1024)
            t1 = self.g_bf(A1_RT1, 1024)
            p1 = E.emit("dv", lambda e, o=rot[0:64, :], i=stap[64:128, :]:
                        e.tensor_scalar_mul(o, i, -1.0),
                        waits=[pev] + rope_guard, inc=self.s_dv)
            p2 = E.emit("dv", lambda e, o=rot[64:128, :], i=stap[0:64, :]:
                        e.tensor_copy(o, i), waits=[pev], inc=self.s_dv)
            p3 = E.emit("dv", lambda e, o=t1, a=stap, b=self.g_bf(A1_COS, 1024):
                        e.tensor_tensor(o, a, b, ALU.mult), waits=[pev], inc=self.s_dv)
            p4 = E.emit("dv", lambda e, o=rot, a=rot, b=self.g_bf(A1_SIN, 1024):
                        e.tensor_tensor(o, a, b, ALU.mult), waits=[p1, p2],
                        inc=self.s_dv)
            p5 = E.emit("dv", lambda e, o=stap, a=t1, b=rot:
                        e.tensor_tensor(o, a, b, ALU.add), waits=[p3, p4],
                        inc=self.s_dv)
            st.wrote(p5)
            self.rA1.read(p5)
            p = E.emit("ac", lambda e, o=self.qfmd[oc * 128:(oc + 1) * 128, :], i=stap:
                       e.dma_start(out=o, in_=i), waits=[p5],
                       inc=self.s_st[oc % 2], inck=16)
            st.read(p)
            self.qst_pts[oc] = [p]

        self.proj_fm(self.wqt, 0, 32, None, SCALE, GA0, self.rA0, qsink,
                     xfm_halves=halves)
        if "qfmd" in self.tap_aps:
            self.tap_dram(self.tap_aps["qfmd"], self.qfmd, list(self.s_st))
        for s in (self.kh + self.vh + self.qh + self.ctxs):
            s.seed(self.rA1.for_write())
        self.p3_attention("b")
        if "ctxd" in self.tap_aps:
            self.tap_dram(self.tap_aps["ctxd"], self.ctxd, list(self.s_stc))
        self.p4_outproj(self.wot, bias=False, xsrc=self.xd, dst=self.xd)

    def finalize(self):
        E = self.E
        allsems = ([self.s_gd, self.s_sy, self.s_setup, self.s_cxl, self.s_misc,
                    self.s_lor, self.s_ag] + self.s_xt + self.s_w + self.s_k +
                   self.s_v + self.s_q + self.s_st + self.s_stc + self.s_stx)
        last = []
        if self.upto < 4:
            last = [E.emit("sy", lambda e: e.dma_start(out=self.y, in_=self.xq),
                           inc=self.s_sy, inck=16)]
        elif self.upto < 7:
            last = [E.emit("sy", lambda e: e.dma_start(out=self.y, in_=self.xd),
                           waits=list(self.last_store), inc=self.s_sy, inck=16)]
        else:
            last = list(self.last_store)
        if "xd" in self.tap_aps and self.last_store is not None:
            last = [E.emit("sy", lambda e:
                           e.dma_start(out=self.tap_aps["xd"], in_=self.xd),
                           waits=list(self.last_store), inc=self.s_sy, inck=16)]
        waits = last + [Pt(s, s.n) for s in allsems]
        E.emit("sy", lambda e: e.nop(), waits=waits)


def build_program(upto=7, taps=()):
    return Builder(upto=upto, taps=taps).build()



def _tile_w(W):
    dc = W.reshape(32, 128, -1, 128)
    return np.ascontiguousarray(dc.transpose(2, 1, 0, 3).reshape(-1, 128, 4096))


def _rope_tables():
    dim = 128
    inv_freq = (1.0 / (10000.0 ** (np.arange(0, dim, 2, dtype=np.float32)
                                   / np.float32(dim)))).astype(np.float32)
    t = np.arange(BT, dtype=np.float32)
    freqs = np.outer(t, inv_freq).astype(np.float32)
    emb = np.concatenate([freqs, freqs], axis=-1)
    return np.cos(emb).astype(np.float32), np.sin(emb).astype(np.float32)


_UPTO = 7
_TAPS = ()


def kernel(**inputs):
    x = np.ascontiguousarray(inputs["L0_attn_a_x"])
    qkvw = np.ascontiguousarray(inputs["L0_attn_a_attn_qkvw"])
    qkvb = np.asarray(inputs["L0_attn_a_attn_qkvb"])
    ow = np.ascontiguousarray(inputs["L0_attn_a_attn_ow"])
    ob = np.asarray(inputs["L0_attn_a_attn_ob"])
    kc = np.ascontiguousarray(np.asarray(inputs["L0_attn_b_k_cache"]).reshape(512, 16384))
    vc = np.ascontiguousarray(np.asarray(inputs["L0_attn_b_v_cache"]).reshape(512, 16384))
    wq = np.asarray(inputs["L0_attn_b_Wq"])
    wo = np.asarray(inputs["L0_attn_b_Wo"])
    bt = np.ascontiguousarray(np.asarray(inputs["L0_attn_b_block_table"]).astype(np.int32))
    fw = np.asarray(inputs["L0_ffn_weight"])
    fb = np.asarray(inputs["L0_ffn_bias"])
    lA = np.ascontiguousarray(inputs["L0_ffn_lora_A"])
    lB = np.ascontiguousarray(inputs["L0_ffn_lora_B"])
    lsi = np.ascontiguousarray(np.asarray(inputs["L0_ffn_lora_start_indices"]).astype(np.int32))

    qkvw = _tile_w(qkvw)
    ow = _tile_w(ow)
    wqt = _tile_w(np.ascontiguousarray(wq.T))
    wot = _tile_w(np.ascontiguousarray(wo.T))
    wft = _tile_w(np.ascontiguousarray(fw.T))
    qkvbT = np.ascontiguousarray(qkvb.reshape(96, 128).T)
    obT = np.ascontiguousarray(ob.reshape(32, 128).T)
    fbT = np.ascontiguousarray(fb.reshape(32, 128).T)
    cos_t, sin_t = _rope_tables()

    nc = build_program(upto=_UPTO, taps=_TAPS)
    in_maps = []
    for c in range(NCORES):
        b, qi = c // 4, c % 4
        in_maps.append({
            "xq": np.ascontiguousarray(x[b, qi * NT:(qi + 1) * NT]),
            "qkvw": qkvw, "qkvbT": qkvbT, "ow": ow, "obT": obT,
            "wqt": wqt, "wot": wot, "wft": wft, "fbT": fbT,
            "kcache": kc, "vcache": vc, "btab": bt, "lsi": lsi,
            "bval": np.array([[float(b)]], dtype=np.float32),
            "loraA": lA, "loraB": lB,
            "cosf": np.ascontiguousarray(cos_t[qi * NT:(qi + 1) * NT].T),
            "sinf": np.ascontiguousarray(sin_t[qi * NT:(qi + 1) * NT].T),
        })
    res = run_bass_kernel_spmd(nc, in_maps, core_ids=list(range(NCORES)))
    out = np.empty((2, BT, D), dtype=np.float32)
    for c in range(NCORES):
        b, qi = c // 4, c % 4
        out[b, qi * NT:(qi + 1) * NT] = res.results[c]["y"]
    if _TAPS:
        kernel.tap_results = res.results
    return out


# version 08
# speedup vs baseline: 1.5216x; 1.1084x over previous
import numpy as np

import concourse.bass as bass
import concourse.mybir as mybir
from concourse.bass_utils import run_bass_kernel_spmd
from concourse.bass_core_rust import IndirectOffsetOnAxis

F32 = mybir.dt.float32
BF16 = mybir.dt.bfloat16
I32 = mybir.dt.int32
AF = mybir.ActivationFunctionType
ALU = mybir.AluOpType
AX = mybir.AxisListType

D = 4096
NT = 1024
BT = 4096
NH = 32
HD = 128
SCALE = float(1.0 / np.sqrt(128.0))
NCORES = 8



class Sem:
    __slots__ = ("name", "n", "h")

    def __init__(self, name):
        self.name = name
        self.n = 0
        self.h = None


class Pt:
    __slots__ = ("sem", "val")

    def __init__(self, sem, val):
        self.sem = sem
        self.val = val


class Slot:
    __slots__ = ("ws", "r")

    def __init__(self):
        self.ws = []
        self.r = []

    @property
    def w(self):
        return self.ws[-1] if self.ws else None

    def rw(self):
        return list(self.ws)

    def for_write(self):
        return list(self.r) + list(self.ws)

    def wrote(self, pt):
        assert pt is not None
        self.ws = [pt]
        self.r = []

    def add_write(self, pt):
        assert pt is not None
        self.ws.append(pt)

    def read(self, pt):
        if pt is not None:
            self.r.append(pt)

    def seed(self, pts):
        self.r.extend(p for p in pts if p is not None)


class Emitter:
    ENGINES = ("pe", "dv", "ac", "gp", "sy")

    def __init__(self):
        self.q = {e: [] for e in self.ENGINES}
        self.sems = {}
        self.waited = {e: {} for e in self.ENGINES}

    def S(self, name):
        if name not in self.sems:
            self.sems[name] = Sem(name)
        return self.sems[name]

    def emit(self, eng, fn, waits=(), inc=None, inck=1):
        seen = {}
        for pt in waits:
            if pt is None:
                continue
            key = pt.sem.name
            cur = seen.get(key, self.waited[eng].get(key, -1))
            if pt.val > cur:
                seen[key] = pt.val
        ws = []
        for key, val in seen.items():
            self.waited[eng][key] = max(self.waited[eng].get(key, -1), val)
            ws.append((self.sems[key], val))
        pt_out = None
        if inc is not None:
            inc.n += inck
            pt_out = Pt(inc, inc.n)
        self.q[eng].append((ws, fn, inc, inck))
        return pt_out

    def replay(self, block):
        def run(stream):
            def body(eng):
                for ws, fn, inc, inck in stream:
                    for sem, val in ws:
                        eng.wait_ge(sem.h, val)
                    ins = fn(eng)
                    if inc is not None:
                        ins.then_inc(inc.h, inck)
            return body

        block.tensor(run(self.q["pe"]))
        block.vector(run(self.q["dv"]))
        block.scalar(run(self.q["ac"]))
        block.gpsimd(run(self.q["gp"]))
        block.sync(run(self.q["sy"]))



GCOLS = 105536
GA0 = 0
GA1 = 32768
GC = 65536
GD = 81920
GE = 90112
GH = 98304
ST0 = 99392
ST1 = 100416
GD2 = 101440
ST2 = GD2

A1_K0 = GA1 + 0
A1_K1 = GA1 + 4096
A1_V0 = GA1 + 8192
A1_V1 = GA1 + 12288
A1_Q0 = GA1 + 16384
A1_Q1 = GA1 + 17408
A1_CTXS0 = GA1 + 18432
A1_CTXS1 = GA1 + 19456
A1_GST = GA1 + 20480
A1_GSTB = GA1 + 24576
A1_COS = GA1 + 28672
A1_SIN = GA1 + 29696
A1_ROT = GA1 + 30720
A1_RT1 = GA1 + 31744

D_AHT = GD + 0
D_BHT = GD + 512
D_ZSB = GD + 4608

H_ID = GH + 0
H_D0 = GH + 128
H_QKVB = GH + 640
H_OBT = GH + 832
H_FBT = GH + 896
H_ANM = GH + 960
H_ASUM = GH + 966
H_ASS = GH + 990
H_ARC = GH + 996
H_LN = GH + 1008
H_LSI_I = GH + 1040
H_LSI_F = GH + 1048
H_BVAL = GH + 1056
H_TMP = GH + 1058
H_HSEL = GH + 1074
H_BTAB = GH + 1080

LN_S1, LN_NEGM, LN_M2, LN_VAR, LN_LNV, LN_RSTD, LN_SUMSQ = range(7)


def lncol(stat, par):
    return H_LN + stat * 4 + par * 2


class Builder:
    def __init__(self, upto=7, taps=()):
        self.upto = upto
        self.taps = taps
        self.nc = bass.Bass("TRN2", target_bir_lowering=False)
        nc = self.nc
        self.xq = nc.dram_tensor("xq", [NT, D], F32, kind="ExternalInput").ap()
        self.qkvw = nc.dram_tensor("qkvw", [96, 128, 4096], F32, kind="ExternalInput").ap()
        self.qkvbT = nc.dram_tensor("qkvbT", [128, 96], F32, kind="ExternalInput").ap()
        self.ow = nc.dram_tensor("ow", [32, 128, 4096], F32, kind="ExternalInput").ap()
        self.obT = nc.dram_tensor("obT", [128, 32], F32, kind="ExternalInput").ap()
        self.wqt = nc.dram_tensor("wqt", [32, 128, 4096], F32, kind="ExternalInput").ap()
        self.wot = nc.dram_tensor("wot", [32, 128, 4096], F32, kind="ExternalInput").ap()
        self.wft = nc.dram_tensor("wft", [32, 128, 4096], F32, kind="ExternalInput").ap()
        self.fbT = nc.dram_tensor("fbT", [128, 32], F32, kind="ExternalInput").ap()
        self.kcache = nc.dram_tensor("kcache", [512, 16384], F32, kind="ExternalInput").ap()
        self.vcache = nc.dram_tensor("vcache", [512, 16384], F32, kind="ExternalInput").ap()
        self.btab = nc.dram_tensor("btab", [256], I32, kind="ExternalInput").ap()
        self.lsi = nc.dram_tensor("lsi", [4], I32, kind="ExternalInput").ap()
        self.bval = nc.dram_tensor("bval", [1, 1], F32, kind="ExternalInput").ap()
        self.loraA = nc.dram_tensor("loraA", [3, 16, D], F32, kind="ExternalInput").ap()
        self.loraB = nc.dram_tensor("loraB", [3, D, 16], F32, kind="ExternalInput").ap()
        self.cosf = nc.dram_tensor("cosf", [128, NT], F32, kind="ExternalInput").ap()
        self.sinf = nc.dram_tensor("sinf", [128, NT], F32, kind="ExternalInput").ap()
        self.y = nc.dram_tensor("y", [NT, D], F32, kind="ExternalOutput").ap()
        self.xd = nc.dram_tensor("xd", [NT, D], F32, kind="Internal").ap()
        self.kvin = {}
        self.kvag = {}
        for kv in "KV":
            for hf in range(2):
                self.kvin[kv, hf] = nc.dram_tensor(
                    "kvin%s%d" % (kv, hf), [D // 2, NT], BF16, kind="Internal").ap()
                self.kvag[kv, hf] = nc.dram_tensor(
                    "kvag%s%d" % (kv, hf), [2 * D, NT], BF16, kind="Internal").ap()
        self.ag_pts = {}
        self.qfmd = nc.dram_tensor("qfmd", [D, NT], BF16, kind="Internal").ap()
        self.ctxd = nc.dram_tensor("ctxd", [D, NT], BF16, kind="Internal").ap()
        self.kgd = nc.dram_tensor("kgd", [8, BT, HD], BF16, kind="Internal").ap()
        self.vgd = nc.dram_tensor("vgd", [8, BT, HD], BF16, kind="Internal").ap()

        self.tap_aps = {}
        shapes = {"qfmd": ([D, NT], BF16), "ctxd": ([D, NT], BF16),
                  "kgd": ([8, BT, HD], BF16), "vgd": ([8, BT, HD], BF16),
                  "xd": ([NT, D], F32)}
        for t in taps:
            if t == "adbg":
                self.dbg_pbT = nc.dram_tensor("tap_pbT", [128, 32768], BF16,
                                              kind="ExternalOutput").ap()
                self.dbg_probs = nc.dram_tensor("tap_probs", [128, 8192], BF16,
                                                kind="ExternalOutput").ap()
                self.dbg_small = nc.dram_tensor("tap_small", [128, 1024], F32,
                                                kind="ExternalOutput").ap()
                continue
            sh, dt = shapes[t]
            self.tap_aps[t] = nc.dram_tensor("tap_" + t, sh, dt,
                                             kind="ExternalOutput").ap()

        E = Emitter()
        self.E = E
        self.s_pe = E.S("s_pe")
        self.s_dv = E.S("s_dv")
        self.s_ac = E.S("s_ac")
        self.s_gp = E.S("s_gp")
        self.s_sy = E.S("s_sy")
        self.s_sc = E.S("s_sc")
        self.s_gd = E.S("s_gd")
        self.s_setup = E.S("s_setup")
        self.s_xt = [E.S("s_xt0"), E.S("s_xt1")]
        self.s_w = [E.S("s_w0"), E.S("s_w1")]
        self.s_k = [E.S("s_k0"), E.S("s_k1")]
        self.s_v = [E.S("s_v0"), E.S("s_v1")]
        self.s_q = [E.S("s_q0"), E.S("s_q1")]
        self.s_st = [E.S("s_st0"), E.S("s_st1"), E.S("s_st2")]
        self.s_stc = [E.S("s_stc0"), E.S("s_stc1")]
        self.s_stx = [E.S("s_stx0"), E.S("s_stx1")]
        self.s_cxl = E.S("s_cxl")
        self.s_misc = E.S("s_misc")
        self.s_lor = E.S("s_lor")
        self.s_gd2 = E.S("s_gd2")
        self.s_ag = E.S("s_ag")

        self.xt = [Slot(), Slot()]
        self.lnt = [Slot(), Slot()]
        self.lns = [Slot(), Slot()]
        self.wst = [Slot(), Slot()]
        self.xfm = [Slot(), Slot()]
        self.stg = [Slot(), Slot(), Slot()]
        self.psA = Slot()
        self.psB = Slot()
        self.psC = [Slot(), Slot()]
        self.psD = Slot()
        self.kh = [Slot(), Slot()]
        self.vh = [Slot(), Slot()]
        self.qh = [Slot(), Slot()]
        self.ctxs = [Slot(), Slot()]
        self.gst = Slot()
        self.rA0 = Slot()
        self.rA1 = Slot()
        self.identity = Slot()
        self.biases = Slot()
        self.att_nm = [Slot(), Slot(), Slot()]
        self.att_sums = [Slot(), Slot(), Slot()]
        self.att_ss = [Slot(), Slot(), Slot()]
        self.gstB = Slot()
        self.probsx = Slot()
        self._gather_steps = []
        self.kst_pts = [[] for _ in range(32)]
        self.vst_pts = [[] for _ in range(32)]
        self.qst_pts = [[] for _ in range(32)]
        self.lora = {k: Slot() for k in ("AhT", "BhT", "zsb", "hsel")}
        self.btsb = Slot()
        self.last_store = None

    def g_bf(self, col, n, parts=128, p0=0):
        return self.grand[p0:p0 + parts, col:col + n]

    def g_f32(self, col, n, parts=128, p0=0):
        return self.grand_f32[p0:p0 + parts, col // 2:col // 2 + n]

    def g_i32(self, col, n, parts=128, p0=0):
        return self.grand_i32[p0:p0 + parts, col // 2:col // 2 + n]

    def head_guards(self):
        pts = []
        for s in (self.kh + self.vh + self.qh + self.ctxs + [self.gst, self.gstB]):
            pts.extend(s.for_write())
        return pts

    def build(self):
        from contextlib import ExitStack
        nc, E = self.nc, self.E
        with ExitStack() as stack:
            grand_h = stack.enter_context(nc.sbuf_tensor("grand", [128, GCOLS], BF16))
            arena_h = stack.enter_context(nc.psum_tensor("arena", [128, 4096], F32))
            for name in list(E.sems):
                E.sems[name].h = stack.enter_context(nc.semaphore(name))
            self.grand = grand_h[:]
            self.grand_f32 = grand_h[:].bitcast(F32)
            self.grand_i32 = grand_h[:].bitcast(I32)
            self.arena = arena_h[:]
            self.arena_bf = arena_h[:].bitcast(BF16)

            self.p0_setup()
            if self.upto >= 1:
                self.p1_attn_a_proj()
            if self.upto >= 2:
                self.p2_gather()
                if self.upto == 2:
                    self.gather_flush()
            if self.upto >= 3:
                self.rA0.seed(self.xfm[0].for_write())
                for s in (self.kh + self.vh + self.qh + self.ctxs +
                          [self.gst, self.gstB]):
                    s.seed(self.xfm[1].for_write())
                self.probsx.seed(self.stg[2].for_write())
                self.p3_attention("a")
                if self.upto >= 2:
                    self.gather_flush()
                if "ctxd" in self.tap_aps:
                    self.tap_dram(self.tap_aps["ctxd"], self.ctxd, list(self.s_stc))
                if "adbg" in self.taps:
                    E = self.E
                    gates = [Pt(s, s.n) for s in (self.s_dv, self.s_ac, self.s_pe,
                                                  self.s_gp, self.s_sc)]
                    E.emit("sy", lambda e, o=self.dbg_pbT, i=self.g_bf(GA0, 32768):
                           e.dma_start(out=o, in_=i), waits=gates,
                           inc=self.s_sy, inck=16)
                    E.emit("sy", lambda e, o=self.dbg_probs, i=self.g_bf(GD, 8192):
                           e.dma_start(out=o, in_=i), waits=gates,
                           inc=self.s_sy, inck=16)
                    E.emit("sy", lambda e, o=self.dbg_small, i=self.g_f32(GH, 1024):
                           e.dma_start(out=o, in_=i), waits=gates,
                           inc=self.s_sy, inck=16)
            if self.upto >= 4:
                self.p4_outproj(self.ow, bias=True, xsrc=self.xq, dst=self.xd,
                                do_ln=self.upto >= 5)
            if self.upto >= 5:
                self.p5_ffn(dst=self.xd)
            if self.upto >= 6:
                self.p6_attn_b()
            if self.upto >= 7:
                self.p5_ffn(dst=self.y)
            self.finalize()

            blk = stack.enter_context(nc.Block())
            E.replay(blk)
        return nc


    def p0_setup(self):
        E = self.E
        g = self.g_f32
        for dst_col, n, src in ((H_QKVB, 96, self.qkvbT), (H_OBT, 32, self.obT),
                                (H_FBT, 32, self.fbT)):
            pt = E.emit("sy", lambda e, o=g(dst_col, n), i=src:
                        e.dma_start(out=o, in_=i), inc=self.s_setup, inck=16)
        idap = self.g_bf(H_ID, 128)
        p1 = E.emit("gp", lambda e, o=idap: e.iota(
            o, pattern=[[1, 128]], base=0, channel_multiplier=-1,
            allow_small_or_imprecise_dtypes=True), inc=self.s_gp)
        p2 = E.emit("gp", lambda e, o=idap: e.tensor_single_scalar(
            o, o, 0.0, ALU.is_equal), waits=[p1], inc=self.s_gp)
        self.identity.wrote(p2)
        p3 = E.emit("sy", lambda e, o=self.g_i32(H_LSI_I, 4),
                    i=self.lsi.unsqueeze(0).broadcast_to([128, 4]):
                    e.dma_start(out=o, in_=i), inc=self.s_setup, inck=16)
        p4 = E.emit("sy", lambda e, o=g(H_BVAL, 1),
                    i=self.bval.broadcast_to([128, 1]):
                    e.dma_start(out=o, in_=i), inc=self.s_setup, inck=16)
        setup_pt = Pt(self.s_setup, self.s_setup.n)
        self.biases.wrote(setup_pt)
        p4 = setup_pt
        p5 = E.emit("dv", lambda e, o=g(H_LSI_F, 4), i=self.g_i32(H_LSI_I, 4):
                    e.tensor_copy(o, i), waits=[setup_pt], inc=self.s_dv)
        bv = g(H_BVAL, 1)
        lf = g(H_LSI_F, 4)

        def tcol(k):
            return g(H_TMP + 2 * k, 1)

        def tt(o, a, b, op, w):
            return E.emit("dv", lambda e, o=o, a=a, b=b, op=op:
                          e.tensor_tensor(o, a, b, op), waits=w, inc=self.s_dv)

        def ts2(o, a, s1, s2, w):
            return E.emit("dv", lambda e, o=o, a=a, s1=s1, s2=s2:
                          e.tensor_scalar(o, a, s1, s2, ALU.mult, ALU.add),
                          waits=w, inc=self.s_dv)

        c = [tt(tcol(k), bv, lf[:, k:k + 1], ALU.is_lt, [p4, p5]) for k in range(4)]
        pu = ts2(tcol(4), tcol(0), -1.0, 1.0, [c[0]])
        pv1 = tt(tcol(5), tcol(1), tcol(4), ALU.mult, [c[1], pu])
        pw = ts2(tcol(6), tcol(1), -2.0, 2.0, [c[1]])
        pv2 = tt(tcol(7), tcol(5), tcol(6), ALU.add, [pv1, pw])
        pinner = tt(tcol(4), tcol(2), tcol(7), ALU.mult, [c[2], pv2])
        pr2 = ts2(tcol(6), tcol(3), -1.0, 4.0, [c[3]])
        pnc2 = ts2(tcol(5), tcol(2), -1.0, 1.0, [c[2]])
        po2 = tt(tcol(6), tcol(5), tcol(6), ALU.mult, [pr2, pnc2])
        pres = tt(tcol(7), tcol(4), tcol(6), ALU.add, [pinner, po2])
        hs = None
        for a in range(3):
            hs = E.emit("dv", lambda e, o=g(H_HSEL + 2 * a, 1), i=tcol(7), v=float(2 + a):
                        e.tensor_single_scalar(o, i, v, ALU.is_equal),
                        waits=[pres], inc=self.s_dv)
        self.lora["hsel"].wrote(hs)

    def ln_pass(self, src, row0, ntiles, xfm_col, xfm_slot, gate=()):
        E = self.E
        g = self.g_f32
        last_evac = None
        half0 = None
        first = True
        for j in range(ntiles):
            par = j % 2
            xt = self.xt[par]
            xtap = g(GC + par * 8192, 4096)
            lns = self.lns[par]
            r0 = row0 + j * 128
            pl = E.emit("sy" if par == 0 else "gp",
                        lambda e, o=xtap, i=src[r0:r0 + 128, :]:
                        e.dma_start(out=o, in_=i),
                        waits=xt.for_write() + (gate or []),
                        inc=self.s_xt[par], inck=16)
            xt.wrote(pl)
            lnt = self.lnt[par]
            lntap = self.g_bf(GD + par * 4096, 4096)
            c_sq, c_s1, c_nm, c_m2, c_v, c_lnv, c_rstd = (
                g(lncol(LN_SUMSQ, par), 1), g(lncol(LN_S1, par), 1),
                g(lncol(LN_NEGM, par), 1), g(lncol(LN_M2, par), 1),
                g(lncol(LN_VAR, par), 1), g(lncol(LN_LNV, par), 1),
                g(lncol(LN_RSTD, par), 1))
            psq = E.emit("ac", lambda e, o=lntap, i=xtap, a=c_sq:
                         e.activation(o, i, AF.Square, bias=0.0, scale=1.0, accum_out=a),
                         waits=[xt.w] + lnt.for_write() + lns.for_write(), inc=self.s_ac)
            ps1 = E.emit("dv", lambda e, o=c_s1, i=xtap:
                         e.reduce_sum(o, i, axis=AX.X), waits=[xt.w], inc=self.s_dv)
            pnm = E.emit("dv", lambda e, o=c_nm, i=c_s1:
                         e.tensor_scalar_mul(o, i, -1.0 / 4096.0),
                         waits=[ps1], inc=self.s_dv)
            pm2 = E.emit("dv", lambda e, o=c_m2, i=c_nm:
                         e.tensor_scalar(o, i, i, -1e-5, ALU.mult, ALU.add),
                         waits=[pnm], inc=self.s_dv)
            pvar = E.emit("dv", lambda e, o=c_v, a=c_sq, b=c_m2:
                          e.scalar_tensor_tensor(o, a, 1.0 / 4096.0, b,
                                                 ALU.mult, ALU.subtract),
                          waits=[psq, pm2], inc=self.s_dv)
            plnv = E.emit("ac", lambda e, o=c_lnv, i=c_v:
                          e.activation(o, i, AF.Ln, bias=0.0, scale=1.0),
                          waits=[pvar], inc=self.s_ac)
            prstd = E.emit("ac", lambda e, o=c_rstd, i=c_lnv:
                           e.activation(o, i, AF.Exp, bias=0.0, scale=-0.5),
                           waits=[plnv], inc=self.s_ac)
            pnorm = E.emit("dv", lambda e, o=lntap, i=xtap, s1=c_nm, s2=c_rstd:
                           e.tensor_scalar(o, i, s1, s2, ALU.add, ALU.mult),
                           waits=[prstd, pnm, xt.w, psq], inc=self.s_dv)
            lnt.wrote(pnorm)
            lns.read(pnorm)
            xt.read(pnorm)
            for bi in range(4):
                pc = self.psC[bi % 2]
                pcap = self.arena_bf[:, 4096 + (bi % 2) * 1024:
                                     4096 + (bi % 2) * 1024 + 1024]
                gpt = None
                for jj in range(8):
                    dc = bi * 8 + jj
                    w = ([pnorm, self.identity.w] + pc.for_write()) if jj == 0 else []
                    gpt = E.emit("pe", lambda e, o=pcap[:, jj * 128:jj * 128 + 128],
                                 i=lntap[:, dc * 128:dc * 128 + 128],
                                 idn=self.g_bf(H_ID, 128):
                                 e.transpose(o, i, idn),
                                 waits=w, inc=self.s_pe if jj == 7 else None)
                pc.wrote(gpt)
                lnt.read(gpt)
                base = xfm_col + (bi * 8) * 1024 + j * 128
                dst = self.grand[:, base: base + 8 * 1024].rearrange(
                    "p (c t) -> p c t", c=8)[:, :, 0:128]
                pev = E.emit("dv", lambda e, o=dst,
                             i=pcap.rearrange("p (c t) -> p c t", c=8):
                             e.tensor_copy(o, i),
                             waits=[gpt] + (xfm_slot.for_write() if first else []),
                             inc=self.s_dv)
                first = False
                pc.read(pev)
                last_evac = pev
            if j == ntiles // 2 - 1:
                half0 = last_evac
        xfm_slot.wrote(last_evac)
        return [half0, last_evac]

    def proj_fm(self, wdram, wchunk0, nchunks, bias_fn, scale, xfm_col, xfm_slot, sink,
                lora=False, xfm_halves=None):
        E = self.E
        for oc in range(nchunks):
            ws = self.wst[oc % 2]
            wsap = self.g_bf(GE + (oc % 2) * 4096, 4096)
            pw = E.emit("gp", lambda e, o=wsap, i=wdram[wchunk0 + oc]:
                        e.dma_start(out=o, in_=i),
                        waits=ws.for_write(), inc=self.s_w[oc % 2], inck=16)
            ws.wrote(pw)
            ps = self.psA if (oc % 2 == 0) else self.psB
            psap = self.arena[:, (oc % 2) * 1024:(oc % 2) * 1024 + 1024]
            gpt = None
            for dc in range(32):
                for sp in range(2):
                    if dc == 0 and sp == 0:
                        w = [pw, xfm_halves[0] if xfm_halves else xfm_slot.w] \
                            + ps.for_write()
                    elif dc == 0 and sp == 1 and xfm_halves:
                        w = [xfm_halves[1]]
                    else:
                        w = []
                    is_last = (dc == 31 and sp == 1) and not lora
                    gpt = E.emit("pe", lambda e, o=psap[:, sp * 512:sp * 512 + 512],
                                 l=wsap[:, dc * 128:dc * 128 + 128],
                                 r=self.grand[:, xfm_col + dc * 1024 + sp * 512:
                                              xfm_col + dc * 1024 + sp * 512 + 512],
                                 st=(dc == 0), sp_=is_last:
                                 e.matmul(o, l, r, start=st, stop=sp_),
                                 waits=w, inc=self.s_pe if is_last else None)
            if lora:
                for sp in range(2):
                    is_last = sp == 1
                    gpt = E.emit("pe", lambda e, o=psap[:, sp * 512:sp * 512 + 512],
                                 l=self.g_bf(D_BHT + oc * 128, 128, parts=16),
                                 r=self.g_bf(D_ZSB + sp * 512, 512, parts=16),
                                 sp_=is_last:
                                 e.matmul(o, l, r, start=False, stop=sp_),
                                 waits=([self.lora["BhT"].w, self.lora["zsb"].w]
                                        if sp == 0 else []),
                                 inc=self.s_pe if is_last else None)
            ps.wrote(gpt)
            ws.read(gpt)
            xfm_slot.read(gpt)
            if lora:
                self.lora["BhT"].read(gpt)
                self.lora["zsb"].read(gpt)
            st = self.stg[oc % 2]
            stap = self.g_bf(ST0 + (oc % 2) * 1024, 1024)
            bap = bias_fn(oc) if bias_fn is not None else None
            if bap is not None and scale is not None:
                pev = E.emit("dv", lambda e, o=stap, i=psap, b=bap, s=scale:
                             e.tensor_scalar(o, i, b, s, ALU.add, ALU.mult),
                             waits=[gpt, self.biases.w] + st.for_write(), inc=self.s_dv)
            elif bap is not None:
                pev = E.emit("dv", lambda e, o=stap, i=psap, b=bap:
                             e.tensor_scalar(o, i, b, None, ALU.add),
                             waits=[gpt, self.biases.w] + st.for_write(), inc=self.s_dv)
            elif scale is not None:
                pev = E.emit("dv", lambda e, o=stap, i=psap, s=scale:
                             e.tensor_scalar_mul(o, i, s),
                             waits=[gpt] + st.for_write(), inc=self.s_dv)
            else:
                pev = E.emit("dv", lambda e, o=stap, i=psap: e.tensor_copy(o, i),
                             waits=[gpt] + st.for_write(), inc=self.s_dv)
            ps.read(pev)
            st.wrote(pev)
            sink(oc, stap, st, pev)

    def qkvb_col(self, chunk0):
        def f(oc):
            return self.g_f32(H_QKVB, 96)[:, chunk0 + oc:chunk0 + oc + 1]
        return f

    def p1_attn_a_proj(self):
        E = self.E
        xfm_slot = self.xfm[0]
        halves = self.ln_pass(self.xq, 0, 8, GA0, xfm_slot)

        def kvsink(dst):
            def sink(oc, stap, st, pev):
                p = E.emit("ac", lambda e,
                           o=dst[oc * 128:(oc + 1) * 128, :],
                           i=stap: e.dma_start(out=o, in_=i),
                           waits=[pev], inc=self.s_st[oc % 2], inck=16)
                st.read(p)
            return sink

        for kv, base in (("K", 32), ("V", 64)):
            for hf in range(2):
                self.proj_fm(self.qkvw, base + hf * 16, 16,
                             self.qkvb_col(base + hf * 16), None,
                             GA0, xfm_slot, kvsink(self.kvin[kv, hf]),
                             xfm_halves=halves)
                kv_done = [Pt(s, s.n) for s in self.s_st]
                self.ag_pts[kv, hf] = E.emit(
                    "gp", lambda e, i=self.kvin[kv, hf], o=self.kvag[kv, hf]:
                    e.collective_compute(
                        "AllGather", ALU.bypass,
                        replica_groups=[[0, 1, 2, 3], [4, 5, 6, 7]],
                        ins=[i.opt()], outs=[o.opt()]),
                    waits=kv_done, inc=self.s_ag, inck=1)

        def qsink(oc, stap, st, pev):
            p = E.emit("ac", lambda e, o=self.qfmd[oc * 128:(oc + 1) * 128, :],
                       i=stap: e.dma_start(out=o, in_=i),
                       waits=[pev], inc=self.s_st[oc % 2], inck=16)
            st.read(p)
            self.qst_pts[oc].append(p)
        self.proj_fm(self.qkvw, 0, 32, self.qkvb_col(0), SCALE,
                     GA0, xfm_slot, qsink, xfm_halves=halves)
        for t in ("qfmd",):
            if t in self.tap_aps:
                self.tap_dram(self.tap_aps[t], getattr(self, t), list(self.s_st))

    def tap_dram(self, dst, src, gate_sem):
        sems = gate_sem if isinstance(gate_sem, list) else [gate_sem]
        self.E.emit("sy", lambda e, o=dst, i=src: e.dma_start(out=o, in_=i),
                    waits=[Pt(s, s.n) for s in sems], inc=self.s_sy, inck=16)

    def p2_gather(self):
        E = self.E
        pb = E.emit("gp", lambda e, o=self.g_i32(H_BTAB, 2),
                    i=self.btab.rearrange("(a p) -> p a", p=128):
                    e.dma_start(out=o, in_=i, allow_slow_non_contiguous=True),
                    inc=self.s_gd, inck=16)
        self.btsb.wrote(pb)
        steps = []
        for src_, dst in ((self.kcache, self.kgd), (self.vcache, self.vgd)):
            for h in range(8):
                for half in range(2):
                    steps.append((src_, dst, h, half))

        def make_step(idx):
            src_, dst, h, half = steps[idx]
            par = idx % 2
            gslot = self.gst if par == 0 else self.gstB
            gsem = self.s_gd if par == 0 else self.s_gd2
            gstap = self.g_f32(A1_GST if par == 0 else A1_GSTB, 2048)

            def emit():
                pg = E.emit("gp", lambda e, o=gstap, i=src_, eo=h * 2048,
                            ix=self.g_i32(H_BTAB, 2)[:, half:half + 1]:
                            e.indirect_dma_start(o, None, i,
                                                 IndirectOffsetOnAxis(ix, 0),
                                                 element_offset=eo),
                            waits=[pb] + gslot.for_write(), inc=gsem, inck=16)
                gslot.wrote(pg)
                ps = E.emit("gp", lambda e,
                            o=dst[h, half * 2048:(half + 1) * 2048, :]
                            .rearrange("(p r) d -> p r d", p=128),
                            i=gstap.rearrange("p (r d) -> p r d", r=16):
                            e.dma_start(out=o, in_=i),
                            waits=[pg], inc=gsem, inck=16)
                gslot.read(ps)
            return emit

        self._gather_steps = [make_step(i) for i in range(len(steps))]

    def gather_tick(self, n=2):
        for _ in range(n):
            if self._gather_steps:
                self._gather_steps.pop(0)()

    def gather_flush(self):
        while self._gather_steps:
            self._gather_steps.pop(0)()
        self.gather_gate = [Pt(self.s_gd, self.s_gd.n),
                            Pt(self.s_gd2, self.s_gd2.n)]
        for t in ("kgd", "vgd"):
            if t in self.tap_aps:
                self.tap_dram(self.tap_aps[t], getattr(self, t), self.s_gd)

    def p3_attention(self, mode):
        E = self.E
        heads = [None] * NH

        def inplace_tm_fm_build(S, Sap, gate_pt):
            for bi in range(4):
                pc = self.psC[bi % 2]
                pcap = self.arena_bf[:, 4096 + (bi % 2) * 1024:
                                     4096 + (bi % 2) * 1024 + 1024]
                gpt = None
                for jj in range(8):
                    cc = bi * 8 + jj
                    w = ([gate_pt, self.identity.w] + pc.for_write()) if jj == 0 else []
                    gpt = E.emit("pe",
                                 lambda e, o=pcap[:, jj * 128:jj * 128 + 128],
                                 i=Sap[:, cc * 128:cc * 128 + 128],
                                 idn=self.g_bf(H_ID, 128):
                                 e.transpose(o, i, idn), waits=w,
                                 inc=self.s_pe if jj == 7 else None)
                pc.wrote(gpt)
                S.read(gpt)
                pev = E.emit("dv", lambda e,
                             o=Sap[:, bi * 1024:bi * 1024 + 1024],
                             i=pcap: e.tensor_copy(o, i),
                             waits=[gpt], inc=self.s_dv)
                pc.read(pev)
                if bi == 0:
                    S.wrote(pev)
                else:
                    S.add_write(pev)

        pending_builds = []

        def load_head(h):
            if mode == "a":
                K, Kap = self.kh[h % 2], self.g_bf(A1_K0 + (h % 2) * 4096, 4096)
                V, Vap = self.vh[h % 2], self.g_bf(A1_V0 + (h % 2) * 4096, 4096)
                agvK = self.kvag["K", h // 16].rearrange(
                    "(j c p) t -> c p j t", j=4, p=128)
                agvV = self.kvag["V", h // 16].rearrange(
                    "(j c p) t -> c p j t", j=4, p=128)
                pk = E.emit("sy", lambda e, o=Kap.rearrange("p (j t) -> p j t", j=4),
                            i=agvK[h % 16]:
                            e.dma_start(out=o, in_=i),
                            waits=K.for_write() + [self.ag_pts["K", h // 16]],
                            inc=self.s_k[h % 2], inck=16)
                K.wrote(pk)
                pv = E.emit("gp", lambda e, o=Vap.rearrange("p (j t) -> p j t", j=4),
                            i=agvV[h % 16]:
                            e.dma_start(out=o, in_=i),
                            waits=V.for_write() + [self.ag_pts["V", h // 16]],
                            inc=self.s_v[h % 2], inck=16)
                V.wrote(pv)
                pending_builds.append(
                    lambda V=V, Vap=Vap, pv=pv: inplace_tm_fm_build(V, Vap, pv))
            else:
                kvh = h // 4
                K, Kap = self.kh[kvh % 2], self.g_bf(A1_K0 + (kvh % 2) * 4096, 4096)
                V, Vap = self.vh[kvh % 2], self.g_bf(A1_V0 + (kvh % 2) * 4096, 4096)
                if h % 4 == 0:
                    pk = E.emit("sy", lambda e,
                                o=Kap.rearrange("p (c d) -> p c d", c=32),
                                i=self.kgd[kvh].rearrange("(c p) d -> p c d", p=128):
                                e.dma_start(out=o, in_=i),
                                waits=K.for_write() + self.gather_gate,
                                inc=self.s_k[kvh % 2], inck=16)
                    K.wrote(pk)
                    pending_builds.append(
                        lambda K=K, Kap=Kap, pk=pk: inplace_tm_fm_build(K, Kap, pk))
                    pv = E.emit("sy", lambda e,
                                o=Vap.rearrange("p (c d) -> p c d", c=32),
                                i=self.vgd[kvh].rearrange("(c p) d -> p c d", p=128):
                                e.dma_start(out=o, in_=i),
                                waits=V.for_write() + self.gather_gate,
                                inc=self.s_v[kvh % 2], inck=16)
                    V.wrote(pv)
            Q, Qap = self.qh[h % 2], self.g_bf(A1_Q0 + (h % 2) * 1024, 1024)
            pq = E.emit("sy", lambda e, o=Qap, i=self.qfmd[h * 128:(h + 1) * 128, :]:
                        e.dma_start(out=o, in_=i),
                        waits=Q.for_write() + self.qst_pts[h],
                        inc=self.s_q[h % 2], inck=16)
            Q.wrote(pq)
            heads[h] = (K, Kap, V, Vap, Q, Qap)

        def flush_builds():
            while pending_builds:
                pending_builds.pop(0)()

        def emit_ctx(h):
            K, Kap, V, Vap, Q, Qap = heads[h]
            ctxap = self.arena[:, 3072:4096]
            gpt = None
            for c in range(32):
                for sp in range(2):
                    w = (self.rA0.rw() + V.rw() + self.psD.for_write()) \
                        if (c == 0 and sp == 0) else []
                    is_last = c == 31 and sp == 1
                    gpt = E.emit("pe", lambda e, o=ctxap[:, sp * 512:sp * 512 + 512],
                                 l=Vap[:, c * 128:c * 128 + 128],
                                 r=self.grand[:, GA0 + c * 1024 + sp * 512:
                                              GA0 + c * 1024 + sp * 512 + 512],
                                 st=(c == 0), spp=(c == 31):
                                 e.matmul(o, l, r, start=st, stop=spp),
                                 waits=w, inc=self.s_pe if is_last else None)
            self.psD.wrote(gpt)
            self.rA0.read(gpt)
            V.read(gpt)
            cs = self.ctxs[h % 2]
            csap = self.g_bf(A1_CTXS0 + (h % 2) * 1024, 1024)
            pev = E.emit("dv", lambda e, o=csap, i=ctxap: e.tensor_copy(o, i),
                         waits=[gpt] + cs.for_write(), inc=self.s_dv)
            self.psD.read(pev)
            cs.wrote(pev)
            pst = E.emit("ac", lambda e, o=self.ctxd[h * 128:(h + 1) * 128, :], i=csap:
                         e.dma_start(out=o, in_=i), waits=[pev],
                         inc=self.s_stc[h % 2], inck=16)
            cs.read(pst)

        NU = NH * 8
        unit_state = [None] * NU
        for gu in range(NU + 2):
            h, i = divmod(gu, 8)
            if gu == 0:
                load_head(0)
                load_head(1)
                if mode == "b":
                    flush_builds()
            if gu < NU:
                self.unit_quarters(gu, 0, 2, heads[h], unit_state)
            if gu >= 1 and gu - 1 < NU:
                self.unit_norm(gu - 1, unit_state)
            if gu >= 2:
                self.unit_T(gu - 2, unit_state, (0, 1))
            if gu < NU:
                self.unit_quarters(gu, 2, 4, heads[h], unit_state)
            if gu >= 2:
                self.unit_T(gu - 2, unit_state, (2, 3))
                if (gu - 2) % 8 == 3:
                    flush_builds()
                if (gu - 2) % 8 == 7:
                    hh = (gu - 2) // 8
                    emit_ctx(hh)
                    if hh + 2 < NH:
                        load_head(hh + 2)
                    if mode == "a":
                        self.gather_tick(2)

    def probs3(self, par):
        slots = [self.lnt[0], self.lnt[1], self.probsx]
        cols = [GD, GD + 4096, GD2]
        return slots[par], self.g_bf(cols[par], 4096)

    def unit_quarters(self, gu, k0, k1, head, unit_state):
        E = self.E
        g = self.g_f32
        K, Kap, V, Vap, Q, Qap = head
        i = gu % 8
        par = gu % 3
        if unit_state[gu] is None:
            unit_state[gu] = {"par": par}
        stt = unit_state[gu]
        probs, probsap = self.probs3(par)
        stt["probs"], stt["probsap"] = probs, probsap
        nm_ap = g(H_ANM + 2 * par, 1)
        sums_ap = g(H_ASUM + 8 * par, 4)
        stt["nm_ap"], stt["sums_ap"] = nm_ap, sums_ap
        for k in range(k0, k1):
            ps = self.psA if k % 2 == 0 else self.psB
            psap = self.arena[:, (k % 2) * 1024:(k % 2) * 1024 + 1024]
            gpt = None
            for sp in range(2):
                w = ([K.w, Q.w] + ps.for_write()) if sp == 0 else []
                gpt = E.emit("pe", lambda e, o=psap[:, sp * 512:sp * 512 + 512],
                             l=Qap[:, i * 128:i * 128 + 128],
                             r=Kap[:, k * 1024 + sp * 512:k * 1024 + sp * 512 + 512]:
                             e.matmul(o, l, r, start=True, stop=True),
                             waits=w, inc=self.s_pe if sp == 1 else None)
            ps.wrote(gpt)
            K.read(gpt)
            Q.read(gpt)
            if k == 0:
                nm = self.att_nm[par]
                pmx = E.emit("dv", lambda e, o=nm_ap, i=self.arena[:, 0:1024:8]:
                             e.reduce_max(o, i, axis=AX.X, negate=True),
                             waits=[gpt] + nm.for_write(), inc=self.s_dv)
                nm.wrote(pmx)
                stt["pmx"] = pmx
            sums = self.att_sums[par]
            pex = E.emit("ac", lambda e, o=probsap[:, k * 1024:k * 1024 + 1024],
                         i=psap, b=nm_ap, a=sums_ap[:, k:k + 1]:
                         e.activation(o, i, AF.Exp, bias=b, scale=1.0, accum_out=a),
                         waits=[stt["pmx"], gpt] +
                               (probs.for_write() if k == 0 else []) +
                               (sums.for_write() if k == 0 else []),
                         inc=self.s_ac)
            ps.read(pex)
            if k == 0:
                probs.wrote(pex)
                sums.wrote(pex)
            else:
                probs.add_write(pex)
                sums.add_write(pex)
            if k == 3:
                stt["exp_last"] = pex
                self.att_nm[par].read(pex)

    def unit_norm(self, gu, unit_state):
        E = self.E
        g = self.g_f32
        stt = unit_state[gu]
        par = stt["par"]
        probs, probsap = stt["probs"], stt["probsap"]
        ss = self.att_ss[par]
        pss = E.emit("dv", lambda e, o=g(H_ASS + 2 * par, 1), i=stt["sums_ap"]:
                     e.reduce_sum(o, i, axis=AX.X),
                     waits=[stt["exp_last"]] + ss.for_write(), inc=self.s_dv)
        prc = E.emit("dv", lambda e, o=g(H_ARC + 2 * par, 1), i=g(H_ASS + 2 * par, 1):
                     e.reciprocal(o, i), waits=[pss], inc=self.s_dv)
        ss.wrote(prc)
        self.att_sums[par].read(pss)
        pr = E.emit("dv", lambda e, o=probsap, s=g(H_ARC + 2 * par, 1):
                    e.tensor_scalar_mul(o, o, s),
                    waits=[prc, probs.w], inc=self.s_dv)
        probs.wrote(pr)
        ss.read(pr)
        stt["pr"] = pr

    def unit_T(self, gu, unit_state, bis):
        E = self.E
        stt = unit_state[gu]
        i = gu % 8
        probs, probsap = stt["probs"], stt["probsap"]
        pr = stt["pr"]
        for bi in bis:
            pc = self.psC[bi % 2]
            pcap = self.arena_bf[:, 4096 + (bi % 2) * 1024:
                                 4096 + (bi % 2) * 1024 + 1024]
            gpt = None
            for jj in range(8):
                cc = bi * 8 + jj
                w = ([pr, self.identity.w] + pc.for_write()) if jj == 0 else []
                gpt = E.emit("pe", lambda e, o=pcap[:, jj * 128:jj * 128 + 128],
                             i=probsap[:, cc * 128:cc * 128 + 128],
                             dk=self.g_bf(H_ID, 128):
                             e.transpose(o, i, dk), waits=w,
                             inc=self.s_pe if jj == 7 else None)
            pc.wrote(gpt)
            probs.read(gpt)
            base = GA0 + (bi * 8) * 1024 + i * 128
            dst = self.grand[:, base: base + 8 * 1024].rearrange(
                "p (c t) -> p c t", c=8)[:, :, 0:128]
            pev = E.emit("dv",
                         lambda e, o=dst, i=pcap.rearrange("p (c t) -> p c t", c=8):
                         e.tensor_copy(o, i),
                         waits=[gpt] + (self.rA0.for_write() if (i == 0 and bi == 0)
                                        else []),
                         inc=self.s_dv)
            pc.read(pev)
            if i == 0 and bi == 0:
                self.rA0.wrote(pev)
            else:
                self.rA0.add_write(pev)

    def p4_outproj(self, wdram, bias, xsrc, dst, do_ln=False):
        E = self.E
        pl = E.emit("sy", lambda e,
                    o=self.grand[:, GA0:GA0 + 32768].rearrange("p (c t) -> p c t", c=32),
                    i=self.ctxd.rearrange("(c p) t -> p c t", p=128):
                    e.dma_start(out=o, in_=i),
                    waits=self.rA0.for_write() + [Pt(s, s.n) for s in self.s_stc],
                    inc=self.s_cxl, inck=16)
        self.rA0.wrote(pl)
        bias_fn = (lambda oc: self.g_f32(H_OBT, 32)[:, oc:oc + 1]) if bias else None
        guards = self.head_guards() + self.rA1.for_write()

        def osink(oc, stap, st, pev):
            p = E.emit("dv", lambda e, o=self.g_bf(GA1 + oc * 1024, 1024), i=stap:
                       e.tensor_copy(o, i),
                       waits=[pev] + (guards if oc == 0 else []),
                       inc=self.s_dv)
            st.read(p)
            self.rA1.wrote(p)

        self.proj_fm(wdram, 0, 32, bias_fn, None, GA0, self.rA0, osink)
        self.add_pass(xsrc, dst, do_ln=do_ln)

    def add_pass(self, xsrc, dst, do_ln=False):
        E = self.E
        g = self.g_f32
        half0 = None
        last_evac = None
        first = True
        for j in range(8):
            par = j % 2
            xt = self.xt[par]
            xtap = g(GC + par * 8192, 4096)
            pl = E.emit("sy" if par == 0 else "gp",
                        lambda e, o=xtap, i=xsrc[j * 128:(j + 1) * 128, :]:
                        e.dma_start(out=o, in_=i),
                        waits=xt.for_write() + (self.last_store or []),
                        inc=self.s_xt[par], inck=16)
            xt.wrote(pl)
            addpts = []
            for half in range(2):
                if half == 0:
                    slots = [self.psC[0], self.psC[1]]
                else:
                    slots = [self.psD]
                pap = self.arena_bf[:, (4 + half * 2) * 1024:
                                    (4 + half * 2) * 1024 + 2048]
                gpt = None
                for ocl in range(16):
                    oc = half * 16 + ocl
                    if ocl == 0:
                        w = [self.rA1.w, self.identity.w]
                        for s in slots:
                            w += s.for_write()
                    else:
                        w = []
                    gpt = E.emit("pe", lambda e, o=pap[:, ocl * 128:ocl * 128 + 128],
                                 i=self.g_bf(GA1 + oc * 1024 + j * 128, 128),
                                 idn=self.g_bf(H_ID, 128):
                                 e.transpose(o, i, idn), waits=w,
                                 inc=self.s_pe if ocl == 15 else None)
                for s in slots:
                    s.wrote(gpt)
                self.rA1.read(gpt)
                padd = E.emit("dv", lambda e, o=xtap[:, half * 2048:half * 2048 + 2048],
                              i=pap: e.tensor_tensor(o, o, i, ALU.add),
                              waits=[gpt, xt.w], inc=self.s_dv)
                for s in slots:
                    s.read(padd)
                addpts.append(padd)
            pst = E.emit("ac", lambda e, o=dst[j * 128:(j + 1) * 128, :], i=xtap:
                         e.dma_start(out=o, in_=i), waits=addpts,
                         inc=self.s_stx[par], inck=16)
            xt.read(pst)
            if do_ln:
                lnt = self.lnt[par]
                lns = self.lns[par]
                lntap = self.g_bf(GD + par * 4096, 4096)
                c_sq, c_s1, c_nm, c_m2, c_v, c_lnv, c_rstd = (
                    g(lncol(LN_SUMSQ, par), 1), g(lncol(LN_S1, par), 1),
                    g(lncol(LN_NEGM, par), 1), g(lncol(LN_M2, par), 1),
                    g(lncol(LN_VAR, par), 1), g(lncol(LN_LNV, par), 1),
                    g(lncol(LN_RSTD, par), 1))
                psq = E.emit("ac", lambda e, o=lntap, i=xtap, a=c_sq:
                             e.activation(o, i, AF.Square, bias=0.0, scale=1.0,
                                          accum_out=a),
                             waits=addpts + lnt.for_write() + lns.for_write(),
                             inc=self.s_ac)
                ps1 = E.emit("dv", lambda e, o=c_s1, i=xtap:
                             e.reduce_sum(o, i, axis=AX.X), waits=addpts,
                             inc=self.s_dv)
                pnm = E.emit("dv", lambda e, o=c_nm, i=c_s1:
                             e.tensor_scalar_mul(o, i, -1.0 / 4096.0),
                             waits=[ps1], inc=self.s_dv)
                pm2 = E.emit("dv", lambda e, o=c_m2, i=c_nm:
                             e.tensor_scalar(o, i, i, -1e-5, ALU.mult, ALU.add),
                             waits=[pnm], inc=self.s_dv)
                pvar = E.emit("dv", lambda e, o=c_v, a=c_sq, b=c_m2:
                              e.scalar_tensor_tensor(o, a, 1.0 / 4096.0, b,
                                                     ALU.mult, ALU.subtract),
                              waits=[psq, pm2], inc=self.s_dv)
                plnv = E.emit("ac", lambda e, o=c_lnv, i=c_v:
                              e.activation(o, i, AF.Ln, bias=0.0, scale=1.0),
                              waits=[pvar], inc=self.s_ac)
                prstd = E.emit("ac", lambda e, o=c_rstd, i=c_lnv:
                               e.activation(o, i, AF.Exp, bias=0.0, scale=-0.5),
                               waits=[plnv], inc=self.s_ac)
                pnorm = E.emit("dv", lambda e, o=lntap, i=xtap, s1=c_nm, s2=c_rstd:
                               e.tensor_scalar(o, i, s1, s2, ALU.add, ALU.mult),
                               waits=[prstd, pnm, psq] + addpts, inc=self.s_dv)
                lnt.wrote(pnorm)
                lns.read(pnorm)
                xt.read(pnorm)
                for bi in range(4):
                    pc = self.psC[bi % 2]
                    pcap = self.arena_bf[:, 4096 + (bi % 2) * 1024:
                                         4096 + (bi % 2) * 1024 + 1024]
                    gpt = None
                    for jj in range(8):
                        dc = bi * 8 + jj
                        w = ([pnorm, self.identity.w] + pc.for_write()) \
                            if jj == 0 else []
                        gpt = E.emit("pe",
                                     lambda e, o=pcap[:, jj * 128:jj * 128 + 128],
                                     i=lntap[:, dc * 128:dc * 128 + 128],
                                     idn=self.g_bf(H_ID, 128):
                                     e.transpose(o, i, idn),
                                     waits=w, inc=self.s_pe if jj == 7 else None)
                    pc.wrote(gpt)
                    lnt.read(gpt)
                    base = GA0 + (bi * 8) * 1024 + j * 128
                    dfm = self.grand[:, base: base + 8 * 1024].rearrange(
                        "p (c t) -> p c t", c=8)[:, :, 0:128]
                    pev = E.emit("dv", lambda e, o=dfm,
                                 i=pcap.rearrange("p (c t) -> p c t", c=8):
                                 e.tensor_copy(o, i),
                                 waits=[gpt] + (self.rA0.for_write()
                                                if first else []),
                                 inc=self.s_dv)
                    first = False
                    pc.read(pev)
                    last_evac = pev
                if j == 3:
                    half0 = last_evac
        self.last_store = [Pt(s, s.n) for s in self.s_stx]
        if do_ln:
            self.rA0.wrote(last_evac)
            self.next_halves = [half0, last_evac]

    def p5_ffn(self, dst):
        E = self.E
        g = self.g_f32
        halves = self.next_halves
        hw = self.lora["hsel"].w
        for a in range(3):
            E.emit("gp", lambda e, o=self.g_bf(GA1 + a * 4096, 4096, parts=16),
                   i=self.loraA[a]:
                   e.dma_start(out=o, in_=i),
                   waits=(self.rA1.for_write() + self.head_guards())
                   if a == 0 else [],
                   inc=self.s_lor, inck=16)
        for a in range(3):
            E.emit("gp", lambda e,
                   o=self.g_bf(GA1 + 16384 + a * 512, 512).rearrange("p (c r) -> p c r", c=32),
                   i=self.loraB[a].rearrange("(c p) r -> p c r", p=128):
                   e.dma_start(out=o, in_=i), waits=[],
                   inc=self.s_lor, inck=16)
        lor_pt = Pt(self.s_lor, self.s_lor.n)
        apts = [lor_pt, lor_pt, lor_pt]
        ah = self.g_bf(GA1 + 3 * 4096, 4096, parts=16)
        p = E.emit("dv", lambda e, o=ah, i=self.g_bf(GA1 + 0, 4096, parts=16),
                   s=g(H_HSEL, 3, parts=16)[:, 0:1]:
                   e.tensor_scalar_mul(o, i, s), waits=[apts[0], hw], inc=self.s_dv)
        for a in (1, 2):
            p = E.emit("dv", lambda e, o=ah, i=self.g_bf(GA1 + a * 4096, 4096, parts=16),
                       s=g(H_HSEL, 3, parts=16)[:, a:a + 1]:
                       e.scalar_tensor_tensor(o, i, s, o, ALU.mult, ALU.add),
                       waits=[apts[a], p], inc=self.s_dv)
        pah = p
        self.rA1.wrote(pah)
        for bi in range(4):
            pc = self.psC[bi % 2]
            pcap = self.arena_bf[:, 4096 + (bi % 2) * 1024:
                                 4096 + (bi % 2) * 1024 + 1024]
            gpt = None
            for jj in range(8):
                cc = bi * 8 + jj
                w = ([pah, self.identity.w] + pc.for_write()) if jj == 0 else []
                gpt = E.emit("pe", lambda e, o=pcap[:, jj * 16:jj * 16 + 16],
                             i=ah[:, cc * 128:cc * 128 + 128],
                             idn=self.g_bf(H_ID, 128, parts=16)[:, 0:16]:
                             e.transpose(o, i, idn), waits=w,
                             inc=self.s_pe if jj == 7 else None)
            pc.wrote(gpt)
            self.rA1.read(gpt)
            pev = E.emit("dv", lambda e, o=self.g_bf(D_AHT + bi * 128, 128),
                         i=pcap[:, 0:128]: e.tensor_copy(o, i),
                         waits=[gpt] + (self.lora["AhT"].for_write() +
                                        self.lnt[0].for_write() if bi == 0 else []),
                         inc=self.s_dv)
            pc.read(pev)
            self.lora["AhT"].wrote(pev)
        bpts = [lor_pt, lor_pt, lor_pt]
        bh = self.g_bf(GA1 + 16384 + 3 * 512, 512)
        p = E.emit("dv", lambda e, o=bh, i=self.g_bf(GA1 + 16384, 512),
                   s=g(H_HSEL, 3)[:, 0:1]:
                   e.tensor_scalar_mul(o, i, s), waits=[bpts[0], hw], inc=self.s_dv)
        for a in (1, 2):
            p = E.emit("dv", lambda e, o=bh, i=self.g_bf(GA1 + 16384 + a * 512, 512),
                       s=g(H_HSEL, 3)[:, a:a + 1]:
                       e.scalar_tensor_tensor(o, i, s, o, ALU.mult, ALU.add),
                       waits=[bpts[a], p], inc=self.s_dv)
        pbh = p
        self.rA1.wrote(pbh)
        for bi in range(4):
            pc = self.psC[bi % 2]
            pcap = self.arena_bf[0:16, 4096 + (bi % 2) * 1024:
                                 4096 + (bi % 2) * 1024 + 1024]
            gpt = None
            for jj in range(8):
                cc = bi * 8 + jj
                w = ([pbh, self.identity.w] + pc.for_write()) if jj == 0 else []
                gpt = E.emit("pe", lambda e, o=pcap[:, jj * 128:jj * 128 + 128],
                             i=bh[:, cc * 16:cc * 16 + 16],
                             idn=self.g_bf(H_ID, 128):
                             e.transpose(o, i, idn), waits=w,
                             inc=self.s_pe if jj == 7 else None)
            pc.wrote(gpt)
            self.rA1.read(gpt)
            pev = E.emit("dv", lambda e, o=self.g_bf(D_BHT + bi * 1024, 1024, parts=16),
                         i=pcap: e.tensor_copy(o, i),
                         waits=[gpt] + (self.lora["BhT"].for_write() +
                                        self.lnt[1].for_write() if bi == 0 else []),
                         inc=self.s_dv)
            pc.read(pev)
            self.lora["BhT"].wrote(pev)
        zap = self.arena[0:16, 3072:4096]
        gpt = None
        for dc in range(32):
            for sp in range(2):
                w = ([self.lora["AhT"].w, self.rA0.w] + self.psD.for_write()) \
                    if (dc == 0 and sp == 0) else []
                is_last = dc == 31 and sp == 1
                gpt = E.emit("pe", lambda e, o=zap[:, sp * 512:sp * 512 + 512],
                             l=self.g_bf(D_AHT + dc * 16, 16),
                             r=self.grand[:, GA0 + dc * 1024 + sp * 512:
                                          GA0 + dc * 1024 + sp * 512 + 512],
                             st=(dc == 0), spp=is_last:
                             e.matmul(o, l, r, start=st, stop=spp),
                             waits=w, inc=self.s_pe if is_last else None)
        self.psD.wrote(gpt)
        self.lora["AhT"].read(gpt)
        self.rA0.read(gpt)
        pz = E.emit("dv", lambda e, o=self.g_bf(D_ZSB, 1024, parts=16), i=zap:
                    e.tensor_copy(o, i),
                    waits=[gpt] + self.lora["zsb"].for_write(), inc=self.s_dv)
        self.psD.read(pz)
        self.lora["zsb"].wrote(pz)

        def fsink(oc, stap, st, pev):
            p = E.emit("dv", lambda e, o=self.g_bf(GA1 + oc * 1024, 1024), i=stap:
                       e.tensor_copy(o, i),
                       waits=[pev] + (self.rA1.for_write() if oc == 0 else []),
                       inc=self.s_dv)
            st.read(p)
            self.rA1.wrote(p)

        self.proj_fm(self.wft, 0, 32,
                     lambda oc: self.g_f32(H_FBT, 32)[:, oc:oc + 1], None,
                     GA0, self.rA0, fsink, lora=True, xfm_halves=halves)
        self.add_pass(self.xd, dst, do_ln=(dst is self.xd))

    def p6_attn_b(self):
        E = self.E
        g = self.g_f32
        halves = self.next_halves
        E.emit("gp", lambda e, o=self.g_bf(A1_COS, 1024), i=self.cosf:
               e.dma_start(out=o, in_=i),
               waits=self.rA1.for_write() + self.gstB.for_write() +
               [Pt(self.s_gd, self.s_gd.n), Pt(self.s_gd2, self.s_gd2.n)],
               inc=self.s_misc, inck=16)
        E.emit("gp", lambda e, o=self.g_bf(A1_SIN, 1024), i=self.sinf:
               e.dma_start(out=o, in_=i), inc=self.s_misc, inck=16)
        psin = Pt(self.s_misc, self.s_misc.n)
        self.rA1.wrote(psin)
        rope_guard = [psin]

        def qsink(oc, stap, st, pev):
            rot = self.g_bf(A1_ROT, 1024)
            t1 = self.g_bf(A1_RT1, 1024)
            p1 = E.emit("dv", lambda e, o=rot[0:64, :], i=stap[64:128, :]:
                        e.tensor_scalar_mul(o, i, -1.0),
                        waits=[pev] + rope_guard, inc=self.s_dv)
            p2 = E.emit("dv", lambda e, o=rot[64:128, :], i=stap[0:64, :]:
                        e.tensor_copy(o, i), waits=[pev], inc=self.s_dv)
            p3 = E.emit("dv", lambda e, o=t1, a=stap, b=self.g_bf(A1_COS, 1024):
                        e.tensor_tensor(o, a, b, ALU.mult), waits=[pev], inc=self.s_dv)
            p4 = E.emit("dv", lambda e, o=rot, a=rot, b=self.g_bf(A1_SIN, 1024):
                        e.tensor_tensor(o, a, b, ALU.mult), waits=[p1, p2],
                        inc=self.s_dv)
            p5 = E.emit("dv", lambda e, o=stap, a=t1, b=rot:
                        e.tensor_tensor(o, a, b, ALU.add), waits=[p3, p4],
                        inc=self.s_dv)
            st.wrote(p5)
            self.rA1.read(p5)
            p = E.emit("ac", lambda e, o=self.qfmd[oc * 128:(oc + 1) * 128, :], i=stap:
                       e.dma_start(out=o, in_=i), waits=[p5],
                       inc=self.s_st[oc % 2], inck=16)
            st.read(p)
            self.qst_pts[oc] = [p]

        self.proj_fm(self.wqt, 0, 32, None, SCALE, GA0, self.rA0, qsink,
                     xfm_halves=halves)
        if "qfmd" in self.tap_aps:
            self.tap_dram(self.tap_aps["qfmd"], self.qfmd, list(self.s_st))
        for s in (self.kh + self.vh + self.qh + self.ctxs):
            s.seed(self.rA1.for_write())
        self.p3_attention("b")
        if "ctxd" in self.tap_aps:
            self.tap_dram(self.tap_aps["ctxd"], self.ctxd, list(self.s_stc))
        self.p4_outproj(self.wot, bias=False, xsrc=self.xd, dst=self.xd,
                        do_ln=True)

    def finalize(self):
        E = self.E
        allsems = ([self.s_gd, self.s_sy, self.s_setup, self.s_cxl, self.s_misc,
                    self.s_lor, self.s_ag] + self.s_xt + self.s_w + self.s_k +
                   self.s_v + self.s_q + self.s_st + self.s_stc + self.s_stx)
        last = []
        if self.upto < 4:
            last = [E.emit("sy", lambda e: e.dma_start(out=self.y, in_=self.xq),
                           inc=self.s_sy, inck=16)]
        elif self.upto < 7:
            last = [E.emit("sy", lambda e: e.dma_start(out=self.y, in_=self.xd),
                           waits=list(self.last_store), inc=self.s_sy, inck=16)]
        else:
            last = list(self.last_store)
        if "xd" in self.tap_aps and self.last_store is not None:
            last = [E.emit("sy", lambda e:
                           e.dma_start(out=self.tap_aps["xd"], in_=self.xd),
                           waits=list(self.last_store), inc=self.s_sy, inck=16)]
        waits = last + [Pt(s, s.n) for s in allsems]
        E.emit("sy", lambda e: e.nop(), waits=waits)


def build_program(upto=7, taps=()):
    return Builder(upto=upto, taps=taps).build()



def _tile_w(W):
    dc = W.reshape(32, 128, -1, 128)
    return np.ascontiguousarray(dc.transpose(2, 1, 0, 3).reshape(-1, 128, 4096))


def _rope_tables():
    dim = 128
    inv_freq = (1.0 / (10000.0 ** (np.arange(0, dim, 2, dtype=np.float32)
                                   / np.float32(dim)))).astype(np.float32)
    t = np.arange(BT, dtype=np.float32)
    freqs = np.outer(t, inv_freq).astype(np.float32)
    emb = np.concatenate([freqs, freqs], axis=-1)
    return np.cos(emb).astype(np.float32), np.sin(emb).astype(np.float32)


_UPTO = 7
_TAPS = ()


def kernel(**inputs):
    x = np.ascontiguousarray(inputs["L0_attn_a_x"])
    qkvw = np.ascontiguousarray(inputs["L0_attn_a_attn_qkvw"])
    qkvb = np.asarray(inputs["L0_attn_a_attn_qkvb"])
    ow = np.ascontiguousarray(inputs["L0_attn_a_attn_ow"])
    ob = np.asarray(inputs["L0_attn_a_attn_ob"])
    kc = np.ascontiguousarray(np.asarray(inputs["L0_attn_b_k_cache"]).reshape(512, 16384))
    vc = np.ascontiguousarray(np.asarray(inputs["L0_attn_b_v_cache"]).reshape(512, 16384))
    wq = np.asarray(inputs["L0_attn_b_Wq"])
    wo = np.asarray(inputs["L0_attn_b_Wo"])
    bt = np.ascontiguousarray(np.asarray(inputs["L0_attn_b_block_table"]).astype(np.int32))
    fw = np.asarray(inputs["L0_ffn_weight"])
    fb = np.asarray(inputs["L0_ffn_bias"])
    lA = np.ascontiguousarray(inputs["L0_ffn_lora_A"])
    lB = np.ascontiguousarray(inputs["L0_ffn_lora_B"])
    lsi = np.ascontiguousarray(np.asarray(inputs["L0_ffn_lora_start_indices"]).astype(np.int32))

    qkvw = _tile_w(qkvw)
    ow = _tile_w(ow)
    wqt = _tile_w(np.ascontiguousarray(wq.T))
    wot = _tile_w(np.ascontiguousarray(wo.T))
    wft = _tile_w(np.ascontiguousarray(fw.T))
    qkvbT = np.ascontiguousarray(qkvb.reshape(96, 128).T)
    obT = np.ascontiguousarray(ob.reshape(32, 128).T)
    fbT = np.ascontiguousarray(fb.reshape(32, 128).T)
    cos_t, sin_t = _rope_tables()

    nc = build_program(upto=_UPTO, taps=_TAPS)
    in_maps = []
    for c in range(NCORES):
        b, qi = c // 4, c % 4
        in_maps.append({
            "xq": np.ascontiguousarray(x[b, qi * NT:(qi + 1) * NT]),
            "qkvw": qkvw, "qkvbT": qkvbT, "ow": ow, "obT": obT,
            "wqt": wqt, "wot": wot, "wft": wft, "fbT": fbT,
            "kcache": kc, "vcache": vc, "btab": bt, "lsi": lsi,
            "bval": np.array([[float(b)]], dtype=np.float32),
            "loraA": lA, "loraB": lB,
            "cosf": np.ascontiguousarray(cos_t[qi * NT:(qi + 1) * NT].T),
            "sinf": np.ascontiguousarray(sin_t[qi * NT:(qi + 1) * NT].T),
        })
    res = run_bass_kernel_spmd(nc, in_maps, core_ids=list(range(NCORES)))
    out = np.empty((2, BT, D), dtype=np.float32)
    for c in range(NCORES):
        b, qi = c // 4, c % 4
        out[b, qi * NT:(qi + 1) * NT] = res.results[c]["y"]
    if _TAPS:
        kernel.tap_results = res.results
    return out


# version 09
# speedup vs baseline: 1.5355x; 1.0091x over previous
import numpy as np

import concourse.bass as bass
import concourse.mybir as mybir
from concourse.bass_utils import run_bass_kernel_spmd
from concourse.bass_core_rust import IndirectOffsetOnAxis

F32 = mybir.dt.float32
BF16 = mybir.dt.bfloat16
I32 = mybir.dt.int32
AF = mybir.ActivationFunctionType
ALU = mybir.AluOpType
AX = mybir.AxisListType

D = 4096
NT = 1024
BT = 4096
NH = 32
HD = 128
SCALE = float(1.0 / np.sqrt(128.0))
NCORES = 8



class Sem:
    __slots__ = ("name", "n", "h")

    def __init__(self, name):
        self.name = name
        self.n = 0
        self.h = None


class Pt:
    __slots__ = ("sem", "val")

    def __init__(self, sem, val):
        self.sem = sem
        self.val = val


class Slot:
    __slots__ = ("ws", "r")

    def __init__(self):
        self.ws = []
        self.r = []

    @property
    def w(self):
        return self.ws[-1] if self.ws else None

    def rw(self):
        return list(self.ws)

    def for_write(self):
        return list(self.r) + list(self.ws)

    def wrote(self, pt):
        assert pt is not None
        self.ws = [pt]
        self.r = []

    def add_write(self, pt):
        assert pt is not None
        self.ws.append(pt)

    def read(self, pt):
        if pt is not None:
            self.r.append(pt)

    def seed(self, pts):
        self.r.extend(p for p in pts if p is not None)


class Emitter:
    ENGINES = ("pe", "dv", "ac", "gp", "sy")

    def __init__(self):
        self.q = {e: [] for e in self.ENGINES}
        self.sems = {}
        self.waited = {e: {} for e in self.ENGINES}

    def S(self, name):
        if name not in self.sems:
            self.sems[name] = Sem(name)
        return self.sems[name]

    def emit(self, eng, fn, waits=(), inc=None, inck=1):
        seen = {}
        for pt in waits:
            if pt is None:
                continue
            key = pt.sem.name
            cur = seen.get(key, self.waited[eng].get(key, -1))
            if pt.val > cur:
                seen[key] = pt.val
        ws = []
        for key, val in seen.items():
            self.waited[eng][key] = max(self.waited[eng].get(key, -1), val)
            ws.append((self.sems[key], val))
        pt_out = None
        if inc is not None:
            inc.n += inck
            pt_out = Pt(inc, inc.n)
        self.q[eng].append((ws, fn, inc, inck))
        return pt_out

    def replay(self, block):
        def run(stream):
            def body(eng):
                for ws, fn, inc, inck in stream:
                    for sem, val in ws:
                        eng.wait_ge(sem.h, val)
                    ins = fn(eng)
                    if inc is not None:
                        ins.then_inc(inc.h, inck)
            return body

        block.tensor(run(self.q["pe"]))
        block.vector(run(self.q["dv"]))
        block.scalar(run(self.q["ac"]))
        block.gpsimd(run(self.q["gp"]))
        block.sync(run(self.q["sy"]))



GCOLS = 105536
GA0 = 0
GA1 = 32768
GC = 65536
GD = 81920
GE = 90112
GH = 98304
ST0 = 99392
ST1 = 100416
GD2 = 101440
ST2 = GD2

A1_K0 = GA1 + 0
A1_K1 = GA1 + 4096
A1_V0 = GA1 + 8192
A1_V1 = GA1 + 12288
A1_Q0 = GA1 + 16384
A1_Q1 = GA1 + 17408
A1_CTXS0 = GA1 + 18432
A1_CTXS1 = GA1 + 19456
A1_GST = GA1 + 20480
A1_GSTB = GA1 + 24576
A1_COS = GA1 + 28672
A1_SIN = GA1 + 29696
A1_ROT = GA1 + 30720
A1_RT1 = GA1 + 31744

D_AHT = GD + 0
D_BHT = GD + 512
D_ZSB = GD + 4608

H_ID = GH + 0
H_D0 = GH + 128
H_QKVB = GH + 640
H_OBT = GH + 832
H_FBT = GH + 896
H_ANM = GH + 960
H_ASUM = GH + 966
H_ASS = GH + 990
H_ARC = GH + 996
H_LN = GH + 1008
H_LSI_I = GH + 1040
H_LSI_F = GH + 1048
H_BVAL = GH + 1056
H_TMP = GH + 1058
H_HSEL = GH + 1074
H_BTAB = GH + 1080

LN_S1, LN_NEGM, LN_M2, LN_VAR, LN_LNV, LN_RSTD, LN_SUMSQ = range(7)


def lncol(stat, par):
    return H_LN + stat * 4 + par * 2


class Builder:
    def __init__(self, upto=7, taps=()):
        self.upto = upto
        self.taps = taps
        self.nc = bass.Bass("TRN2", target_bir_lowering=False)
        nc = self.nc
        self.xq = nc.dram_tensor("xq", [NT, D], F32, kind="ExternalInput").ap()
        self.qkvw = nc.dram_tensor("qkvw", [96, 128, 4096], F32, kind="ExternalInput").ap()
        self.qkvbT = nc.dram_tensor("qkvbT", [128, 96], F32, kind="ExternalInput").ap()
        self.ow = nc.dram_tensor("ow", [32, 128, 4096], F32, kind="ExternalInput").ap()
        self.obT = nc.dram_tensor("obT", [128, 32], F32, kind="ExternalInput").ap()
        self.wqt = nc.dram_tensor("wqt", [32, 128, 4096], F32, kind="ExternalInput").ap()
        self.wot = nc.dram_tensor("wot", [32, 128, 4096], F32, kind="ExternalInput").ap()
        self.wft = nc.dram_tensor("wft", [32, 128, 4096], F32, kind="ExternalInput").ap()
        self.fbT = nc.dram_tensor("fbT", [128, 32], F32, kind="ExternalInput").ap()
        self.kcache = nc.dram_tensor("kcache", [512, 16384], F32, kind="ExternalInput").ap()
        self.vcache = nc.dram_tensor("vcache", [512, 16384], F32, kind="ExternalInput").ap()
        self.btab = nc.dram_tensor("btab", [256], I32, kind="ExternalInput").ap()
        self.lsi = nc.dram_tensor("lsi", [4], I32, kind="ExternalInput").ap()
        self.bval = nc.dram_tensor("bval", [1, 1], F32, kind="ExternalInput").ap()
        self.loraA = nc.dram_tensor("loraA", [3, 16, D], F32, kind="ExternalInput").ap()
        self.loraB = nc.dram_tensor("loraB", [3, D, 16], F32, kind="ExternalInput").ap()
        self.cosf = nc.dram_tensor("cosf", [128, NT], F32, kind="ExternalInput").ap()
        self.sinf = nc.dram_tensor("sinf", [128, NT], F32, kind="ExternalInput").ap()
        self.y = nc.dram_tensor("y", [NT, D], F32, kind="ExternalOutput").ap()
        self.xd = nc.dram_tensor("xd", [NT, D], F32, kind="Internal").ap()
        self.kvin = {}
        self.kvag = {}
        for kv in "KV":
            for hf in range(2):
                self.kvin[kv, hf] = nc.dram_tensor(
                    "kvin%s%d" % (kv, hf), [D // 2, NT], BF16, kind="Internal").ap()
                self.kvag[kv, hf] = nc.dram_tensor(
                    "kvag%s%d" % (kv, hf), [2 * D, NT], BF16, kind="Internal").ap()
        self.ag_pts = {}
        self.qfmd = nc.dram_tensor("qfmd", [D, NT], BF16, kind="Internal").ap()
        self.ctxd = nc.dram_tensor("ctxd", [D, NT], BF16, kind="Internal").ap()
        self.kgd = nc.dram_tensor("kgd", [8, BT, HD], BF16, kind="Internal").ap()
        self.vgd = nc.dram_tensor("vgd", [8, BT, HD], BF16, kind="Internal").ap()

        self.tap_aps = {}
        shapes = {"qfmd": ([D, NT], BF16), "ctxd": ([D, NT], BF16),
                  "kgd": ([8, BT, HD], BF16), "vgd": ([8, BT, HD], BF16),
                  "xd": ([NT, D], F32)}
        for t in taps:
            if t == "adbg":
                self.dbg_pbT = nc.dram_tensor("tap_pbT", [128, 32768], BF16,
                                              kind="ExternalOutput").ap()
                self.dbg_probs = nc.dram_tensor("tap_probs", [128, 8192], BF16,
                                                kind="ExternalOutput").ap()
                self.dbg_small = nc.dram_tensor("tap_small", [128, 1024], F32,
                                                kind="ExternalOutput").ap()
                continue
            sh, dt = shapes[t]
            self.tap_aps[t] = nc.dram_tensor("tap_" + t, sh, dt,
                                             kind="ExternalOutput").ap()

        E = Emitter()
        self.E = E
        self.s_pe = E.S("s_pe")
        self.s_dv = E.S("s_dv")
        self.s_ac = E.S("s_ac")
        self.s_gp = E.S("s_gp")
        self.s_sy = E.S("s_sy")
        self.s_sc = E.S("s_sc")
        self.s_gd = E.S("s_gd")
        self.s_setup = E.S("s_setup")
        self.s_xt = [E.S("s_xt0"), E.S("s_xt1")]
        self.s_w = [E.S("s_w0"), E.S("s_w1")]
        self.s_k = [E.S("s_k0"), E.S("s_k1")]
        self.s_v = [E.S("s_v0"), E.S("s_v1")]
        self.s_q = [E.S("s_q0"), E.S("s_q1")]
        self.s_st = [E.S("s_st0"), E.S("s_st1"), E.S("s_st2")]
        self.s_stc = [E.S("s_stc0"), E.S("s_stc1")]
        self.s_stx = [E.S("s_stx0"), E.S("s_stx1")]
        self.s_cxl = E.S("s_cxl")
        self.s_misc = E.S("s_misc")
        self.s_lor = E.S("s_lor")
        self.s_gd2 = E.S("s_gd2")
        self.s_ag = E.S("s_ag")

        self.xt = [Slot(), Slot()]
        self.lnt = [Slot(), Slot()]
        self.lns = [Slot(), Slot()]
        self.wst = [Slot(), Slot()]
        self.xfm = [Slot(), Slot()]
        self.stg = [Slot(), Slot(), Slot()]
        self.psA = Slot()
        self.psB = Slot()
        self.psC = [Slot(), Slot()]
        self.psD = Slot()
        self.kh = [Slot(), Slot()]
        self.vh = [Slot(), Slot()]
        self.qh = [Slot(), Slot()]
        self.ctxs = [Slot(), Slot()]
        self.gst = Slot()
        self.rA0 = Slot()
        self.rA1 = Slot()
        self.identity = Slot()
        self.biases = Slot()
        self.att_nm = [Slot(), Slot(), Slot()]
        self.att_sums = [Slot(), Slot(), Slot()]
        self.att_ss = [Slot(), Slot(), Slot()]
        self.gstB = Slot()
        self.probsx = Slot()
        self._gather_steps = []
        self.kst_pts = [[] for _ in range(32)]
        self.vst_pts = [[] for _ in range(32)]
        self.qst_pts = [[] for _ in range(32)]
        self.lora = {k: Slot() for k in ("AhT", "BhT", "zsb", "hsel")}
        self.btsb = Slot()
        self.last_store = None

    def g_bf(self, col, n, parts=128, p0=0):
        return self.grand[p0:p0 + parts, col:col + n]

    def g_f32(self, col, n, parts=128, p0=0):
        return self.grand_f32[p0:p0 + parts, col // 2:col // 2 + n]

    def g_i32(self, col, n, parts=128, p0=0):
        return self.grand_i32[p0:p0 + parts, col // 2:col // 2 + n]

    def head_guards(self):
        pts = []
        for s in (self.kh + self.vh + self.qh + self.ctxs + [self.gst, self.gstB]):
            pts.extend(s.for_write())
        return pts

    def build(self):
        from contextlib import ExitStack
        nc, E = self.nc, self.E
        with ExitStack() as stack:
            grand_h = stack.enter_context(nc.sbuf_tensor("grand", [128, GCOLS], BF16))
            arena_h = stack.enter_context(nc.psum_tensor("arena", [128, 4096], F32))
            for name in list(E.sems):
                E.sems[name].h = stack.enter_context(nc.semaphore(name))
            self.grand = grand_h[:]
            self.grand_f32 = grand_h[:].bitcast(F32)
            self.grand_i32 = grand_h[:].bitcast(I32)
            self.arena = arena_h[:]
            self.arena_bf = arena_h[:].bitcast(BF16)

            self.p0_setup()
            if self.upto >= 1:
                self.p1_attn_a_proj()
            if self.upto >= 2:
                self.p2_gather()
                if self.upto == 2:
                    self.gather_flush()
            if self.upto >= 3:
                self.rA0.seed(self.xfm[0].for_write())
                for s in (self.kh + self.vh + self.qh + self.ctxs +
                          [self.gst, self.gstB]):
                    s.seed(self.xfm[1].for_write())
                self.probsx.seed(self.stg[2].for_write())
                self.p3_attention("a")
                if self.upto >= 2:
                    self.gather_flush()
                if "ctxd" in self.tap_aps:
                    self.tap_dram(self.tap_aps["ctxd"], self.ctxd, list(self.s_stc))
                if "adbg" in self.taps:
                    E = self.E
                    gates = [Pt(s, s.n) for s in (self.s_dv, self.s_ac, self.s_pe,
                                                  self.s_gp, self.s_sc)]
                    E.emit("sy", lambda e, o=self.dbg_pbT, i=self.g_bf(GA0, 32768):
                           e.dma_start(out=o, in_=i), waits=gates,
                           inc=self.s_sy, inck=16)
                    E.emit("sy", lambda e, o=self.dbg_probs, i=self.g_bf(GD, 8192):
                           e.dma_start(out=o, in_=i), waits=gates,
                           inc=self.s_sy, inck=16)
                    E.emit("sy", lambda e, o=self.dbg_small, i=self.g_f32(GH, 1024):
                           e.dma_start(out=o, in_=i), waits=gates,
                           inc=self.s_sy, inck=16)
            if self.upto >= 4:
                self.p4_outproj(self.ow, bias=True, xsrc=self.xq, dst=self.xd,
                                do_ln=self.upto >= 5)
            if self.upto >= 5:
                self.p5_ffn(dst=self.xd)
            if self.upto >= 6:
                self.p6_attn_b()
            if self.upto >= 7:
                self.p5_ffn(dst=self.y)
            self.finalize()

            blk = stack.enter_context(nc.Block())
            E.replay(blk)
        return nc


    def p0_setup(self):
        E = self.E
        g = self.g_f32
        for dst_col, n, src in ((H_QKVB, 96, self.qkvbT), (H_OBT, 32, self.obT),
                                (H_FBT, 32, self.fbT)):
            pt = E.emit("sy", lambda e, o=g(dst_col, n), i=src:
                        e.dma_start(out=o, in_=i), inc=self.s_setup, inck=16)
        idap = self.g_bf(H_ID, 128)
        p1 = E.emit("gp", lambda e, o=idap: e.iota(
            o, pattern=[[1, 128]], base=0, channel_multiplier=-1,
            allow_small_or_imprecise_dtypes=True), inc=self.s_gp)
        p2 = E.emit("gp", lambda e, o=idap: e.tensor_single_scalar(
            o, o, 0.0, ALU.is_equal), waits=[p1], inc=self.s_gp)
        self.identity.wrote(p2)
        p3 = E.emit("sy", lambda e, o=self.g_i32(H_LSI_I, 4),
                    i=self.lsi.unsqueeze(0).broadcast_to([128, 4]):
                    e.dma_start(out=o, in_=i), inc=self.s_setup, inck=16)
        p4 = E.emit("sy", lambda e, o=g(H_BVAL, 1),
                    i=self.bval.broadcast_to([128, 1]):
                    e.dma_start(out=o, in_=i), inc=self.s_setup, inck=16)
        setup_pt = Pt(self.s_setup, self.s_setup.n)
        self.biases.wrote(setup_pt)
        p4 = setup_pt
        p5 = E.emit("dv", lambda e, o=g(H_LSI_F, 4), i=self.g_i32(H_LSI_I, 4):
                    e.tensor_copy(o, i), waits=[setup_pt], inc=self.s_dv)
        bv = g(H_BVAL, 1)
        lf = g(H_LSI_F, 4)

        def tcol(k):
            return g(H_TMP + 2 * k, 1)

        def tt(o, a, b, op, w):
            return E.emit("dv", lambda e, o=o, a=a, b=b, op=op:
                          e.tensor_tensor(o, a, b, op), waits=w, inc=self.s_dv)

        def ts2(o, a, s1, s2, w):
            return E.emit("dv", lambda e, o=o, a=a, s1=s1, s2=s2:
                          e.tensor_scalar(o, a, s1, s2, ALU.mult, ALU.add),
                          waits=w, inc=self.s_dv)

        c = [tt(tcol(k), bv, lf[:, k:k + 1], ALU.is_lt, [p4, p5]) for k in range(4)]
        pu = ts2(tcol(4), tcol(0), -1.0, 1.0, [c[0]])
        pv1 = tt(tcol(5), tcol(1), tcol(4), ALU.mult, [c[1], pu])
        pw = ts2(tcol(6), tcol(1), -2.0, 2.0, [c[1]])
        pv2 = tt(tcol(7), tcol(5), tcol(6), ALU.add, [pv1, pw])
        pinner = tt(tcol(4), tcol(2), tcol(7), ALU.mult, [c[2], pv2])
        pr2 = ts2(tcol(6), tcol(3), -1.0, 4.0, [c[3]])
        pnc2 = ts2(tcol(5), tcol(2), -1.0, 1.0, [c[2]])
        po2 = tt(tcol(6), tcol(5), tcol(6), ALU.mult, [pr2, pnc2])
        pres = tt(tcol(7), tcol(4), tcol(6), ALU.add, [pinner, po2])
        hs = None
        for a in range(3):
            hs = E.emit("dv", lambda e, o=g(H_HSEL + 2 * a, 1), i=tcol(7), v=float(2 + a):
                        e.tensor_single_scalar(o, i, v, ALU.is_equal),
                        waits=[pres], inc=self.s_dv)
        self.lora["hsel"].wrote(hs)

    def ln_pass(self, src, row0, ntiles, xfm_col, xfm_slot, gate=()):
        E = self.E
        g = self.g_f32
        last_evac = None
        half0 = None
        first = True
        for j in range(ntiles):
            par = j % 2
            xt = self.xt[par]
            xtap = g(GC + par * 8192, 4096)
            lns = self.lns[par]
            r0 = row0 + j * 128
            pl = E.emit("sy" if par == 0 else "gp",
                        lambda e, o=xtap, i=src[r0:r0 + 128, :]:
                        e.dma_start(out=o, in_=i),
                        waits=xt.for_write() + (gate or []),
                        inc=self.s_xt[par], inck=16)
            xt.wrote(pl)
            lnt = self.lnt[par]
            lntap = self.g_bf(GD + par * 4096, 4096)
            c_sq, c_s1, c_nm, c_m2, c_v, c_lnv, c_rstd = (
                g(lncol(LN_SUMSQ, par), 1), g(lncol(LN_S1, par), 1),
                g(lncol(LN_NEGM, par), 1), g(lncol(LN_M2, par), 1),
                g(lncol(LN_VAR, par), 1), g(lncol(LN_LNV, par), 1),
                g(lncol(LN_RSTD, par), 1))
            psq = E.emit("ac", lambda e, o=lntap, i=xtap, a=c_sq:
                         e.activation(o, i, AF.Square, bias=0.0, scale=1.0, accum_out=a),
                         waits=[xt.w] + lnt.for_write() + lns.for_write(), inc=self.s_ac)
            ps1 = E.emit("dv", lambda e, o=c_s1, i=xtap:
                         e.reduce_sum(o, i, axis=AX.X), waits=[xt.w], inc=self.s_dv)
            pnm = E.emit("dv", lambda e, o=c_nm, i=c_s1:
                         e.tensor_scalar_mul(o, i, -1.0 / 4096.0),
                         waits=[ps1], inc=self.s_dv)
            pm2 = E.emit("dv", lambda e, o=c_m2, i=c_nm:
                         e.tensor_scalar(o, i, i, -1e-5, ALU.mult, ALU.add),
                         waits=[pnm], inc=self.s_dv)
            pvar = E.emit("dv", lambda e, o=c_v, a=c_sq, b=c_m2:
                          e.scalar_tensor_tensor(o, a, 1.0 / 4096.0, b,
                                                 ALU.mult, ALU.subtract),
                          waits=[psq, pm2], inc=self.s_dv)
            plnv = E.emit("ac", lambda e, o=c_lnv, i=c_v:
                          e.activation(o, i, AF.Ln, bias=0.0, scale=1.0),
                          waits=[pvar], inc=self.s_ac)
            prstd = E.emit("ac", lambda e, o=c_rstd, i=c_lnv:
                           e.activation(o, i, AF.Exp, bias=0.0, scale=-0.5),
                           waits=[plnv], inc=self.s_ac)
            pnorm = E.emit("dv", lambda e, o=lntap, i=xtap, s1=c_nm, s2=c_rstd:
                           e.tensor_scalar(o, i, s1, s2, ALU.add, ALU.mult),
                           waits=[prstd, pnm, xt.w, psq], inc=self.s_dv)
            lnt.wrote(pnorm)
            lns.read(pnorm)
            xt.read(pnorm)
            for bi in range(4):
                pc = self.psC[bi % 2]
                pcap = self.arena_bf[:, 4096 + (bi % 2) * 1024:
                                     4096 + (bi % 2) * 1024 + 1024]
                gpt = None
                for jj in range(8):
                    dc = bi * 8 + jj
                    w = ([pnorm, self.identity.w] + pc.for_write()) if jj == 0 else []
                    gpt = E.emit("pe", lambda e, o=pcap[:, jj * 128:jj * 128 + 128],
                                 i=lntap[:, dc * 128:dc * 128 + 128],
                                 idn=self.g_bf(H_ID, 128):
                                 e.transpose(o, i, idn),
                                 waits=w, inc=self.s_pe if jj == 7 else None)
                pc.wrote(gpt)
                lnt.read(gpt)
                base = xfm_col + (bi * 8) * 1024 + j * 128
                dst = self.grand[:, base: base + 8 * 1024].rearrange(
                    "p (c t) -> p c t", c=8)[:, :, 0:128]
                pev = E.emit("dv", lambda e, o=dst,
                             i=pcap.rearrange("p (c t) -> p c t", c=8):
                             e.tensor_copy(o, i),
                             waits=[gpt] + (xfm_slot.for_write() if first else []),
                             inc=self.s_dv)
                first = False
                pc.read(pev)
                last_evac = pev
            if j == ntiles // 2 - 1:
                half0 = last_evac
        xfm_slot.wrote(last_evac)
        return [half0, last_evac]

    def proj_fm(self, wdram, wchunk0, nchunks, bias_fn, scale, xfm_col, xfm_slot, sink,
                lora=False, xfm_halves=None):
        E = self.E
        for oc in range(nchunks):
            ws = self.wst[oc % 2]
            wsap = self.g_bf(GE + (oc % 2) * 4096, 4096)
            pw = E.emit("gp", lambda e, o=wsap, i=wdram[wchunk0 + oc]:
                        e.dma_start(out=o, in_=i),
                        waits=ws.for_write(), inc=self.s_w[oc % 2], inck=16)
            ws.wrote(pw)
            ps = self.psA if (oc % 2 == 0) else self.psB
            psap = self.arena[:, (oc % 2) * 1024:(oc % 2) * 1024 + 1024]
            gpt = None
            for dc in range(32):
                for sp in range(2):
                    if dc == 0 and sp == 0:
                        w = [pw, xfm_halves[0] if xfm_halves else xfm_slot.w] \
                            + ps.for_write()
                    elif dc == 0 and sp == 1 and xfm_halves:
                        w = [xfm_halves[1]]
                    else:
                        w = []
                    is_last = (dc == 31 and sp == 1) and not lora
                    gpt = E.emit("pe", lambda e, o=psap[:, sp * 512:sp * 512 + 512],
                                 l=wsap[:, dc * 128:dc * 128 + 128],
                                 r=self.grand[:, xfm_col + dc * 1024 + sp * 512:
                                              xfm_col + dc * 1024 + sp * 512 + 512],
                                 st=(dc == 0), sp_=(dc == 31 and not lora):
                                 e.matmul(o, l, r, start=st, stop=sp_),
                                 waits=w, inc=self.s_pe if is_last else None)
            if lora:
                for sp in range(2):
                    is_last = sp == 1
                    gpt = E.emit("pe", lambda e, o=psap[:, sp * 512:sp * 512 + 512],
                                 l=self.g_bf(D_BHT + oc * 128, 128, parts=16),
                                 r=self.g_bf(D_ZSB + sp * 512, 512, parts=16):
                                 e.matmul(o, l, r, start=False, stop=True),
                                 waits=([self.lora["BhT"].w, self.lora["zsb"].w]
                                        if sp == 0 else []),
                                 inc=self.s_pe if is_last else None)
            ps.wrote(gpt)
            ws.read(gpt)
            xfm_slot.read(gpt)
            if lora:
                self.lora["BhT"].read(gpt)
                self.lora["zsb"].read(gpt)
            st = self.stg[oc % 2]
            stap = self.g_bf(ST0 + (oc % 2) * 1024, 1024)
            bap = bias_fn(oc) if bias_fn is not None else None
            if bap is not None and scale is not None:
                pev = E.emit("dv", lambda e, o=stap, i=psap, b=bap, s=scale:
                             e.tensor_scalar(o, i, b, s, ALU.add, ALU.mult),
                             waits=[gpt, self.biases.w] + st.for_write(), inc=self.s_dv)
            elif bap is not None:
                pev = E.emit("dv", lambda e, o=stap, i=psap, b=bap:
                             e.tensor_scalar(o, i, b, None, ALU.add),
                             waits=[gpt, self.biases.w] + st.for_write(), inc=self.s_dv)
            elif scale is not None:
                pev = E.emit("dv", lambda e, o=stap, i=psap, s=scale:
                             e.tensor_scalar_mul(o, i, s),
                             waits=[gpt] + st.for_write(), inc=self.s_dv)
            else:
                pev = E.emit("dv", lambda e, o=stap, i=psap: e.tensor_copy(o, i),
                             waits=[gpt] + st.for_write(), inc=self.s_dv)
            ps.read(pev)
            st.wrote(pev)
            sink(oc, stap, st, pev)

    def qkvb_col(self, chunk0):
        def f(oc):
            return self.g_f32(H_QKVB, 96)[:, chunk0 + oc:chunk0 + oc + 1]
        return f

    def p1_attn_a_proj(self):
        E = self.E
        xfm_slot = self.xfm[0]
        halves = self.ln_pass(self.xq, 0, 8, GA0, xfm_slot)

        def kvsink(dst):
            def sink(oc, stap, st, pev):
                p = E.emit("ac", lambda e,
                           o=dst[oc * 128:(oc + 1) * 128, :],
                           i=stap: e.dma_start(out=o, in_=i),
                           waits=[pev], inc=self.s_st[oc % 2], inck=16)
                st.read(p)
            return sink

        for kv, base in (("K", 32), ("V", 64)):
            for hf in range(2):
                self.proj_fm(self.qkvw, base + hf * 16, 16,
                             self.qkvb_col(base + hf * 16), None,
                             GA0, xfm_slot, kvsink(self.kvin[kv, hf]),
                             xfm_halves=halves)
                kv_done = [Pt(s, s.n) for s in self.s_st]
                self.ag_pts[kv, hf] = E.emit(
                    "gp", lambda e, i=self.kvin[kv, hf], o=self.kvag[kv, hf]:
                    e.collective_compute(
                        "AllGather", ALU.bypass,
                        replica_groups=[[0, 1, 2, 3], [4, 5, 6, 7]],
                        ins=[i.opt()], outs=[o.opt()]),
                    waits=kv_done, inc=self.s_ag, inck=1)

        def qsink(oc, stap, st, pev):
            p = E.emit("ac", lambda e, o=self.qfmd[oc * 128:(oc + 1) * 128, :],
                       i=stap: e.dma_start(out=o, in_=i),
                       waits=[pev], inc=self.s_st[oc % 2], inck=16)
            st.read(p)
            self.qst_pts[oc].append(p)
        self.proj_fm(self.qkvw, 0, 32, self.qkvb_col(0), SCALE,
                     GA0, xfm_slot, qsink, xfm_halves=halves)
        for t in ("qfmd",):
            if t in self.tap_aps:
                self.tap_dram(self.tap_aps[t], getattr(self, t), list(self.s_st))

    def tap_dram(self, dst, src, gate_sem):
        sems = gate_sem if isinstance(gate_sem, list) else [gate_sem]
        self.E.emit("sy", lambda e, o=dst, i=src: e.dma_start(out=o, in_=i),
                    waits=[Pt(s, s.n) for s in sems], inc=self.s_sy, inck=16)

    def p2_gather(self):
        E = self.E
        pb = E.emit("gp", lambda e, o=self.g_i32(H_BTAB, 2),
                    i=self.btab.rearrange("(a p) -> p a", p=128):
                    e.dma_start(out=o, in_=i, allow_slow_non_contiguous=True),
                    inc=self.s_gd, inck=16)
        self.btsb.wrote(pb)
        steps = []
        for src_, dst in ((self.kcache, self.kgd), (self.vcache, self.vgd)):
            for h in range(8):
                for half in range(2):
                    steps.append((src_, dst, h, half))

        def make_step(idx):
            src_, dst, h, half = steps[idx]
            par = idx % 2
            gslot = self.gst if par == 0 else self.gstB
            gsem = self.s_gd if par == 0 else self.s_gd2
            gstap = self.g_f32(A1_GST if par == 0 else A1_GSTB, 2048)

            def emit():
                pg = E.emit("gp", lambda e, o=gstap, i=src_, eo=h * 2048,
                            ix=self.g_i32(H_BTAB, 2)[:, half:half + 1]:
                            e.indirect_dma_start(o, None, i,
                                                 IndirectOffsetOnAxis(ix, 0),
                                                 element_offset=eo),
                            waits=[pb] + gslot.for_write(), inc=gsem, inck=16)
                gslot.wrote(pg)
                ps = E.emit("gp", lambda e,
                            o=dst[h, half * 2048:(half + 1) * 2048, :]
                            .rearrange("(p r) d -> p r d", p=128),
                            i=gstap.rearrange("p (r d) -> p r d", r=16):
                            e.dma_start(out=o, in_=i),
                            waits=[pg], inc=gsem, inck=16)
                gslot.read(ps)
            return emit

        self._gather_steps = [make_step(i) for i in range(len(steps))]

    def gather_tick(self, n=2):
        for _ in range(n):
            if self._gather_steps:
                self._gather_steps.pop(0)()

    def gather_flush(self):
        while self._gather_steps:
            self._gather_steps.pop(0)()
        self.gather_gate = [Pt(self.s_gd, self.s_gd.n),
                            Pt(self.s_gd2, self.s_gd2.n)]
        for t in ("kgd", "vgd"):
            if t in self.tap_aps:
                self.tap_dram(self.tap_aps[t], getattr(self, t), self.s_gd)

    def p3_attention(self, mode):
        E = self.E
        heads = [None] * NH

        def inplace_tm_fm_build(S, Sap, gate_pt):
            for bi in range(4):
                pc = self.psC[bi % 2]
                pcap = self.arena_bf[:, 4096 + (bi % 2) * 1024:
                                     4096 + (bi % 2) * 1024 + 1024]
                gpt = None
                for jj in range(8):
                    cc = bi * 8 + jj
                    w = ([gate_pt, self.identity.w] + pc.for_write()) if jj == 0 else []
                    gpt = E.emit("pe",
                                 lambda e, o=pcap[:, jj * 128:jj * 128 + 128],
                                 i=Sap[:, cc * 128:cc * 128 + 128],
                                 idn=self.g_bf(H_ID, 128):
                                 e.transpose(o, i, idn), waits=w,
                                 inc=self.s_pe if jj == 7 else None)
                pc.wrote(gpt)
                S.read(gpt)
                pev = E.emit("dv", lambda e,
                             o=Sap[:, bi * 1024:bi * 1024 + 1024],
                             i=pcap: e.tensor_copy(o, i),
                             waits=[gpt], inc=self.s_dv)
                pc.read(pev)
                if bi == 0:
                    S.wrote(pev)
                else:
                    S.add_write(pev)

        pending_builds = []

        def load_head(h):
            if mode == "a":
                K, Kap = self.kh[h % 2], self.g_bf(A1_K0 + (h % 2) * 4096, 4096)
                V, Vap = self.vh[h % 2], self.g_bf(A1_V0 + (h % 2) * 4096, 4096)
                agvK = self.kvag["K", h // 16].rearrange(
                    "(j c p) t -> c p j t", j=4, p=128)
                agvV = self.kvag["V", h // 16].rearrange(
                    "(j c p) t -> c p j t", j=4, p=128)
                pk = E.emit("sy", lambda e, o=Kap.rearrange("p (j t) -> p j t", j=4),
                            i=agvK[h % 16]:
                            e.dma_start(out=o, in_=i),
                            waits=K.for_write() + [self.ag_pts["K", h // 16]],
                            inc=self.s_k[h % 2], inck=16)
                K.wrote(pk)
                pv = E.emit("gp", lambda e, o=Vap.rearrange("p (j t) -> p j t", j=4),
                            i=agvV[h % 16]:
                            e.dma_start(out=o, in_=i),
                            waits=V.for_write() + [self.ag_pts["V", h // 16]],
                            inc=self.s_v[h % 2], inck=16)
                V.wrote(pv)
                pending_builds.append(
                    lambda V=V, Vap=Vap, pv=pv: inplace_tm_fm_build(V, Vap, pv))
            else:
                kvh = h // 4
                K, Kap = self.kh[kvh % 2], self.g_bf(A1_K0 + (kvh % 2) * 4096, 4096)
                V, Vap = self.vh[kvh % 2], self.g_bf(A1_V0 + (kvh % 2) * 4096, 4096)
                if h % 4 == 0:
                    pk = E.emit("sy", lambda e,
                                o=Kap.rearrange("p (c d) -> p c d", c=32),
                                i=self.kgd[kvh].rearrange("(c p) d -> p c d", p=128):
                                e.dma_start(out=o, in_=i),
                                waits=K.for_write() + self.gather_gate,
                                inc=self.s_k[kvh % 2], inck=16)
                    K.wrote(pk)
                    pending_builds.append(
                        lambda K=K, Kap=Kap, pk=pk: inplace_tm_fm_build(K, Kap, pk))
                    pv = E.emit("sy", lambda e,
                                o=Vap.rearrange("p (c d) -> p c d", c=32),
                                i=self.vgd[kvh].rearrange("(c p) d -> p c d", p=128):
                                e.dma_start(out=o, in_=i),
                                waits=V.for_write() + self.gather_gate,
                                inc=self.s_v[kvh % 2], inck=16)
                    V.wrote(pv)
            Q, Qap = self.qh[h % 2], self.g_bf(A1_Q0 + (h % 2) * 1024, 1024)
            pq = E.emit("sy", lambda e, o=Qap, i=self.qfmd[h * 128:(h + 1) * 128, :]:
                        e.dma_start(out=o, in_=i),
                        waits=Q.for_write() + self.qst_pts[h],
                        inc=self.s_q[h % 2], inck=16)
            Q.wrote(pq)
            heads[h] = (K, Kap, V, Vap, Q, Qap)

        def flush_builds():
            while pending_builds:
                pending_builds.pop(0)()

        def emit_ctx(h):
            K, Kap, V, Vap, Q, Qap = heads[h]
            ctxap = self.arena[:, 3072:4096]
            gpt = None
            for c in range(32):
                for sp in range(2):
                    w = (self.rA0.rw() + V.rw() + self.psD.for_write()) \
                        if (c == 0 and sp == 0) else []
                    is_last = c == 31 and sp == 1
                    gpt = E.emit("pe", lambda e, o=ctxap[:, sp * 512:sp * 512 + 512],
                                 l=Vap[:, c * 128:c * 128 + 128],
                                 r=self.grand[:, GA0 + c * 1024 + sp * 512:
                                              GA0 + c * 1024 + sp * 512 + 512],
                                 st=(c == 0), spp=(c == 31):
                                 e.matmul(o, l, r, start=st, stop=spp),
                                 waits=w, inc=self.s_pe if is_last else None)
            self.psD.wrote(gpt)
            self.rA0.read(gpt)
            V.read(gpt)
            cs = self.ctxs[h % 2]
            csap = self.g_bf(A1_CTXS0 + (h % 2) * 1024, 1024)
            pev = E.emit("dv", lambda e, o=csap, i=ctxap: e.tensor_copy(o, i),
                         waits=[gpt] + cs.for_write(), inc=self.s_dv)
            self.psD.read(pev)
            cs.wrote(pev)
            pst = E.emit("ac", lambda e, o=self.ctxd[h * 128:(h + 1) * 128, :], i=csap:
                         e.dma_start(out=o, in_=i), waits=[pev],
                         inc=self.s_stc[h % 2], inck=16)
            cs.read(pst)

        NU = NH * 8
        unit_state = [None] * NU
        for gu in range(NU + 2):
            h, i = divmod(gu, 8)
            if gu == 0:
                load_head(0)
                load_head(1)
                if mode == "b":
                    flush_builds()
            if gu < NU:
                self.unit_quarters(gu, 0, 2, heads[h], unit_state)
            if gu >= 1 and gu - 1 < NU:
                self.unit_norm(gu - 1, unit_state)
            if gu >= 2:
                self.unit_T(gu - 2, unit_state, (0, 1))
            if gu < NU:
                self.unit_quarters(gu, 2, 4, heads[h], unit_state)
            if gu >= 2:
                self.unit_T(gu - 2, unit_state, (2, 3))
                if (gu - 2) % 8 == 3:
                    flush_builds()
                if (gu - 2) % 8 == 7:
                    hh = (gu - 2) // 8
                    emit_ctx(hh)
                    if hh + 2 < NH:
                        load_head(hh + 2)
                    if mode == "a":
                        self.gather_tick(2)

    def probs3(self, par):
        slots = [self.lnt[0], self.lnt[1], self.probsx]
        cols = [GD, GD + 4096, GD2]
        return slots[par], self.g_bf(cols[par], 4096)

    def unit_quarters(self, gu, k0, k1, head, unit_state):
        E = self.E
        g = self.g_f32
        K, Kap, V, Vap, Q, Qap = head
        i = gu % 8
        par = gu % 3
        if unit_state[gu] is None:
            unit_state[gu] = {"par": par}
        stt = unit_state[gu]
        probs, probsap = self.probs3(par)
        stt["probs"], stt["probsap"] = probs, probsap
        nm_ap = g(H_ANM + 2 * par, 1)
        sums_ap = g(H_ASUM + 8 * par, 4)
        stt["nm_ap"], stt["sums_ap"] = nm_ap, sums_ap
        for k in range(k0, k1):
            ps = self.psA if k % 2 == 0 else self.psB
            psap = self.arena[:, (k % 2) * 1024:(k % 2) * 1024 + 1024]
            gpt = None
            for sp in range(2):
                w = ([K.w, Q.w] + ps.for_write()) if sp == 0 else []
                gpt = E.emit("pe", lambda e, o=psap[:, sp * 512:sp * 512 + 512],
                             l=Qap[:, i * 128:i * 128 + 128],
                             r=Kap[:, k * 1024 + sp * 512:k * 1024 + sp * 512 + 512]:
                             e.matmul(o, l, r, start=True, stop=True),
                             waits=w, inc=self.s_pe if sp == 1 else None)
            ps.wrote(gpt)
            K.read(gpt)
            Q.read(gpt)
            if k == 0:
                nm = self.att_nm[par]
                pmx = E.emit("dv", lambda e, o=nm_ap, i=self.arena[:, 0:1024:8]:
                             e.reduce_max(o, i, axis=AX.X, negate=True),
                             waits=[gpt] + nm.for_write(), inc=self.s_dv)
                nm.wrote(pmx)
                stt["pmx"] = pmx
            sums = self.att_sums[par]
            pex = E.emit("ac", lambda e, o=probsap[:, k * 1024:k * 1024 + 1024],
                         i=psap, b=nm_ap, a=sums_ap[:, k:k + 1]:
                         e.activation(o, i, AF.Exp, bias=b, scale=1.0, accum_out=a),
                         waits=[stt["pmx"], gpt] +
                               (probs.for_write() if k == 0 else []) +
                               (sums.for_write() if k == 0 else []),
                         inc=self.s_ac)
            ps.read(pex)
            if k == 0:
                probs.wrote(pex)
                sums.wrote(pex)
            else:
                probs.add_write(pex)
                sums.add_write(pex)
            if k == 3:
                stt["exp_last"] = pex
                self.att_nm[par].read(pex)

    def unit_norm(self, gu, unit_state):
        E = self.E
        g = self.g_f32
        stt = unit_state[gu]
        par = stt["par"]
        probs, probsap = stt["probs"], stt["probsap"]
        ss = self.att_ss[par]
        pss = E.emit("dv", lambda e, o=g(H_ASS + 2 * par, 1), i=stt["sums_ap"]:
                     e.reduce_sum(o, i, axis=AX.X),
                     waits=[stt["exp_last"]] + ss.for_write(), inc=self.s_dv)
        prc = E.emit("dv", lambda e, o=g(H_ARC + 2 * par, 1), i=g(H_ASS + 2 * par, 1):
                     e.reciprocal(o, i), waits=[pss], inc=self.s_dv)
        ss.wrote(prc)
        self.att_sums[par].read(pss)
        pr = E.emit("dv", lambda e, o=probsap, s=g(H_ARC + 2 * par, 1):
                    e.tensor_scalar_mul(o, o, s),
                    waits=[prc, probs.w], inc=self.s_dv)
        probs.wrote(pr)
        ss.read(pr)
        stt["pr"] = pr

    def unit_T(self, gu, unit_state, bis):
        E = self.E
        stt = unit_state[gu]
        i = gu % 8
        probs, probsap = stt["probs"], stt["probsap"]
        pr = stt["pr"]
        for bi in bis:
            pc = self.psC[bi % 2]
            pcap = self.arena_bf[:, 4096 + (bi % 2) * 1024:
                                 4096 + (bi % 2) * 1024 + 1024]
            gpt = None
            for jj in range(8):
                cc = bi * 8 + jj
                w = ([pr, self.identity.w] + pc.for_write()) if jj == 0 else []
                gpt = E.emit("pe", lambda e, o=pcap[:, jj * 128:jj * 128 + 128],
                             i=probsap[:, cc * 128:cc * 128 + 128],
                             dk=self.g_bf(H_ID, 128):
                             e.transpose(o, i, dk), waits=w,
                             inc=self.s_pe if jj == 7 else None)
            pc.wrote(gpt)
            probs.read(gpt)
            base = GA0 + (bi * 8) * 1024 + i * 128
            dst = self.grand[:, base: base + 8 * 1024].rearrange(
                "p (c t) -> p c t", c=8)[:, :, 0:128]
            pev = E.emit("dv",
                         lambda e, o=dst, i=pcap.rearrange("p (c t) -> p c t", c=8):
                         e.tensor_copy(o, i),
                         waits=[gpt] + (self.rA0.for_write() if (i == 0 and bi == 0)
                                        else []),
                         inc=self.s_dv)
            pc.read(pev)
            if i == 0 and bi == 0:
                self.rA0.wrote(pev)
            else:
                self.rA0.add_write(pev)

    def p4_outproj(self, wdram, bias, xsrc, dst, do_ln=False):
        E = self.E
        pl = E.emit("sy", lambda e,
                    o=self.grand[:, GA0:GA0 + 32768].rearrange("p (c t) -> p c t", c=32),
                    i=self.ctxd.rearrange("(c p) t -> p c t", p=128):
                    e.dma_start(out=o, in_=i),
                    waits=self.rA0.for_write() + [Pt(s, s.n) for s in self.s_stc],
                    inc=self.s_cxl, inck=16)
        self.rA0.wrote(pl)
        bias_fn = (lambda oc: self.g_f32(H_OBT, 32)[:, oc:oc + 1]) if bias else None
        guards = self.head_guards() + self.rA1.for_write()

        def osink(oc, stap, st, pev):
            p = E.emit("dv", lambda e, o=self.g_bf(GA1 + oc * 1024, 1024), i=stap:
                       e.tensor_copy(o, i),
                       waits=[pev] + (guards if oc == 0 else []),
                       inc=self.s_dv)
            st.read(p)
            self.rA1.wrote(p)

        self.proj_fm(wdram, 0, 32, bias_fn, None, GA0, self.rA0, osink)
        self.add_pass(xsrc, dst, do_ln=do_ln)

    def add_pass(self, xsrc, dst, do_ln=False):
        E = self.E
        g = self.g_f32
        half0 = None
        last_evac = None
        first = True
        for j in range(8):
            par = j % 2
            xt = self.xt[par]
            xtap = g(GC + par * 8192, 4096)
            pl = E.emit("sy" if par == 0 else "gp",
                        lambda e, o=xtap, i=xsrc[j * 128:(j + 1) * 128, :]:
                        e.dma_start(out=o, in_=i),
                        waits=xt.for_write() + (self.last_store or []),
                        inc=self.s_xt[par], inck=16)
            xt.wrote(pl)
            addpts = []
            for half in range(2):
                if half == 0:
                    slots = [self.psC[0], self.psC[1]]
                else:
                    slots = [self.psD]
                pap = self.arena_bf[:, (4 + half * 2) * 1024:
                                    (4 + half * 2) * 1024 + 2048]
                gpt = None
                for ocl in range(16):
                    oc = half * 16 + ocl
                    if ocl == 0:
                        w = [self.rA1.w, self.identity.w]
                        for s in slots:
                            w += s.for_write()
                    else:
                        w = []
                    gpt = E.emit("pe", lambda e, o=pap[:, ocl * 128:ocl * 128 + 128],
                                 i=self.g_bf(GA1 + oc * 1024 + j * 128, 128),
                                 idn=self.g_bf(H_ID, 128):
                                 e.transpose(o, i, idn), waits=w,
                                 inc=self.s_pe if ocl == 15 else None)
                for s in slots:
                    s.wrote(gpt)
                self.rA1.read(gpt)
                padd = E.emit("dv", lambda e, o=xtap[:, half * 2048:half * 2048 + 2048],
                              i=pap: e.tensor_tensor(o, o, i, ALU.add),
                              waits=[gpt, xt.w], inc=self.s_dv)
                for s in slots:
                    s.read(padd)
                addpts.append(padd)
            pst = E.emit("ac", lambda e, o=dst[j * 128:(j + 1) * 128, :], i=xtap:
                         e.dma_start(out=o, in_=i), waits=addpts,
                         inc=self.s_stx[par], inck=16)
            xt.read(pst)
            if do_ln:
                lnt = self.lnt[par]
                lns = self.lns[par]
                lntap = self.g_bf(GD + par * 4096, 4096)
                c_sq, c_s1, c_nm, c_m2, c_v, c_lnv, c_rstd = (
                    g(lncol(LN_SUMSQ, par), 1), g(lncol(LN_S1, par), 1),
                    g(lncol(LN_NEGM, par), 1), g(lncol(LN_M2, par), 1),
                    g(lncol(LN_VAR, par), 1), g(lncol(LN_LNV, par), 1),
                    g(lncol(LN_RSTD, par), 1))
                psq = E.emit("ac", lambda e, o=lntap, i=xtap, a=c_sq:
                             e.activation(o, i, AF.Square, bias=0.0, scale=1.0,
                                          accum_out=a),
                             waits=addpts + lnt.for_write() + lns.for_write(),
                             inc=self.s_ac)
                ps1 = E.emit("dv", lambda e, o=c_s1, i=xtap:
                             e.reduce_sum(o, i, axis=AX.X), waits=addpts,
                             inc=self.s_dv)
                pnm = E.emit("dv", lambda e, o=c_nm, i=c_s1:
                             e.tensor_scalar_mul(o, i, -1.0 / 4096.0),
                             waits=[ps1], inc=self.s_dv)
                pm2 = E.emit("dv", lambda e, o=c_m2, i=c_nm:
                             e.tensor_scalar(o, i, i, -1e-5, ALU.mult, ALU.add),
                             waits=[pnm], inc=self.s_dv)
                pvar = E.emit("dv", lambda e, o=c_v, a=c_sq, b=c_m2:
                              e.scalar_tensor_tensor(o, a, 1.0 / 4096.0, b,
                                                     ALU.mult, ALU.subtract),
                              waits=[psq, pm2], inc=self.s_dv)
                plnv = E.emit("ac", lambda e, o=c_lnv, i=c_v:
                              e.activation(o, i, AF.Ln, bias=0.0, scale=1.0),
                              waits=[pvar], inc=self.s_ac)
                prstd = E.emit("ac", lambda e, o=c_rstd, i=c_lnv:
                               e.activation(o, i, AF.Exp, bias=0.0, scale=-0.5),
                               waits=[plnv], inc=self.s_ac)
                pnorm = E.emit("dv", lambda e, o=lntap, i=xtap, s1=c_nm, s2=c_rstd:
                               e.tensor_scalar(o, i, s1, s2, ALU.add, ALU.mult),
                               waits=[prstd, pnm, psq] + addpts, inc=self.s_dv)
                lnt.wrote(pnorm)
                lns.read(pnorm)
                xt.read(pnorm)
                for bi in range(4):
                    pc = self.psC[bi % 2]
                    pcap = self.arena_bf[:, 4096 + (bi % 2) * 1024:
                                         4096 + (bi % 2) * 1024 + 1024]
                    gpt = None
                    for jj in range(8):
                        dc = bi * 8 + jj
                        w = ([pnorm, self.identity.w] + pc.for_write()) \
                            if jj == 0 else []
                        gpt = E.emit("pe",
                                     lambda e, o=pcap[:, jj * 128:jj * 128 + 128],
                                     i=lntap[:, dc * 128:dc * 128 + 128],
                                     idn=self.g_bf(H_ID, 128):
                                     e.transpose(o, i, idn),
                                     waits=w, inc=self.s_pe if jj == 7 else None)
                    pc.wrote(gpt)
                    lnt.read(gpt)
                    base = GA0 + (bi * 8) * 1024 + j * 128
                    dfm = self.grand[:, base: base + 8 * 1024].rearrange(
                        "p (c t) -> p c t", c=8)[:, :, 0:128]
                    pev = E.emit("dv", lambda e, o=dfm,
                                 i=pcap.rearrange("p (c t) -> p c t", c=8):
                                 e.tensor_copy(o, i),
                                 waits=[gpt] + (self.rA0.for_write()
                                                if first else []),
                                 inc=self.s_dv)
                    first = False
                    pc.read(pev)
                    last_evac = pev
                if j == 3:
                    half0 = last_evac
        self.last_store = [Pt(s, s.n) for s in self.s_stx]
        if do_ln:
            self.rA0.wrote(last_evac)
            self.next_halves = [half0, last_evac]

    def p5_ffn(self, dst):
        E = self.E
        g = self.g_f32
        halves = self.next_halves
        hw = self.lora["hsel"].w
        for a in range(3):
            E.emit("gp", lambda e, o=self.g_bf(GA1 + a * 4096, 4096, parts=16),
                   i=self.loraA[a]:
                   e.dma_start(out=o, in_=i),
                   waits=(self.rA1.for_write() + self.head_guards())
                   if a == 0 else [],
                   inc=self.s_lor, inck=16)
        for a in range(3):
            E.emit("gp", lambda e,
                   o=self.g_bf(GA1 + 16384 + a * 512, 512).rearrange("p (c r) -> p c r", c=32),
                   i=self.loraB[a].rearrange("(c p) r -> p c r", p=128):
                   e.dma_start(out=o, in_=i), waits=[],
                   inc=self.s_lor, inck=16)
        lor_pt = Pt(self.s_lor, self.s_lor.n)
        apts = [lor_pt, lor_pt, lor_pt]
        ah = self.g_bf(GA1 + 3 * 4096, 4096, parts=16)
        p = E.emit("dv", lambda e, o=ah, i=self.g_bf(GA1 + 0, 4096, parts=16),
                   s=g(H_HSEL, 3, parts=16)[:, 0:1]:
                   e.tensor_scalar_mul(o, i, s), waits=[apts[0], hw], inc=self.s_dv)
        for a in (1, 2):
            p = E.emit("dv", lambda e, o=ah, i=self.g_bf(GA1 + a * 4096, 4096, parts=16),
                       s=g(H_HSEL, 3, parts=16)[:, a:a + 1]:
                       e.scalar_tensor_tensor(o, i, s, o, ALU.mult, ALU.add),
                       waits=[apts[a], p], inc=self.s_dv)
        pah = p
        self.rA1.wrote(pah)
        for bi in range(4):
            pc = self.psC[bi % 2]
            pcap = self.arena_bf[:, 4096 + (bi % 2) * 1024:
                                 4096 + (bi % 2) * 1024 + 1024]
            gpt = None
            for jj in range(8):
                cc = bi * 8 + jj
                w = ([pah, self.identity.w] + pc.for_write()) if jj == 0 else []
                gpt = E.emit("pe", lambda e, o=pcap[:, jj * 16:jj * 16 + 16],
                             i=ah[:, cc * 128:cc * 128 + 128],
                             idn=self.g_bf(H_ID, 128, parts=16)[:, 0:16]:
                             e.transpose(o, i, idn), waits=w,
                             inc=self.s_pe if jj == 7 else None)
            pc.wrote(gpt)
            self.rA1.read(gpt)
            pev = E.emit("dv", lambda e, o=self.g_bf(D_AHT + bi * 128, 128),
                         i=pcap[:, 0:128]: e.tensor_copy(o, i),
                         waits=[gpt] + (self.lora["AhT"].for_write() +
                                        self.lnt[0].for_write() if bi == 0 else []),
                         inc=self.s_dv)
            pc.read(pev)
            self.lora["AhT"].wrote(pev)
        bpts = [lor_pt, lor_pt, lor_pt]
        bh = self.g_bf(GA1 + 16384 + 3 * 512, 512)
        p = E.emit("dv", lambda e, o=bh, i=self.g_bf(GA1 + 16384, 512),
                   s=g(H_HSEL, 3)[:, 0:1]:
                   e.tensor_scalar_mul(o, i, s), waits=[bpts[0], hw], inc=self.s_dv)
        for a in (1, 2):
            p = E.emit("dv", lambda e, o=bh, i=self.g_bf(GA1 + 16384 + a * 512, 512),
                       s=g(H_HSEL, 3)[:, a:a + 1]:
                       e.scalar_tensor_tensor(o, i, s, o, ALU.mult, ALU.add),
                       waits=[bpts[a], p], inc=self.s_dv)
        pbh = p
        self.rA1.wrote(pbh)
        for bi in range(4):
            pc = self.psC[bi % 2]
            pcap = self.arena_bf[0:16, 4096 + (bi % 2) * 1024:
                                 4096 + (bi % 2) * 1024 + 1024]
            gpt = None
            for jj in range(8):
                cc = bi * 8 + jj
                w = ([pbh, self.identity.w] + pc.for_write()) if jj == 0 else []
                gpt = E.emit("pe", lambda e, o=pcap[:, jj * 128:jj * 128 + 128],
                             i=bh[:, cc * 16:cc * 16 + 16],
                             idn=self.g_bf(H_ID, 128):
                             e.transpose(o, i, idn), waits=w,
                             inc=self.s_pe if jj == 7 else None)
            pc.wrote(gpt)
            self.rA1.read(gpt)
            pev = E.emit("dv", lambda e, o=self.g_bf(D_BHT + bi * 1024, 1024, parts=16),
                         i=pcap: e.tensor_copy(o, i),
                         waits=[gpt] + (self.lora["BhT"].for_write() +
                                        self.lnt[1].for_write() if bi == 0 else []),
                         inc=self.s_dv)
            pc.read(pev)
            self.lora["BhT"].wrote(pev)
        zap = self.arena[0:16, 3072:4096]
        gpt = None
        for dc in range(32):
            for sp in range(2):
                w = ([self.lora["AhT"].w, self.rA0.w] + self.psD.for_write()) \
                    if (dc == 0 and sp == 0) else []
                is_last = dc == 31 and sp == 1
                gpt = E.emit("pe", lambda e, o=zap[:, sp * 512:sp * 512 + 512],
                             l=self.g_bf(D_AHT + dc * 16, 16),
                             r=self.grand[:, GA0 + dc * 1024 + sp * 512:
                                          GA0 + dc * 1024 + sp * 512 + 512],
                             st=(dc == 0), spp=(dc == 31):
                             e.matmul(o, l, r, start=st, stop=spp),
                             waits=w, inc=self.s_pe if is_last else None)
        self.psD.wrote(gpt)
        self.lora["AhT"].read(gpt)
        self.rA0.read(gpt)
        pz = E.emit("dv", lambda e, o=self.g_bf(D_ZSB, 1024, parts=16), i=zap:
                    e.tensor_copy(o, i),
                    waits=[gpt] + self.lora["zsb"].for_write(), inc=self.s_dv)
        self.psD.read(pz)
        self.lora["zsb"].wrote(pz)

        def fsink(oc, stap, st, pev):
            p = E.emit("dv", lambda e, o=self.g_bf(GA1 + oc * 1024, 1024), i=stap:
                       e.tensor_copy(o, i),
                       waits=[pev] + (self.rA1.for_write() if oc == 0 else []),
                       inc=self.s_dv)
            st.read(p)
            self.rA1.wrote(p)

        self.proj_fm(self.wft, 0, 32,
                     lambda oc: self.g_f32(H_FBT, 32)[:, oc:oc + 1], None,
                     GA0, self.rA0, fsink, lora=True, xfm_halves=halves)
        self.add_pass(self.xd, dst, do_ln=(dst is self.xd))

    def p6_attn_b(self):
        E = self.E
        g = self.g_f32
        halves = self.next_halves
        E.emit("gp", lambda e, o=self.g_bf(A1_COS, 1024), i=self.cosf:
               e.dma_start(out=o, in_=i),
               waits=self.rA1.for_write() + self.gstB.for_write() +
               [Pt(self.s_gd, self.s_gd.n), Pt(self.s_gd2, self.s_gd2.n)],
               inc=self.s_misc, inck=16)
        E.emit("gp", lambda e, o=self.g_bf(A1_SIN, 1024), i=self.sinf:
               e.dma_start(out=o, in_=i), inc=self.s_misc, inck=16)
        psin = Pt(self.s_misc, self.s_misc.n)
        self.rA1.wrote(psin)
        rope_guard = [psin]

        def qsink(oc, stap, st, pev):
            rot = self.g_bf(A1_ROT, 1024)
            t1 = self.g_bf(A1_RT1, 1024)
            p1 = E.emit("dv", lambda e, o=rot[0:64, :], i=stap[64:128, :]:
                        e.tensor_scalar_mul(o, i, -1.0),
                        waits=[pev] + rope_guard, inc=self.s_dv)
            p2 = E.emit("dv", lambda e, o=rot[64:128, :], i=stap[0:64, :]:
                        e.tensor_copy(o, i), waits=[pev], inc=self.s_dv)
            p3 = E.emit("dv", lambda e, o=t1, a=stap, b=self.g_bf(A1_COS, 1024):
                        e.tensor_tensor(o, a, b, ALU.mult), waits=[pev], inc=self.s_dv)
            p4 = E.emit("dv", lambda e, o=rot, a=rot, b=self.g_bf(A1_SIN, 1024):
                        e.tensor_tensor(o, a, b, ALU.mult), waits=[p1, p2],
                        inc=self.s_dv)
            p5 = E.emit("dv", lambda e, o=stap, a=t1, b=rot:
                        e.tensor_tensor(o, a, b, ALU.add), waits=[p3, p4],
                        inc=self.s_dv)
            st.wrote(p5)
            self.rA1.read(p5)
            p = E.emit("ac", lambda e, o=self.qfmd[oc * 128:(oc + 1) * 128, :], i=stap:
                       e.dma_start(out=o, in_=i), waits=[p5],
                       inc=self.s_st[oc % 2], inck=16)
            st.read(p)
            self.qst_pts[oc] = [p]

        self.proj_fm(self.wqt, 0, 32, None, SCALE, GA0, self.rA0, qsink,
                     xfm_halves=halves)
        if "qfmd" in self.tap_aps:
            self.tap_dram(self.tap_aps["qfmd"], self.qfmd, list(self.s_st))
        for s in (self.kh + self.vh + self.qh + self.ctxs):
            s.seed(self.rA1.for_write())
        self.p3_attention("b")
        if "ctxd" in self.tap_aps:
            self.tap_dram(self.tap_aps["ctxd"], self.ctxd, list(self.s_stc))
        self.p4_outproj(self.wot, bias=False, xsrc=self.xd, dst=self.xd,
                        do_ln=True)

    def finalize(self):
        E = self.E
        allsems = ([self.s_gd, self.s_sy, self.s_setup, self.s_cxl, self.s_misc,
                    self.s_lor, self.s_ag] + self.s_xt + self.s_w + self.s_k +
                   self.s_v + self.s_q + self.s_st + self.s_stc + self.s_stx)
        last = []
        if self.upto < 4:
            last = [E.emit("sy", lambda e: e.dma_start(out=self.y, in_=self.xq),
                           inc=self.s_sy, inck=16)]
        elif self.upto < 7:
            last = [E.emit("sy", lambda e: e.dma_start(out=self.y, in_=self.xd),
                           waits=list(self.last_store), inc=self.s_sy, inck=16)]
        else:
            last = list(self.last_store)
        if "xd" in self.tap_aps and self.last_store is not None:
            last = [E.emit("sy", lambda e:
                           e.dma_start(out=self.tap_aps["xd"], in_=self.xd),
                           waits=list(self.last_store), inc=self.s_sy, inck=16)]
        waits = last + [Pt(s, s.n) for s in allsems]
        E.emit("sy", lambda e: e.nop(), waits=waits)


def build_program(upto=7, taps=()):
    return Builder(upto=upto, taps=taps).build()



def _tile_w(W):
    dc = W.reshape(32, 128, -1, 128)
    return np.ascontiguousarray(dc.transpose(2, 1, 0, 3).reshape(-1, 128, 4096))


def _rope_tables():
    dim = 128
    inv_freq = (1.0 / (10000.0 ** (np.arange(0, dim, 2, dtype=np.float32)
                                   / np.float32(dim)))).astype(np.float32)
    t = np.arange(BT, dtype=np.float32)
    freqs = np.outer(t, inv_freq).astype(np.float32)
    emb = np.concatenate([freqs, freqs], axis=-1)
    return np.cos(emb).astype(np.float32), np.sin(emb).astype(np.float32)


_UPTO = 7
_TAPS = ()


def kernel(**inputs):
    x = np.ascontiguousarray(inputs["L0_attn_a_x"])
    qkvw = np.ascontiguousarray(inputs["L0_attn_a_attn_qkvw"])
    qkvb = np.asarray(inputs["L0_attn_a_attn_qkvb"])
    ow = np.ascontiguousarray(inputs["L0_attn_a_attn_ow"])
    ob = np.asarray(inputs["L0_attn_a_attn_ob"])
    kc = np.ascontiguousarray(np.asarray(inputs["L0_attn_b_k_cache"]).reshape(512, 16384))
    vc = np.ascontiguousarray(np.asarray(inputs["L0_attn_b_v_cache"]).reshape(512, 16384))
    wq = np.asarray(inputs["L0_attn_b_Wq"])
    wo = np.asarray(inputs["L0_attn_b_Wo"])
    bt = np.ascontiguousarray(np.asarray(inputs["L0_attn_b_block_table"]).astype(np.int32))
    fw = np.asarray(inputs["L0_ffn_weight"])
    fb = np.asarray(inputs["L0_ffn_bias"])
    lA = np.ascontiguousarray(inputs["L0_ffn_lora_A"])
    lB = np.ascontiguousarray(inputs["L0_ffn_lora_B"])
    lsi = np.ascontiguousarray(np.asarray(inputs["L0_ffn_lora_start_indices"]).astype(np.int32))

    qkvw = _tile_w(qkvw)
    ow = _tile_w(ow)
    wqt = _tile_w(np.ascontiguousarray(wq.T))
    wot = _tile_w(np.ascontiguousarray(wo.T))
    wft = _tile_w(np.ascontiguousarray(fw.T))
    qkvbT = np.ascontiguousarray(qkvb.reshape(96, 128).T)
    obT = np.ascontiguousarray(ob.reshape(32, 128).T)
    fbT = np.ascontiguousarray(fb.reshape(32, 128).T)
    cos_t, sin_t = _rope_tables()

    nc = build_program(upto=_UPTO, taps=_TAPS)
    in_maps = []
    for c in range(NCORES):
        b, qi = c // 4, c % 4
        in_maps.append({
            "xq": np.ascontiguousarray(x[b, qi * NT:(qi + 1) * NT]),
            "qkvw": qkvw, "qkvbT": qkvbT, "ow": ow, "obT": obT,
            "wqt": wqt, "wot": wot, "wft": wft, "fbT": fbT,
            "kcache": kc, "vcache": vc, "btab": bt, "lsi": lsi,
            "bval": np.array([[float(b)]], dtype=np.float32),
            "loraA": lA, "loraB": lB,
            "cosf": np.ascontiguousarray(cos_t[qi * NT:(qi + 1) * NT].T),
            "sinf": np.ascontiguousarray(sin_t[qi * NT:(qi + 1) * NT].T),
        })
    res = run_bass_kernel_spmd(nc, in_maps, core_ids=list(range(NCORES)))
    out = np.empty((2, BT, D), dtype=np.float32)
    for c in range(NCORES):
        b, qi = c // 4, c % 4
        out[b, qi * NT:(qi + 1) * NT] = res.results[c]["y"]
    if _TAPS:
        kernel.tap_results = res.results
    return out
